# version 12
# baseline (speedup 1.0000x reference)
import numpy as np
import ml_dtypes
from contextlib import ExitStack
from functools import partial

import concourse.bass as bass
import concourse.mybir as mybir
from concourse.bass_utils import run_bass_kernel_spmd

F32 = mybir.dt.float32
BF16 = mybir.dt.bfloat16
ALU = mybir.AluOpType
ACT = mybir.ActivationFunctionType
AX = mybir.AxisListType
BF = ml_dtypes.bfloat16

H = 2048
S = 2048
Bb = 4
NTOK = Bb * S
OWN = 1024
CAP = 512
FF = 8192
DK = 16
QTR = 512
NQ = NTOK // QTR
FG = 8
NFG = FF // 128 // FG
NU = 2 * NFG


def A(th):
    return th[(slice(None),) * len(th.shape)]


class VPad:

    def __init__(self, v, n=6):
        self._v = v
        self._n = n

    def __getattr__(self, name):
        f = getattr(self._v, name)
        if name in ("nop", "wait_ge") or not callable(f):
            return f

        def wrapped(*a, **k):
            r = f(*a, **k)
            for _ in range(self._n):
                self._v.nop()
            return r
        return wrapped


class Tracer:
    ENGINES = ("sync", "gpsimd", "tensor", "scalar", "vector")

    def __init__(self, counts=None, events=None):
        self.steps = []
        self.counts = dict(counts) if counts else {}
        self.events = dict(events) if events else {}

    def step(self, engine, emit_fn, waits=(), tag=None, sem=None):
        if sem is None:
            assert engine not in ("sync", "gpsimd"), "DMA steps need an explicit stream sem"
            sem = engine
        inc = 16 if engine in ("sync", "gpsimd") else 1
        for w in waits:
            assert w in self.events, f"wait on unknown/future event {w!r}"
        self.counts[sem] = self.counts.get(sem, 0) + inc
        if tag is not None:
            assert tag not in self.events, f"dup tag {tag}"
            self.events[tag] = (sem, self.counts[sem])
        self.steps.append((engine, tuple(waits), emit_fn, inc, sem))

    def emit(self, bk, sems):
        per = {e: [] for e in self.ENGINES}
        for st in self.steps:
            per[st[0]].append(st)

        def mk(engine):
            def _(eng):
                if engine == "vector" and getattr(self, "pad_vector", False):
                    eng = VPad(eng)
                for _e, waits, emit_fn, inc, sem in per[engine]:
                    done = set()
                    for w in waits:
                        ws, wv = self.events[w]
                        if ws == engine or (ws, wv) in done:
                            continue
                        done.add((ws, wv))
                        eng.wait_ge(sems[ws], wv)
                    last = emit_fn(eng)
                    if last is None:
                        last = eng.nop()
                    last.then_inc(sems[sem], inc)
            return _

        for e in self.ENGINES:
            if per[e]:
                getattr(bk, e)(mk(e))


def build_program():
    nc = bass.Bass("TRN2", target_bir_lowering=False, detect_race_conditions=False)

    def din(name, shape, dt=F32):
        return nc.dram_tensor(name, shape, dt, kind="ExternalInput").ap()

    d_xown = din("xownseq", [4, 128, DK, QTR])
    d_xow = din("xowhalf", [2, 128, DK, QTR])
    d_xstats = din("xstats", [NQ, 128, DK, QTR])
    d_xacc = din("xacc", [128, 8, H])
    d_wq = din("wq", [128, 8, DK, 128])
    d_wk = din("wk", [128, DK, 256])
    d_wv = din("wv", [128, DK, 256])
    d_wo = din("wo", [128, 16, H])
    d_w1 = din("w1", [NFG, 128, FG, DK, 128])
    d_w2 = din("w2", [NFG, 128, FG, H])
    d_rt1 = din("rt1", [128, DK, 10])
    d_bq = din("bqt", [128, 8])
    d_bk = din("bkt", [128, 2])
    d_b1 = din("b1t", [128, 64])
    d_brow = din("brow", [3, H])
    d_ident = din("identc", [128, 128])
    d_lstrict = din("lstrictc", [128, 128])
    d_idlow = din("idlowc", [64, 128])
    d_sh64 = din("shift64c", [64, 128])
    d_shdn = din("shiftdnc", [128, 64])
    d_onesd = din("onesdc", [2, 128])
    d_sel64 = din("sel64c", [65, 128])
    d_onesc = din("onescc", [2, 128])
    d_onesf = din("onesfc", [128, 1])
    d_onesbf = din("onesbfc", [128, 1], BF16)
    d_selc8 = din("selc8c", [64, 8])
    d_out = nc.dram_tensor("outp", [128, 8, H], F32, kind="ExternalOutput").ap()
    d_xnsp = nc.dram_tensor("xnsp", [DK, 128, S], BF16, kind="Internal", addr_space="Local").ap()
    d_xnow = nc.dram_tensor("xnow", [DK, 128, OWN], BF16, kind="Internal", addr_space="Local").ap()
    d_facc = nc.dram_tensor("faccsp", [8, 128, H], F32, kind="Internal", addr_space="Local").ap()

    es = ExitStack()
    eAW = ExitStack()
    with es:
        sb = lambda name, shape, dt: es.enter_context(nc.sbuf_tensor(name, shape, dt))
        identt = sb("identt", [128, 128], F32)
        lstr = sb("lstrt", [128, 128], F32)
        idlow = sb("idlowt", [64, 128], F32)
        sh64 = sb("sh64t", [64, 128], F32)
        shdn = sb("shdnt", [128, 64], F32)
        onesd = sb("onesdt", [2, 128], F32)
        sel64 = sb("sel64t", [65, 128], F32)
        onesc = sb("onesct", [2, 128], F32)
        onesf = sb("onesft", [128, 1], F32)
        onesbf = sb("onesbft", [128, 1], BF16)
        selc8 = sb("selc8t", [64, 8], F32)
        rt1t = sb("rt1t", [128, DK, 10], F32)
        bqt = sb("bqtt", [128, 8], F32)
        bkt = sb("bktt", [128, 2], F32)
        b1t = sb("b1tt", [128, 64], F32)
        bo_bc = sb("bo_bc", [128, H], F32)
        b2_bc = sb("b2_bc", [128, H], F32)
        bv_bc = sb("bv_bc", [128, 256], F32)
        rtraw = sb("rtraw", [128, 64, 10], F32)
        rstdtok = sb("rstdtok", [128, 8], F32)
        gtok = sb("gtok", [128, 8], F32)

        SEMNAMES = ["tensor", "scalar", "vector", "dCON", "dXQ0", "dXQ1", "dBGX", "dXLD0",
                    "dXLD1", "dOUT", "dWA", "dW1", "dW2", "dWOS",
                    "dXNF0", "dXNF1", "dXNP0", "dXNP1", "dSPILL", "dSPILL2", "dSPILLS",
                    "dFLD0", "dFLD1"]
        sems = {e: es.enter_context(nc.semaphore("sem_" + e)) for e in SEMNAMES}

        tr = Tracer()
        consts = [
            (identt, d_ident), (lstr, d_lstrict), (idlow, d_idlow),
            (sh64, d_sh64), (shdn, d_shdn), (onesd, d_onesd), (sel64, d_sel64),
            (onesc, d_onesc), (onesf, d_onesf), (onesbf, d_onesbf),
            (selc8, d_selc8), (rt1t, d_rt1), (bqt, d_bq), (bkt, d_bk), (b1t, d_b1),
        ]
        for dst, src in consts:
            tr.step("sync", lambda sy, dst=dst, src=src: sy.dma_start(out=A(dst), in_=src),
                    sem="dCON")
        tr.step("sync", lambda sy: sy.dma_start(out=bo_bc[:, :], in_=d_brow[0:1, :].broadcast_to([128, H])), sem="dCON")
        tr.step("sync", lambda sy: sy.dma_start(out=b2_bc[:, :], in_=d_brow[1:2, :].broadcast_to([128, H])), sem="dCON")
        tr.step("sync", lambda sy: sy.dma_start(out=bv_bc[:, :], in_=d_brow[2:3, 0:256].broadcast_to([128, 256])),
                sem="dCON", tag="consts")
        with nc.Block("p0") as bk:
            tr.emit(bk, sems)
        carry = {"consts": tr.events["consts"]}
        cnts = dict(tr.counts)

        with ExitStack() as s1:
            xq = s1.enter_context(nc.sbuf_tensor("xq", [128, 2, DK, QTR], F32))
            xn = s1.enter_context(nc.sbuf_tensor("xn", [128, DK, S], BF16))
            xnw = s1.enter_context(nc.sbuf_tensor("xnw", [128, DK, OWN], BF16))
            xsqb = s1.enter_context(nc.sbuf_tensor("xsqb", [128, 2, QTR], BF16))
            xqb = s1.enter_context(nc.sbuf_tensor("xqb", [128, 2, QTR], BF16))
            tmpn = s1.enter_context(nc.sbuf_tensor("tmpn", [128, QTR], F32))
            row2 = s1.enter_context(nc.sbuf_tensor("row2", [2, QTR], F32))
            mrow2 = s1.enter_context(nc.sbuf_tensor("mrow2", [2, QTR], F32))
            rva = s1.enter_context(nc.sbuf_tensor("rva", [1, QTR], F32))
            rvb = s1.enter_context(nc.sbuf_tensor("rvb", [1, QTR], F32))
            ps0 = s1.enter_context(nc.psum_tensor("ps0", [128, 2048], F32))
            ps1 = s1.enter_context(nc.psum_tensor("ps1", [128, 512], F32))
            R_MS = ps0[0:1, 0:512]
            R_S2 = ps0[0:1, 512:1024]
            R_MBC = ps0[:, 1024:1536]
            R_RBC = ps0[:, 1536:2048]

            tr = Tracer(cnts, carry)
            def em_r2z(v):
                v.memset(A(row2), 0.0)
                return v.memset(A(mrow2), 0.0)
            tr.step("vector", em_r2z, tag="r2z")
            for q in range(6):
                w = [f"norm_done{q-2}", f"s2mm{q-2}_{DK-1}"] if q >= 2 else []
                src_ap = d_xown[q, :, :, :] if q < 4 else d_xow[q - 4, :, :, :]
                tr.step("sync", lambda sy, src_ap=src_ap, q=q: sy.dma_start(
                    out=xq[:, q % 2, :, :], in_=src_ap),
                    waits=w, tag=f"xq{q}", sem=f"dXQ{q % 2}")

                for dk in range(DK):
                    w = [f"xq{q}"]
                    if dk >= 2:
                        w.append(f"s2mm{q}_{dk-2}")
                    elif q >= 1:
                        w.append(f"s2mm{q-1}_{DK-1}")

                    def em_xsq(sc, q=q, dk=dk):
                        sc.activation(xsqb[:, dk % 2, :], xq[:, q % 2, dk, :], ACT.Square,
                                      bias=0.0, scale=1.0)
                        return sc.activation(xqb[:, dk % 2, :], xq[:, q % 2, dk, :], ACT.Identity,
                                             bias=0.0, scale=1.0)
                    tr.step("scalar", em_xsq, waits=w, tag=f"xsq{q}_{dk}")
                    w2_ = [f"xsq{q}_{dk}"]
                    if dk == 0 and q >= 1:
                        w2_ += [f"ve{q-1}", f"evst{q-1}"]

                    def em_s2(t, dk=dk):
                        t.matmul(R_S2, onesbf[:, :], xsqb[:, dk % 2, :],
                                 start=(dk == 0), stop=(dk == DK - 1),
                                 skip_group_check=True)
                        return t.matmul(R_MS, onesbf[:, :], xqb[:, dk % 2, :],
                                        start=(dk == 0), stop=(dk == DK - 1),
                                        skip_group_check=True)
                    tr.step("tensor", em_s2, waits=w2_, tag=f"s2mm{q}_{dk}")
                tr.step("vector", lambda v: v.tensor_scalar_mul(mrow2[0:1, :], R_MS, 1.0 / H),
                        waits=[f"s2mm{q}_{DK-1}", "consts", "r2z"]
                        + ([f"mbc{q-1}"] if q else []),
                        tag=f"evst{q}")
                tr.step("tensor", lambda t: t.matmul(R_MBC, onesc[:, :], mrow2[:, :],
                                                     start=True, stop=True, skip_group_check=True),
                        waits=[f"evst{q}"] + ([f"norm_done{q-1}"] if q else []), tag=f"mbc{q}")

                def em_ve(v):
                    v.tensor_copy(rvb[0:1, :], R_MBC[0:1, :])
                    v.tensor_mul(rva[0:1, :], rvb[0:1, :], rvb[0:1, :])
                    v.scalar_tensor_tensor(rvb[0:1, :], R_S2, 1.0 / H, rva[0:1, :],
                                           ALU.mult, ALU.subtract)
                    return v.tensor_scalar_add(rva[0:1, :], rvb[0:1, :], 1e-5)
                tr.step("vector", em_ve, waits=[f"mbc{q}", f"s2mm{q}_{DK-1}"], tag=f"ve{q}")
                tr.step("scalar", lambda sc: sc.activation(rvb[0:1, :], rva[0:1, :], ACT.Sqrt,
                                                           bias=0.0, scale=1.0),
                        waits=[f"ve{q}"], tag=f"sq{q}")

                def em_rrow(v):
                    v.reciprocal(row2[0:1, :], rvb[0:1, :])
                    v.tensor_mul(rvb[0:1, :], row2[0:1, :], row2[0:1, :])
                    v.tensor_mul(rvb[0:1, :], rvb[0:1, :], rva[0:1, :])
                    v.tensor_scalar(rva[0:1, :], rvb[0:1, :], -0.5, 1.5, ALU.mult, ALU.add)
                    return v.tensor_mul(row2[0:1, :], row2[0:1, :], rva[0:1, :])
                tr.step("vector", em_rrow, waits=[f"sq{q}", "r2z"], tag=f"rrow{q}")
                tr.step("tensor", lambda t: t.matmul(R_RBC, onesc[:, :], row2[:, :],
                                                     start=True, stop=True, skip_group_check=True),
                        waits=[f"rrow{q}"], tag=f"rbc{q}")
                if q >= 4:
                    j = q - 4

                    def em_rtr(t):
                        for i in range(4):
                            t.transpose(ps1[:, i:i + 1],
                                        row2[0:1, 128 * i:128 * i + 128],
                                        identt[0:1, 0:1])
                        return t.matmul(ps0[0:1, 0:512], onesbf[:, :], xsqb[:, 0, :],
                                        start=True, stop=True, skip_group_check=True)
                    tr.step("tensor", em_rtr,
                            waits=[f"rrow{q}"] + ([f"rtrev{j-1}"] if j else []), tag=f"rtr{j}")
                    tr.step("vector", lambda v, j=j: v.tensor_copy(rstdtok[:, 4 * j:4 * j + 4],
                                                                   ps1[:, 0:4]),
                            waits=[f"rtr{j}"], tag=f"rtrev{j}")
                for dk in range(DK):
                    def em_norm(v, q=q, dk=dk):
                        v.tensor_sub(tmpn[:, :], xq[:, q % 2, dk, :], R_MBC)
                        if q < 4:
                            return v.tensor_mul(xn[:, dk, QTR * q:QTR * q + QTR],
                                                tmpn[:, :], R_RBC)
                        return v.tensor_mul(xnw[:, dk, QTR * (q - 4):QTR * (q - 4) + QTR],
                                            tmpn[:, :], R_RBC)
                    tr.step("vector", em_norm, waits=[f"rbc{q}"], tag=f"norm{q}_{dk}")
                tr.step("vector", lambda v: None, tag=f"norm_done{q}")
            tr.step("sync", lambda sy: sy.dma_start(out=d_xnsp.rearrange("k p n -> p k n"),
                                                    in_=A(xn)),
                    waits=["norm_done3"], tag="xnspill", sem="dSPILLS")
            tr.step("gpsimd", lambda g: g.dma_start(
                out=d_xnow[:, :, 0:QTR].rearrange("k p n -> p k n"),
                in_=A(xnw)[:, :, 0:QTR]),
                    waits=["norm_done4"], tag="xnowspill0", sem="dSPILL2")
            tr.step("gpsimd", lambda g: g.dma_start(
                out=d_xnow[:, :, QTR:OWN].rearrange("k p n -> p k n"),
                in_=A(xnw)[:, :, QTR:OWN]),
                    waits=["norm_done5"], tag="xnowspill", sem="dSPILL2")
            with nc.Block("p1") as bk:
                tr.emit(bk, sems)
            carry = {"consts": tr.events["consts"], "xnspill": tr.events["xnspill"],
                     "xnowspill0": tr.events["xnowspill0"],
                     "xnowspill": tr.events["xnowspill"]}
            cnts = dict(tr.counts)

        with ExitStack() as s2:
            facc = s2.enter_context(nc.sbuf_tensor("facc", [128, 8, H], F32))
            w1b = s2.enter_context(nc.sbuf_tensor("w1b", [128, FG, DK, 128], BF16))
            w2b = s2.enter_context(nc.sbuf_tensor("w2b", [128, FG, H], BF16))
            mid = s2.enter_context(nc.sbuf_tensor("mid", [128, FG, 512], BF16))
            xnf = s2.enter_context(nc.sbuf_tensor("xnf", [128, 8, 512], BF16))
            xgs = s2.enter_context(nc.sbuf_tensor("xgs", [128, DK, QTR], F32))
            pp = [s2.enter_context(nc.psum_tensor(f"pp{i}", [128, 1024], F32)) for i in range(4)]
            ph1 = [pp[i // 2][:, 512 * (i % 2):512 * (i % 2) + 512] for i in range(8)]

            tr = Tracer(cnts, carry)
            tr.step("gpsimd", lambda g: g.dma_start(out=A(w1b), in_=d_w1[0]), tag="w1_0", sem="dW1")
            tr.step("gpsimd", lambda g: g.dma_start(out=A(w2b), in_=d_w2[0]), tag="w2_0", sem="dW2")

            def bg_quarter(q, cover):
                def em_bgst(t, q=q):
                    for dk in range(DK):
                        for tc4 in range(4):
                            last = t.matmul(
                                pp[0][0:128, 10 * tc4:10 * tc4 + 10],
                                xgs[:, dk, 128 * tc4:128 * tc4 + 128], rt1t[:, dk, :],
                                start=(dk == 0 and tc4 == 0), stop=(dk == DK - 1),
                                skip_group_check=True)
                    return last
                tr.step("tensor", em_bgst, waits=[f"bgx{q}", "consts"] + cover, tag=f"bgst{q}")

                def em_bgrs(v, q=q):
                    return v.tensor_copy(
                        A(rtraw)[:, 4 * q:4 * q + 4, :],
                        pp[0][0:128, 0:40].rearrange("p (a c) -> p a c", c=10))
                tr.step("vector", em_bgrs, waits=[f"bgst{q}"], tag=f"bgrs{q}")

            for u in range(NU):
                fg, hf = u // 2, u % 2
                if u >= 1 and (u - 1) < NQ:
                    bg_quarter(u - 1, [f"gel{u-1}_{FG-1}"])
                if u < NQ:
                    tr.step("sync", lambda sy, u=u: sy.dma_start(out=A(xgs), in_=d_xstats[u]),
                            waits=[f"bgst{u-1}"] if u >= 1 else [], tag=f"bgx{u}", sem="dBGX")

                def xnf_step(gdk):
                    w = ["xnowspill" if hf else "xnowspill0"]
                    if gdk >= 2:
                        w.append(f"h1g{u}_{4*(gdk-2)+3}")
                    elif u >= 1:
                        w.append(f"h1g{u-1}_{4*(gdk+2)+3}")
                    tr.step("gpsimd", lambda g, gdk=gdk, hf=hf: g.dma_start(
                        out=xnf[:, 4 * (gdk % 2):4 * (gdk % 2) + 4, :],
                        in_=d_xnow[4 * gdk:4 * gdk + 4, :, 512 * hf:512 * hf + 512]
                        .rearrange("k p n -> p k n")),
                        waits=w, tag=f"xnf{u}_{gdk}", sem=f"dXNF{gdk % 2}")
                xnf_step(0)
                xnf_step(1)
                for dk in range(DK):
                    gdk = dk // 4

                    def em_h1(t, dk=dk, gdk=gdk):
                        sl = xnf[:, 4 * (gdk % 2) + dk % 4, :]
                        for f in range(FG):
                            last = t.matmul(ph1[f], w1b[:, f, dk, :], sl,
                                            start=(dk == 0), stop=(dk == DK - 1),
                                            skip_group_check=True)
                        return last
                    w = [f"xnf{u}_{gdk}", f"w1_{fg}"]
                    if dk == 0:
                        if u >= 1:
                            w += [f"gel{u-1}_{FG-1}", f"h2e{u-1}_6", f"h2e{u-1}_7"]
                        q = u - 1
                        if 0 <= q < NQ:
                            w.append(f"bgrs{q}")
                    tr.step("tensor", em_h1, waits=w, tag=f"h1g{u}_{dk}")
                    if dk % 4 == 3 and gdk + 2 < 4:
                        xnf_step(gdk + 2)
                for f in range(FG):
                    w = [f"h1g{u}_{DK-1}"]
                    if u >= 1 and f == 0:
                        w.append(f"h2m{u-1}_7")
                    tr.step("scalar", lambda sc, fg=fg, f=f: sc.activation(
                        mid[:, f, :], ph1[f], ACT.Gelu,
                        bias=b1t[:, fg * FG + f:fg * FG + f + 1], scale=1.0),
                        waits=w, tag=f"gel{u}_{f}")
                if hf == 1 and fg + 1 < NFG:
                    tr.step("gpsimd", lambda g, fg=fg: g.dma_start(out=A(w1b), in_=d_w1[fg + 1]),
                            waits=[f"h1g{u}_{DK-1}"], tag=f"w1_{fg+1}", sem="dW1")
                for i in range(8):
                    tc, hh = i // 2, i % 2
                    gtc = 4 * hf + tc

                    def em_h2(t, i=i, tc=tc, hh=hh):
                        for f in range(FG):
                            for c2 in range(2):
                                last = t.matmul(
                                    pp[2 + i % 2][:, 512 * c2:512 * c2 + 512],
                                    mid[:, f, 128 * tc:128 * tc + 128],
                                    w2b[:, f, 1024 * hh + 512 * c2:1024 * hh + 512 * c2 + 512],
                                    start=(f == 0), stop=(f == FG - 1), skip_group_check=True)
                        return last
                    w = [f"gel{u}_{FG-1}", "w2_0" if fg == 0 else f"w2_{fg}"]
                    if i >= 2:
                        w.append(f"h2e{u}_{i-2}")
                    elif u >= 1:
                        w.append(f"h2e{u-1}_{6+i}")
                    tr.step("tensor", em_h2, waits=w, tag=f"h2m{u}_{i}")
                    if u <= 1:
                        def em_h2e(v, gtc=gtc, hh=hh, i=i):
                            return v.tensor_copy(facc[:, gtc, 1024 * hh:1024 * hh + 1024],
                                                 pp[2 + i % 2][:, :])
                    else:
                        def em_h2e(v, gtc=gtc, hh=hh, i=i):
                            sl = facc[:, gtc, 1024 * hh:1024 * hh + 1024]
                            return v.tensor_add(sl, sl, pp[2 + i % 2][:, :])
                    tr.step("vector", em_h2e, waits=[f"h2m{u}_{i}"], tag=f"h2e{u}_{i}")
                if hf == 1 and fg + 1 < NFG:
                    tr.step("gpsimd", lambda g, fg=fg: g.dma_start(out=A(w2b), in_=d_w2[fg + 1]),
                            waits=[f"h2m{u}_7"], tag=f"w2_{fg+1}", sem="dW2")
            bg_quarter(NQ - 1, [f"gel{NU-1}_{FG-1}"])
            tr.step("gpsimd", lambda g: g.dma_start(out=d_facc.rearrange("t p h -> p t h"),
                                                    in_=A(facc)),
                    waits=[f"h2e{NU-1}_7"], tag="faccspill", sem="dSPILL")
            with nc.Block("p2") as bk:
                tr.emit(bk, sems)
            carry = {k: tr.events[k] for k in ["consts", "xnspill", "faccspill"]}
            carry["bglast"] = tr.events[f"bgrs{NQ-1}"]
            cnts = dict(tr.counts)

        with ExitStack() as eqkv:
            qslab = eqkv.enter_context(nc.sbuf_tensor("qslab", [128, 8, S], BF16))
            kslab = eqkv.enter_context(nc.sbuf_tensor("kslab", [128, 2, S], BF16))
            vslab = eqkv.enter_context(nc.sbuf_tensor("vslab", [128, 16, 260], BF16))

            with ExitStack() as s3:
                wks = s3.enter_context(nc.sbuf_tensor("wks", [128, DK, 256], BF16))
                wvs = s3.enter_context(nc.sbuf_tensor("wvs", [128, DK, 256], BF16))
                wqb = s3.enter_context(nc.sbuf_tensor("wqb", [128, 8, DK, 128], BF16))
                xnp = s3.enter_context(nc.sbuf_tensor("xnp", [128, 8, 1024], BF16))
                pp = [s3.enter_context(nc.psum_tensor(f"pq{i}", [128, 1024], F32)) for i in range(4)]
                pv8 = [pp[2 + tc // 4][:, 256 * (tc % 4):256 * (tc % 4) + 256] for tc in range(8)]
                pq8 = [pp[i // 2][:, 512 * (i % 2):512 * (i % 2) + 512] for i in range(8)]

                tr = Tracer(cnts, carry)
                tr.step("gpsimd", lambda g: g.dma_start(out=A(wks), in_=d_wk), sem="dWA")
                tr.step("gpsimd", lambda g: g.dma_start(out=A(wvs), in_=d_wv), sem="dWA")
                tr.step("gpsimd", lambda g: g.dma_start(out=A(wqb), in_=d_wq), sem="dWA",
                        tag="wABurst")
                tr.step("vector", lambda v: v.memset(
                    A(vslab).rearrange("p k (g c) -> p k g c", c=65)[:, :, :, 64:65], 1.0),
                    tag="vones")

                for hf in range(2):
                    def kx_step(gdk, hf=hf):
                        w = ["xnspill"]
                        if gdk >= 2:
                            w.append(f"kv{hf}_{4*(gdk-2)+3}")
                        elif hf == 1:
                            w.append(f"kv0_{4*(gdk+2)+3}")
                        tr.step("gpsimd", lambda g, gdk=gdk, hf=hf: g.dma_start(
                            out=xnp[:, 4 * (gdk % 2):4 * (gdk % 2) + 4, :],
                            in_=d_xnsp[4 * gdk:4 * gdk + 4, :, 1024 * hf:1024 * hf + 1024]
                            .rearrange("k p n -> p k n")),
                            waits=w, tag=f"kx{hf}_{gdk}", sem=f"dXNP{gdk % 2}")
                    kx_step(0)
                    kx_step(1)
                    for dk in range(DK):
                        gdk = dk // 4

                        def em_kv(t, hf=hf, dk=dk, gdk=gdk):
                            sl = xnp[:, 4 * (gdk % 2) + dk % 4, :]
                            for oc in range(2):
                                for c2 in range(2):
                                    t.matmul(pp[oc][:, 512 * c2:512 * c2 + 512],
                                             wks[:, dk, 128 * oc:128 * oc + 128],
                                             sl[:, 512 * c2:512 * c2 + 512],
                                             start=(dk == 0), stop=(dk == DK - 1),
                                             skip_group_check=True)
                            for tc in range(8):
                                last = t.matmul(pv8[tc], sl[:, 128 * tc:128 * tc + 128],
                                                wvs[:, dk, :],
                                                start=(dk == 0 and tc % 2 == 0),
                                                stop=(dk == DK - 1),
                                                skip_group_check=True)
                            return last
                        kw = [f"kx{hf}_{gdk}", "wABurst"]
                        if dk == 0:
                            if hf == 0:
                                kw.append("bglast")
                            else:
                                kw += [f"kev0_{oc}" for oc in range(2)]
                                kw += [f"vev0_{tc}" for tc in range(8)]
                        tr.step("tensor", em_kv, waits=kw, tag=f"kv{hf}_{dk}")
                        if dk % 4 == 3 and gdk + 2 < 4:
                            kx_step(gdk + 2)
                    for oc in range(2):
                        tr.step("vector", lambda v, hf=hf, oc=oc: v.tensor_scalar(
                            kslab[:, oc, OWN * hf:OWN * hf + OWN], pp[oc][:, :],
                            bkt[:, oc:oc + 1], None, ALU.add),
                            waits=[f"kv{hf}_{DK-1}"], tag=f"kev{hf}_{oc}")
                    for tc in range(8):
                        def em_vev(v, hf=hf, tc=tc):
                            kt = 8 * hf + tc
                            return v.tensor_tensor(
                                A(vslab)[:, kt:kt + 1, :].rearrange(
                                    "p o (g c) -> p (o g) c", c=65)[:, :, 0:64],
                                pv8[tc].rearrange("p (g c) -> p g c", c=64),
                                bv_bc[:, :].rearrange("p (g c) -> p g c", c=64),
                                ALU.add)
                        tr.step("vector", em_vev, waits=[f"kv{hf}_{DK-1}", "vones"],
                                tag=f"vev{hf}_{tc}")

                for tq in range(4):
                    def qx_step(gdk, tq=tq):
                        w = []
                        if gdk >= 2:
                            w.append(f"qprj{tq}_{4*(gdk-2)+3}")
                        elif tq == 0:
                            w.append(f"kv1_{4*(gdk+2)+3}")
                        else:
                            w.append(f"qprj{tq-1}_{4*(gdk+2)+3}")
                        tr.step("gpsimd", lambda g, gdk=gdk, tq=tq: g.dma_start(
                            out=xnp[:, 4 * (gdk % 2):4 * (gdk % 2) + 4, 0:512],
                            in_=d_xnsp[4 * gdk:4 * gdk + 4, :, 512 * tq:512 * tq + 512]
                            .rearrange("k p n -> p k n")),
                            waits=w, tag=f"qx{tq}_{gdk}", sem=f"dXNP{gdk % 2}")
                    qx_step(0)
                    qx_step(1)
                    for dk in range(DK):
                        gdk = dk // 4

                        def em_q(t, dk=dk, gdk=gdk):
                            sl = xnp[:, 4 * (gdk % 2) + dk % 4, 0:512]
                            for j in range(8):
                                last = t.matmul(pq8[j], wqb[:, j, dk, :], sl,
                                                start=(dk == 0), stop=(dk == DK - 1),
                                                skip_group_check=True)
                            return last
                        qw = [f"qx{tq}_{gdk}", "wABurst"]
                        if dk == 0:
                            if tq == 0:
                                qw += [f"kev1_{oc}" for oc in range(2)]
                                qw += [f"vev1_{tc}" for tc in range(8)]
                            else:
                                qw += [f"qe{tq-1}_{j}" for j in range(8)]
                        tr.step("tensor", em_q, waits=qw, tag=f"qprj{tq}_{dk}")
                        if dk % 4 == 3 and gdk + 2 < 4:
                            qx_step(gdk + 2)
                    for j in range(8):
                        def em_qe(v, j=j, tq=tq):
                            v.tensor_scalar(
                                qslab[:, j, 256 * tq:256 * tq + 256],
                                pq8[j].rearrange("p (a two) -> p a two", two=2)[:, :, 0],
                                bqt[:, j:j + 1], None, ALU.add)
                            return v.tensor_scalar(
                                qslab[:, j, 1024 + 256 * tq:1024 + 256 * tq + 256],
                                pq8[j].rearrange("p (a two) -> p a two", two=2)[:, :, 1],
                                bqt[:, j:j + 1], None, ALU.add)
                        tr.step("vector", em_qe, waits=[f"qprj{tq}_{DK-1}"], tag=f"qe{tq}_{j}")
                with nc.Block("p3") as bk:
                    tr.emit(bk, sems)
                carry = {k: tr.events[k] for k in ["consts", "faccspill", "bglast"]}
                carry["qlast"] = tr.events["qe3_7"]
                carry["qhalf"] = tr.events["qe1_7"]
                carry["kvlast"] = tr.events["vev1_7"]
                cnts = dict(tr.counts)

            attns = eAW.enter_context(nc.sbuf_tensor("attns", [128, 16, OWN], BF16, side="right"))
            wos = eAW.enter_context(nc.sbuf_tensor("wos", [128, 16, H], BF16, side="right"))
            with ExitStack() as s4:
                probs = s4.enter_context(nc.sbuf_tensor("probs", [128, 2, 1024], BF16))
                avsb = s4.enter_context(nc.sbuf_tensor("avsb", [65, 2, 1024], F32))
                avtmp = s4.enter_context(nc.sbuf_tensor("avtmp", [64, 512], F32))
                rr = s4.enter_context(nc.sbuf_tensor("rr", [65, 512], F32))
                rbsb = s4.enter_context(nc.sbuf_tensor("rbsb", [64, 512], F32))
                ps_s = [s4.enter_context(nc.psum_tensor(f"ps_s{i}", [128, 1024], F32))
                        for i in range(2)]
                ps_av = s4.enter_context(nc.psum_tensor("ps_av", [128, 1024], F32))
                ps_ep = s4.enter_context(nc.psum_tensor("ps_ep", [128, 1024], F32))

                tr = Tracer(cnts, carry)
                for i in range(4):
                    tr.step("gpsimd", lambda g, i=i: g.dma_start(
                        out=wos[:, 4 * i:4 * i + 4, :], in_=d_wo[:, 4 * i:4 * i + 4, :]),
                        tag=f"wo{i}", sem="dWOS")
                tr.step("vector", lambda v: v.memset(rr[0:64, :], 0.0), tag="rrz")
                vv = A(vslab).rearrange("p k (g c) -> p k g c", c=65)
                for pr in range(8):
                    g_, ckv = pr // 2, pr % 2
                    hlA = 8 * ckv + g_
                    hlB = 8 * ckv + 4 + g_
                    for qb in range(4):
                        par = qb // 2
                        mb = 512 * (qb % 2)
                        un = pr * 4 + qb

                        def em_sc(t, ckv=ckv, pr=pr, qb=qb, kt=0):
                            psc = ps_s[kt % 2]
                            t.matmul(psc[:, 0:512],
                                     kslab[0:64, ckv, 128 * kt:128 * kt + 128],
                                     qslab[0:64, pr, 512 * qb:512 * qb + 512],
                                     start=True, stop=True,
                                     tile_position=(0, 0), skip_group_check=True)
                            return t.matmul(psc[:, 512:1024],
                                            kslab[64:128, ckv, 128 * kt:128 * kt + 128],
                                            qslab[64:128, pr, 512 * qb:512 * qb + 512],
                                            start=True, stop=True,
                                            tile_position=(64, 0),
                                            skip_group_check=True)

                        def em_exp(sc, kt=0):
                            return sc.activation(probs[:, kt % 2, :], ps_s[kt % 2][:, :],
                                                 ACT.Exp, bias=0.0, scale=0.125)

                        def em_av(t, ckv=ckv, kt=0):
                            t.matmul(ps_av[0:65, 0:512], vv[:, kt, 2 * ckv, :],
                                     probs[:, kt % 2, 0:512],
                                     start=(kt == 0), stop=(kt == 15), skip_group_check=True)
                            return t.matmul(ps_av[0:65, 512:1024], vv[:, kt, 2 * ckv + 1, :],
                                            probs[:, kt % 2, 512:1024],
                                            start=(kt == 0), stop=(kt == 15),
                                            skip_group_check=True)

                        def emit_av(kt):
                            aw = [f"exp{un}_{kt}"]
                            if kt == 0 and un >= 1:
                                aw.append(f"evac{un-1}")
                            tr.step("tensor", partial(em_av, kt=kt), waits=aw,
                                    tag=f"av{un}_{kt}")

                        for kt in range(16):
                            w = []
                            if un == 0 and kt == 0:
                                w += ["qhalf", "kvlast"]
                            if un == 1 and kt == 0:
                                w.append("qlast")
                            if kt >= 2:
                                w.append(f"exp{un}_{kt-2}")
                            elif un >= 1:
                                w.append(f"exp{un-1}_{14 + kt}")
                            tr.step("tensor", partial(em_sc, kt=kt), waits=w,
                                    tag=f"sc{un}_{kt}")
                            ew = [f"sc{un}_{kt}"]
                            if kt >= 2:
                                ew.append(f"av{un}_{kt-2}")
                            elif un >= 1:
                                ew.append(f"av{un-1}_{14 + kt}")
                            tr.step("scalar", partial(em_exp, kt=kt), waits=ew,
                                    tag=f"exp{un}_{kt}")
                            if kt >= 1:
                                emit_av(kt - 1)
                        emit_av(15)
                        tr.step("vector", lambda v, un=un: v.tensor_copy(
                            avsb[:, un % 2, :], ps_av[0:65, 0:1024]),
                            waits=[f"av{un}_15"], tag=f"evac{un}")
                        for hx, hl in ((0, hlA), (1, hlB)):
                            cA = 512 * hx
                            tr.step("vector", lambda v, cA=cA, un=un: v.reciprocal(
                                rr[64:65, :], avsb[64:65, un % 2, cA:cA + 512]),
                                waits=([f"evac{un}", "rrz"] if hx == 0 else [f"rb{un}_0"]),
                                tag=f"rec{un}_{hx}")
                            tr.step("tensor", lambda t, hx=hx: t.matmul(
                                ps_ep[0:128, 512 * hx:512 * hx + 512], sel64[:, :], rr[:, :],
                                start=True, stop=True, skip_group_check=True),
                                waits=[f"rec{un}_{hx}"], tag=f"rb{un}_{hx}")
                            tr.step("vector", lambda v, hx=hx: v.tensor_copy(
                                rbsb[:, :], ps_ep[0:64, 512 * hx:512 * hx + 512]),
                                waits=[f"rb{un}_{hx}"], tag=f"rbe{un}_{hx}")
                            if par == 0:
                                tr.step("vector", lambda v, hl=hl, mb=mb, cA=cA, un=un:
                                        v.tensor_mul(
                                            attns[0:64, hl, mb:mb + 512],
                                            avsb[0:64, un % 2, cA:cA + 512], rbsb[:, :]),
                                        waits=[f"rbe{un}_{hx}"], tag=f"ep{un}_{hx}")
                            else:
                                tr.step("vector", lambda v, cA=cA, un=un: v.tensor_mul(
                                    avtmp[:, :], avsb[0:64, un % 2, cA:cA + 512], rbsb[:, :]),
                                    waits=[f"rbe{un}_{hx}"], tag=f"avt{un}_{hx}")
                                tr.step("tensor", lambda t, hx=hx: t.matmul(
                                    ps_ep[:, 512 * hx:512 * hx + 512], sh64[:, :], avtmp[:, :],
                                    start=True, stop=True, skip_group_check=True),
                                    waits=[f"avt{un}_{hx}"], tag=f"sh{un}_{hx}")
                                tr.step("vector", lambda v, hl=hl, mb=mb, hx=hx: v.tensor_copy(
                                    attns[64:128, hl, mb:mb + 512],
                                    ps_ep[64:128, 512 * hx:512 * hx + 512]),
                                    waits=[f"sh{un}_{hx}"], tag=f"ep{un}_{hx}")
                with nc.Block("p4") as bk:
                    tr.emit(bk, sems)
                carry = {k: tr.events[k] for k in ["consts", "faccspill", "bglast"]}
                carry["attdone"] = tr.events["ep31_1"]
                carry["wodone"] = tr.events["wo3"]
                cnts = dict(tr.counts)

        with ExitStack() as s5:
            r64 = [s5.enter_context(nc.sbuf_tensor(f"r64_{i}", [128, 64], F32)) for i in range(6)]
            rw = s5.enter_context(nc.sbuf_tensor("rw", [128, 64, 8], F32))
            rw2 = s5.enter_context(nc.sbuf_tensor("rw2", [128, 64, 8], F32))
            sbc = s5.enter_context(nc.sbuf_tensor("sbc", [128, 8], F32))
            s8 = s5.enter_context(nc.sbuf_tensor("s8", [2, 8], F32))
            ea = s5.enter_context(nc.sbuf_tensor("ea", [64, 128], F32))
            eb = s5.enter_context(nc.sbuf_tensor("eb", [64, 128], F32))
            ind = s5.enter_context(nc.sbuf_tensor("ind", [64, 128], F32))
            sca = s5.enter_context(nc.sbuf_tensor("sca", [64, 8, 128], F32))
            scb = s5.enter_context(nc.sbuf_tensor("scb", [64, 8, 128], F32))
            zer = s5.enter_context(nc.sbuf_tensor("zer", [64, 128], F32))
            t01 = s5.enter_context(nc.sbuf_tensor("t01", [64, 16], F32))
            tsb = s5.enter_context(nc.sbuf_tensor("tsb", [128, 8], F32))
            offs = s5.enter_context(nc.sbuf_tensor("offs", [128, 8], F32))
            off1 = s5.enter_context(nc.sbuf_tensor("off1", [64, 8], F32))
            ppx = [s5.enter_context(nc.sbuf_tensor(f"ppx{i}", [64, 128], F32)) for i in range(4)]
            own8 = s5.enter_context(nc.sbuf_tensor("own8", [8, 4, 128], F32))
            uo = s5.enter_context(nc.sbuf_tensor("uo", [8, 8, 128], F32))
            po = s5.enter_context(nc.sbuf_tensor("po", [8, 8, 128], F32))
            utmp8 = s5.enter_context(nc.sbuf_tensor("utmp8", [64, 8, 128], F32))
            o8 = [s5.enter_context(nc.sbuf_tensor(f"o8_{i}", [8, 128], F32)) for i in range(5)]
            ps_a = s5.enter_context(nc.psum_tensor("ps_a", [128, 1024], F32))
            ps_b = s5.enter_context(nc.psum_tensor("ps_b", [128, 1024], F32))
            ps_c = s5.enter_context(nc.psum_tensor("ps_c", [128, 1024], F32))

            tr = Tracer(cnts, carry)
            tr.pad_vector = True

            def drain(t):
                return t.matmul(ps_b[0:1, 512:1024], onesbf[:, :], wos[:, 0, 0:512],
                                start=True, stop=True, skip_group_check=True)

            def em_srow(t):
                for dk in range(DK):
                    t.matmul(ps_a[0:1, 0:10], onesf[:, :], rt1t[:, dk, :],
                             start=(dk == 0), stop=(dk == DK - 1),
                             skip_group_check=True)
                return drain(t)
            tr.step("tensor", em_srow, waits=["consts", "attdone"], tag="srow")

            def em_s8(v):
                v.memset(A(s8), 0.0)
                return v.tensor_copy(s8[0:1, :], ps_a[0:1, 0:8])
            tr.step("vector", em_s8, waits=["srow"], tag="s8ev")
            tr.step("tensor", lambda t: t.matmul(ps_b[0:128, 0:8], onesc[:, :], s8[:, :],
                                                 start=True, stop=True, skip_group_check=True),
                    waits=["s8ev"], tag="sbcmm")
            tr.step("vector", lambda v: v.tensor_copy(sbc[:, :], ps_b[0:128, 0:8]),
                    waits=["sbcmm"], tag="sbcev")
            tr.step("vector", lambda v: v.tensor_scalar_mul(r64[0][:, :], rtraw[:, :, 8], 1.0 / H),
                    waits=["bglast"], tag="mtok")

            def em_u(v):
                for e in range(8):
                    v.tensor_scalar(r64[1][:, :], r64[0][:, :], sbc[:, e:e + 1], None, ALU.mult)
                    v.tensor_sub(rw[:, :, e], rtraw[:, :, e], r64[1][:, :])
                return None
            tr.step("vector", em_u, waits=["mtok", "sbcev"], tag="u_done")

            def em_top(v):
                v.tensor_max(r64[0][:, :], rw[:, :, 0], rw[:, :, 1])
                for e in range(2, 8):
                    v.tensor_max(r64[0][:, :], r64[0][:, :], rw[:, :, e])
                for e in range(8):
                    v.tensor_tensor(r64[1][:, :], rw[:, :, e], r64[0][:, :], ALU.is_equal)
                    v.tensor_scalar_mul(r64[2][:, :], r64[1][:, :], float(8 - e))
                    if e == 0:
                        v.tensor_copy(r64[3][:, :], r64[2][:, :])
                    else:
                        v.tensor_max(r64[3][:, :], r64[3][:, :], r64[2][:, :])
                v.tensor_scalar(r64[4][:, :], r64[3][:, :], -1.0, 8.0, ALU.mult, ALU.add)
                for e in range(8):
                    v.tensor_scalar(r64[1][:, :], r64[4][:, :], float(e), None, ALU.is_equal)
                    v.scalar_tensor_tensor(rw2[:, :, e], r64[1][:, :], -1e30, rw[:, :, e],
                                           ALU.mult, ALU.add)
                v.tensor_max(r64[0][:, :], rw2[:, :, 0], rw2[:, :, 1])
                for e in range(2, 8):
                    v.tensor_max(r64[0][:, :], r64[0][:, :], rw2[:, :, e])
                for e in range(8):
                    v.tensor_tensor(r64[1][:, :], rw2[:, :, e], r64[0][:, :], ALU.is_equal)
                    v.tensor_scalar_mul(r64[2][:, :], r64[1][:, :], float(8 - e))
                    if e == 0:
                        v.tensor_copy(r64[5][:, :], r64[2][:, :])
                    else:
                        v.tensor_max(r64[5][:, :], r64[5][:, :], r64[2][:, :])
                return v.tensor_scalar(r64[5][:, :], r64[5][:, :], -1.0, 8.0, ALU.mult, ALU.add)
            tr.step("vector", em_top, waits=["u_done"], tag="top_done")

            def em_etr(t):
                t.transpose(ps_a[0:64, 0:128], r64[4][:, :], identt[:, :])
                t.transpose(ps_a[0:64, 512:640], r64[5][:, :], identt[:, :])
                return drain(t)
            tr.step("tensor", em_etr, waits=["top_done"], tag="etr")

            def em_eev(v):
                v.tensor_copy(ea[:, :], ps_a[0:64, 0:128])
                v.tensor_copy(eb[:, :], ps_a[0:64, 512:640])
                return v.memset(zer[:, :], 0.0)
            tr.step("vector", em_eev, waits=["etr"], tag="eev")

            def em_scan(v):
                for e in range(8):
                    v.tensor_scalar(ind[:, :], ea[:, :], float(e), None, ALU.is_equal)
                    v.tensor_tensor_scan(sca[:, e, :], ind[:, :], zer[:, :], 0.0,
                                         ALU.add, ALU.add)
                    v.tensor_scalar(ind[:, :], eb[:, :], float(e), None, ALU.is_equal)
                    v.tensor_tensor_scan(scb[:, e, :], ind[:, :], zer[:, :], 0.0,
                                         ALU.add, ALU.add)
                for e in range(8):
                    v.tensor_copy(t01[:, e:e + 1], sca[:, e, 127:128])
                    v.tensor_copy(t01[:, 8 + e:8 + e + 1], scb[:, e, 127:128])
                return None
            tr.step("vector", em_scan, waits=["eev"], tag="scans")

            def em_tmm(t):
                t.matmul(ps_b[0:128, 8:16], idlow[:, :], t01[:, 0:8], start=True, stop=False,
                         skip_group_check=True)
                t.matmul(ps_b[0:128, 8:16], sh64[:, :], t01[:, 8:16], start=False,
                         stop=True, skip_group_check=True)
                return drain(t)
            tr.step("tensor", em_tmm, waits=["scans"], tag="tmm")
            tr.step("vector", lambda v: v.tensor_copy(tsb[:, :], ps_b[0:128, 8:16]),
                    waits=["tmm"], tag="tev")
            def em_offmm(t):
                t.matmul(ps_b[0:128, 16:24], lstr[:, :], tsb[:, :],
                         start=True, stop=True, skip_group_check=True)
                return drain(t)
            tr.step("tensor", em_offmm, waits=["tev"], tag="offmm")
            tr.step("vector", lambda v: v.tensor_copy(offs[:, :], ps_b[0:128, 16:24]),
                    waits=["offmm"], tag="offev")
            def em_off1mm(t):
                t.matmul(ps_b[0:64, 24:32], shdn[:, :], offs[:, :],
                         start=True, stop=True, skip_group_check=True)
                return drain(t)
            tr.step("tensor", em_off1mm, waits=["offev"], tag="off1mm")
            tr.step("vector", lambda v: v.tensor_copy(off1[:, :], ps_b[0:64, 24:32]),
                    waits=["off1mm"], tag="off1ev")

            def em_pp(v):
                for e in range(8):
                    v.tensor_scalar(ind[:, :], ea[:, :], float(e), None, ALU.is_equal)
                    v.tensor_scalar(ppx[2][:, :], sca[:, e, :], offs[0:64, e:e + 1], None, ALU.add)
                    v.tensor_mul(ppx[3][:, :], ppx[2][:, :], ind[:, :])
                    if e == 0:
                        v.tensor_copy(ppx[0][:, :], ppx[3][:, :])
                    else:
                        v.tensor_add(ppx[0][:, :], ppx[0][:, :], ppx[3][:, :])
                for e in range(8):
                    v.tensor_scalar(ind[:, :], eb[:, :], float(e), None, ALU.is_equal)
                    v.tensor_scalar(ppx[2][:, :], scb[:, e, :], off1[:, e:e + 1], None, ALU.add)
                    v.tensor_mul(ppx[3][:, :], ppx[2][:, :], ind[:, :])
                    if e == 0:
                        v.tensor_copy(ppx[1][:, :], ppx[3][:, :])
                    else:
                        v.tensor_add(ppx[1][:, :], ppx[1][:, :], ppx[3][:, :])
                v.tensor_scalar(ppx[2][:, :], ppx[0][:, :], float(CAP), None, ALU.is_le)
                v.tensor_scalar(ppx[3][:, :], ppx[1][:, :], float(CAP), None, ALU.is_le)
                return None
            tr.step("vector", em_pp, waits=["off1ev"], tag="pp_done")

            def em_ownx(t):
                t.matmul(ps_a[0:8, 256:384], selc8[:, :], ppx[2][:, :], start=True, stop=True,
                         skip_group_check=True)
                t.matmul(ps_a[0:8, 640:768], selc8[:, :], ppx[3][:, :], start=True, stop=True,
                         skip_group_check=True)
                t.matmul(ps_c[0:8, 512:640], selc8[:, :], ea[:, :], start=True, stop=True,
                         skip_group_check=True)
                t.transpose(ps_b[0:8, 128:256], rstdtok[:, :], identt[:, :])
                return drain(t)
            tr.step("tensor", em_ownx, waits=["pp_done"], tag="ownx")

            def em_ownev(v):
                v.tensor_copy(own8[:, 0, :], ps_a[0:8, 256:384])
                v.tensor_copy(own8[:, 1, :], ps_a[0:8, 640:768])
                v.tensor_copy(own8[:, 2, :], ps_c[0:8, 512:640])
                return v.tensor_copy(own8[:, 3, :], ps_b[0:8, 128:256])
            tr.step("vector", em_ownev, waits=["ownx"], tag="ownev")
            def em_utr(t):
                for e in range(8):
                    t.transpose(ps_c[0:64, 128 * e:128 * e + 128], rw[:, :, e], identt[:, :])
                return drain(t)
            tr.step("tensor", em_utr, waits=["ownev"], tag="utr_all")
            tr.step("vector", lambda v: v.tensor_copy(
                A(utmp8).rearrange("p a c -> p (a c)"), ps_c[0:64, 0:1024]),
                waits=["utr_all"], tag="utev_all")

            def em_uxm(t):
                for e in range(8):
                    t.matmul(ps_a[0:8, 128 * e:128 * e + 128], selc8[:, :], utmp8[:, e, :],
                             start=(e % 4 == 0), stop=True, skip_group_check=True)
                return drain(t)
            tr.step("tensor", em_uxm, waits=["utev_all"], tag="uxm_all")
            tr.step("vector", lambda v: v.tensor_copy(
                A(uo).rearrange("p a c -> p (a c)"), ps_a[0:8, 0:1024]),
                waits=["uxm_all"], tag="uoev7")

            def em_pmath(v):
                for e in range(8):
                    v.tensor_mul(po[:, e, :], uo[:, e, :], own8[:, 3, :])
                return None
            tr.step("vector", em_pmath, waits=["uoev7", "ownev"], tag="l_done")
            tr.step("scalar", lambda sc: sc.activation(A(uo).rearrange("p a b -> p (a b)"),
                                                       A(po).rearrange("p a b -> p (a b)"),
                                                       ACT.Exp, bias=0.0, scale=1.0),
                    waits=["l_done"], tag="pexp")

            def em_pvals(v):
                v.tensor_add(o8[0][:, :], uo[:, 0, :], uo[:, 1, :])
                for e in range(2, 8):
                    v.tensor_add(o8[0][:, :], o8[0][:, :], uo[:, e, :])
                v.reciprocal(o8[1][:, :], o8[0][:, :])
                v.tensor_max(o8[2][:, :], uo[:, 0, :], uo[:, 1, :])
                for e in range(2, 8):
                    v.tensor_max(o8[2][:, :], o8[2][:, :], uo[:, e, :])
                v.tensor_mul(o8[2][:, :], o8[2][:, :], o8[1][:, :])
                for e in range(8):
                    v.tensor_scalar(o8[3][:, :], own8[:, 2, :], float(e), None, ALU.is_equal)
                    v.tensor_scalar(o8[3][:, :], o8[3][:, :], -1.0, 1.0, ALU.mult, ALU.add)
                    v.tensor_mul(po[:, e, :], uo[:, e, :], o8[3][:, :])
                v.tensor_max(o8[3][:, :], po[:, 0, :], po[:, 1, :])
                for e in range(2, 8):
                    v.tensor_max(o8[3][:, :], o8[3][:, :], po[:, e, :])
                v.tensor_mul(o8[3][:, :], o8[3][:, :], o8[1][:, :])
                v.tensor_mul(o8[4][:, :], own8[:, 0, :], o8[2][:, :])
                v.tensor_mul(o8[0][:, :], own8[:, 1, :], o8[3][:, :])
                return v.tensor_add(o8[4][:, :], o8[4][:, :], o8[0][:, :])
            tr.step("vector", em_pvals, waits=["pexp"], tag="pvals")
            def em_gtr(t):
                t.transpose(ps_c[0:128, 256:264], o8[4][:, :], identt[0:8, 0:8])
                return drain(t)
            tr.step("tensor", em_gtr, waits=["pvals"], tag="g_tr")
            tr.step("vector", lambda v: v.tensor_copy(gtok[:, :], ps_c[0:128, 256:264]),
                    waits=["g_tr"], tag="g_ev")
            with nc.Block("p5a") as bk:
                tr.emit(bk, sems)
            carry = {k: tr.events[k] for k in ["consts", "faccspill", "attdone", "wodone", "g_ev"]}
            cnts = dict(tr.counts)

        with ExitStack() as s6:
            acc = s6.enter_context(nc.sbuf_tensor("acc", [128, 8, H], F32))
            cmb = s6.enter_context(nc.sbuf_tensor("cmb", [128, 512], F32))
            xst = s6.enter_context(nc.sbuf_tensor("xst", [128, 1, H], F32))
            fst = s6.enter_context(nc.sbuf_tensor("fst", [128, 1, H], F32))
            pw = [s6.enter_context(nc.psum_tensor(f"pw{i}", [128, 1024], F32)) for i in range(4)]
            pwv = [pw[i // 2][:, 512 * (i % 2):512 * (i % 2) + 512] for i in range(4)]
            pwv2 = [pw[2 + i // 2][:, 512 * (i % 2):512 * (i % 2) + 512] for i in range(4)]
            tr = Tracer(cnts, carry)
            for tc in range(8):
                tr.step("sync", lambda sy, tc=tc: sy.dma_start(
                    out=xst[:, 0, :], in_=d_xacc[:, tc, :]),
                    waits=[f"cmb{tc-1}_3"] if tc >= 1 else ["attdone"],
                    tag=f"xld{tc}", sem="dXLD0")
                tr.step("gpsimd", lambda g, tc=tc: g.dma_start(
                    out=fst[:, 0, :], in_=d_facc[tc, :, :]),
                    waits=[f"cmb{tc-1}_3"] if tc >= 1 else ["faccspill", "attdone"],
                    tag=f"fld{tc}", sem="dFLD0")

                def em_wo(t, tc=tc):
                    ps = pwv if tc % 2 == 0 else pwv2
                    slw = A(attns)[:, 2 * tc:2 * tc + 2, :].rearrange(
                        "p h (s a) -> p h s a", a=16)
                    for ac in range(16):
                        lhs = slw[:, :, :, ac:ac + 1]
                        for hq in range(4):
                            last = t.matmul(ps[hq], lhs,
                                            wos[:, ac, 512 * hq:512 * hq + 512],
                                            start=(ac == 0), stop=(ac == 15),
                                            skip_group_check=True)
                    return last
                if tc == 0:
                    ww = ["attdone", "wodone", "g_ev"]
                elif tc == 1:
                    ww = ["g_ev"]
                else:
                    ww = [f"cmb{tc-2}_3"]
                tr.step("tensor", em_wo, waits=ww, tag=f"wo_mm{tc}")

                for hq in range(4):
                    def em_cmb(v, tc=tc, hq=hq):
                        ps = (pwv if tc % 2 == 0 else pwv2)[hq]
                        asl = acc[:, tc, 512 * hq:512 * hq + 512]
                        hsl = slice(512 * hq, 512 * hq + 512)
                        v.scalar_tensor_tensor(cmb[:, :], b2_bc[:, hsl],
                                               gtok[:, tc:tc + 1], ps, ALU.mult, ALU.add)
                        v.scalar_tensor_tensor(asl, fst[:, 0, hsl], gtok[:, tc:tc + 1],
                                               cmb[:, :], ALU.mult, ALU.add)
                        v.tensor_add(cmb[:, :], xst[:, 0, hsl], bo_bc[:, hsl])
                        return v.tensor_add(asl, asl, cmb[:, :])
                    tr.step("vector", em_cmb,
                            waits=[f"wo_mm{tc}", f"xld{tc}", f"fld{tc}", "g_ev"],
                            tag=f"cmb{tc}_{hq}")
            tr.step("sync", lambda sy: sy.dma_start(out=d_out[:, :, :], in_=A(acc)),
                    waits=["cmb7_3"], tag="stored", sem="dOUT")
            tr.step("gpsimd", lambda g: None, waits=["stored"], tag="final", sem="dOUT")
            with nc.Block("p5b") as bk:
                tr.emit(bk, sems)
        eAW.close()

    return nc


_PROG = None


def _get_prog():
    global _PROG
    if _PROG is None:
        _PROG = build_program()
    return _PROG


def kernel(L0_attn_X, L0_attn_Wq, L0_attn_bq, L0_attn_Wk, L0_attn_bk, L0_attn_Wv,
           L0_attn_bv, L0_attn_Wo, L0_attn_bo, L0_ffn_router_w, L0_ffn_w1, L0_ffn_b1,
           L0_ffn_w2, L0_ffn_b2):
    X = np.ascontiguousarray(np.asarray(L0_attn_X, dtype=np.float32)).reshape(NTOK, H)
    Wq = np.asarray(L0_attn_Wq, np.float32)
    Wk = np.asarray(L0_attn_Wk, np.float32)
    Wv = np.asarray(L0_attn_Wv, np.float32)
    Wo = np.asarray(L0_attn_Wo, np.float32)
    bq = np.asarray(L0_attn_bq, np.float32)
    bk = np.asarray(L0_attn_bk, np.float32)
    bv = np.asarray(L0_attn_bv, np.float32)
    bo = np.asarray(L0_attn_bo, np.float32)
    rt = np.asarray(L0_ffn_router_w, np.float32)
    w1 = np.asarray(L0_ffn_w1, np.float32)
    b1 = np.asarray(L0_ffn_b1, np.float32)
    w2 = np.asarray(L0_ffn_w2, np.float32)
    b2 = np.asarray(L0_ffn_b2, np.float32)

    XT = np.ascontiguousarray(X.T)
    xstats = np.ascontiguousarray(XT.reshape(DK, 128, NQ, QTR).transpose(2, 1, 0, 3))
    wo_h = np.ascontiguousarray(Wo.T.reshape(16, 128, H).transpose(1, 0, 2))
    w1_h = np.ascontiguousarray(
        w1.T.reshape(DK, 128, NFG, FG, 128).transpose(2, 1, 3, 0, 4))
    w2_h = np.ascontiguousarray(w2.T.reshape(NFG, FG, 128, H).transpose(0, 2, 1, 3))
    rt1 = np.zeros((H, 10), np.float32)
    rt1[:, 0:8] = rt.T
    rt1[:, 8] = 1.0
    rt1_h = np.ascontiguousarray(rt1.reshape(DK, 128, 10).transpose(1, 0, 2))
    b1_t = np.ascontiguousarray(b1.reshape(64, 128).T)
    ident = np.eye(128, dtype=np.float32)
    lstrict = np.ascontiguousarray(np.tril(np.ones((128, 128), np.float32), -1).T)
    idlow = np.zeros((64, 128), np.float32)
    idlow[np.arange(64), np.arange(64)] = 1.0
    shift64 = np.zeros((64, 128), np.float32)
    shift64[np.arange(64), np.arange(64) + 64] = 1.0
    shiftdn = np.zeros((128, 64), np.float32)
    shiftdn[np.arange(64) + 64, np.arange(64)] = 1.0
    onesd = np.zeros((2, 128), np.float32)
    onesd[1, :] = 1.0
    sel64 = np.zeros((65, 128), np.float32)
    sel64[64, :] = 1.0
    onesc = np.zeros((2, 128), np.float32)
    onesc[0, :] = 1.0
    onesf = np.ones((128, 1), np.float32)
    onesbf = np.ones((128, 1), BF)

    nc = _get_prog()
    in_maps = []
    for c in range(8):
        b = c // 2
        cp = c % 2
        seq = XT[:, S * b:S * (b + 1)]
        xown = np.ascontiguousarray(seq.reshape(DK, 128, 4, QTR).transpose(2, 1, 0, 3))
        ownT = seq[:, OWN * cp:OWN * (cp + 1)]
        xow = np.ascontiguousarray(ownT.reshape(DK, 128, 2, QTR).transpose(2, 1, 0, 3))
        xacc = np.ascontiguousarray(
            X[OWN * c:OWN * (c + 1)].reshape(8, 128, H).transpose(1, 0, 2))
        qcols = np.empty(1024, np.int64)
        i = 0
        for ch in range(8):
            g_, ckv = ch // 2, ch % 2
            for kvpar in range(2):
                kvg = 4 * cp + 2 * ckv + kvpar
                for d in range(64):
                    qcols[i] = kvg * 256 + g_ * 64 + d
                    i += 1
        WqTs = np.ascontiguousarray(Wq.T[:, qcols])
        wq_h = np.ascontiguousarray(
            WqTs.reshape(DK, 128, 8, 128).transpose(1, 2, 0, 3))
        bq_t = np.ascontiguousarray(bq[qcols].reshape(8, 128).T)
        kvrows = np.arange(4 * cp * 64, (4 * cp + 4) * 64)
        WkTs = np.ascontiguousarray(Wk[kvrows, :].T)
        wk_h = np.ascontiguousarray(WkTs.reshape(DK, 128, 256).transpose(1, 0, 2))
        WvTs = np.ascontiguousarray(Wv[kvrows, :].T)
        wv_h = np.ascontiguousarray(WvTs.reshape(DK, 128, 256).transpose(1, 0, 2))
        bk_t = np.ascontiguousarray(bk[kvrows].reshape(2, 128).T)
        brow = np.zeros((3, H), np.float32)
        brow[0] = bo
        brow[1] = b2
        brow[2, 0:256] = bv[kvrows]
        selc8 = np.zeros((64, 8), np.float32)
        for j in range(8):
            selc8[8 * c + j, j] = 1.0
        in_maps.append({
            "xownseq": xown, "xowhalf": xow, "xstats": xstats, "xacc": xacc,
            "wq": wq_h, "wk": wk_h, "wv": wv_h, "wo": wo_h,
            "w1": w1_h, "w2": w2_h, "rt1": rt1_h,
            "bqt": bq_t, "bkt": bk_t, "b1t": b1_t, "brow": brow,
            "identc": ident, "lstrictc": lstrict, "idlowc": idlow,
            "shift64c": shift64, "shiftdnc": shiftdn, "onesdc": onesd,
            "sel64c": sel64, "onescc": onesc, "onesfc": onesf,
            "onesbfc": onesbf, "selc8c": selc8,
        })
    res = run_bass_kernel_spmd(nc, in_maps, core_ids=list(range(8)))
    parts = []
    for c in range(8):
        o = res.results[c]["outp"]
        parts.append(np.ascontiguousarray(np.asarray(o).transpose(1, 0, 2)).reshape(OWN, H))
    return np.concatenate(parts, axis=0).reshape(Bb, S, H)


# version 13
# speedup vs baseline: 1.0006x; 1.0006x over previous
import numpy as np
import ml_dtypes
from contextlib import ExitStack
from functools import partial

import concourse.bass as bass
import concourse.mybir as mybir
from concourse.bass_utils import run_bass_kernel_spmd

F32 = mybir.dt.float32
BF16 = mybir.dt.bfloat16
ALU = mybir.AluOpType
ACT = mybir.ActivationFunctionType
AX = mybir.AxisListType
BF = ml_dtypes.bfloat16

H = 2048
S = 2048
Bb = 4
NTOK = Bb * S
OWN = 1024
CAP = 512
FF = 8192
DK = 16
QTR = 512
NQ = NTOK // QTR
FG = 8
NFG = FF // 128 // FG
NU = 2 * NFG


def A(th):
    return th[(slice(None),) * len(th.shape)]


class VPad:

    def __init__(self, v, n=6):
        self._v = v
        self._n = n

    def __getattr__(self, name):
        f = getattr(self._v, name)
        if name in ("nop", "wait_ge") or not callable(f):
            return f

        def wrapped(*a, **k):
            r = f(*a, **k)
            for _ in range(self._n):
                self._v.nop()
            return r
        return wrapped


class Tracer:
    ENGINES = ("sync", "gpsimd", "tensor", "scalar", "vector")

    def __init__(self, counts=None, events=None):
        self.steps = []
        self.counts = dict(counts) if counts else {}
        self.events = dict(events) if events else {}

    def step(self, engine, emit_fn, waits=(), tag=None, sem=None):
        if sem is None:
            assert engine not in ("sync", "gpsimd"), "DMA steps need an explicit stream sem"
            sem = engine
        inc = 16 if engine in ("sync", "gpsimd") else 1
        for w in waits:
            assert w in self.events, f"wait on unknown/future event {w!r}"
        self.counts[sem] = self.counts.get(sem, 0) + inc
        if tag is not None:
            assert tag not in self.events, f"dup tag {tag}"
            self.events[tag] = (sem, self.counts[sem])
        self.steps.append((engine, tuple(waits), emit_fn, inc, sem))

    def emit(self, bk, sems):
        per = {e: [] for e in self.ENGINES}
        for st in self.steps:
            per[st[0]].append(st)

        def mk(engine):
            def _(eng):
                if engine == "vector" and getattr(self, "pad_vector", False):
                    eng = VPad(eng)
                for _e, waits, emit_fn, inc, sem in per[engine]:
                    done = set()
                    for w in waits:
                        ws, wv = self.events[w]
                        if ws == engine or (ws, wv) in done:
                            continue
                        done.add((ws, wv))
                        eng.wait_ge(sems[ws], wv)
                    last = emit_fn(eng)
                    if last is None:
                        last = eng.nop()
                    last.then_inc(sems[sem], inc)
            return _

        for e in self.ENGINES:
            if per[e]:
                getattr(bk, e)(mk(e))


def build_program():
    nc = bass.Bass("TRN2", target_bir_lowering=False, detect_race_conditions=False)

    def din(name, shape, dt=F32):
        return nc.dram_tensor(name, shape, dt, kind="ExternalInput").ap()

    d_xown = din("xownseq", [4, 128, DK, QTR])
    d_xow = din("xowhalf", [2, 128, DK, QTR])
    d_xstats = din("xstats", [NQ, 128, DK, QTR])
    d_xacc = din("xacc", [128, 8, H])
    d_wq = din("wq", [128, 8, DK, 128])
    d_wk = din("wk", [128, DK, 256])
    d_wv = din("wv", [128, DK, 256])
    d_wo = din("wo", [128, 16, H])
    d_w1 = din("w1", [NFG, 128, FG, DK, 128])
    d_w2 = din("w2", [NFG, 128, FG, H])
    d_rt1 = din("rt1", [128, DK, 10])
    d_bq = din("bqt", [128, 8])
    d_bk = din("bkt", [128, 2])
    d_b1 = din("b1t", [128, 64])
    d_brow = din("brow", [3, H])
    d_ident = din("identc", [128, 128])
    d_lstrict = din("lstrictc", [128, 128])
    d_idlow = din("idlowc", [64, 128])
    d_sh64 = din("shift64c", [64, 128])
    d_shdn = din("shiftdnc", [128, 64])
    d_onesd = din("onesdc", [2, 128])
    d_sel64 = din("sel64c", [65, 128])
    d_onesc = din("onescc", [2, 128])
    d_onesf = din("onesfc", [128, 1])
    d_onesbf = din("onesbfc", [128, 1], BF16)
    d_selc8 = din("selc8c", [64, 8])
    d_out = nc.dram_tensor("outp", [128, 8, H], F32, kind="ExternalOutput").ap()
    d_xnsp = nc.dram_tensor("xnsp", [DK, 128, S], BF16, kind="Internal", addr_space="Local").ap()
    d_xnow = nc.dram_tensor("xnow", [DK, 128, OWN], BF16, kind="Internal", addr_space="Local").ap()
    d_facc = nc.dram_tensor("faccsp", [8, 128, H], F32, kind="Internal", addr_space="Local").ap()

    es = ExitStack()
    eAW = ExitStack()
    with es:
        sb = lambda name, shape, dt: es.enter_context(nc.sbuf_tensor(name, shape, dt))
        identt = sb("identt", [128, 128], F32)
        lstr = sb("lstrt", [128, 128], F32)
        idlow = sb("idlowt", [64, 128], F32)
        sh64 = sb("sh64t", [64, 128], F32)
        shdn = sb("shdnt", [128, 64], F32)
        onesd = sb("onesdt", [2, 128], F32)
        sel64 = sb("sel64t", [65, 128], F32)
        onesc = sb("onesct", [2, 128], F32)
        onesf = sb("onesft", [128, 1], F32)
        onesbf = sb("onesbft", [128, 1], BF16)
        selc8 = sb("selc8t", [64, 8], F32)
        rt1t = sb("rt1t", [128, DK, 10], F32)
        bqt = sb("bqtt", [128, 8], F32)
        bkt = sb("bktt", [128, 2], F32)
        b1t = sb("b1tt", [128, 64], F32)
        bo_bc = sb("bo_bc", [128, H], F32)
        b2_bc = sb("b2_bc", [128, H], F32)
        bv_bc = sb("bv_bc", [128, 256], F32)
        rtraw = sb("rtraw", [128, 64, 10], F32)
        rstdtok = sb("rstdtok", [128, 8], F32)
        gtok = sb("gtok", [128, 8], F32)

        SEMNAMES = ["tensor", "scalar", "vector", "dCON", "dXQ0", "dXQ1", "dBGX", "dXLD0",
                    "dXLD1", "dOUT", "dWA", "dW1", "dW2", "dWOS",
                    "dXNF0", "dXNF1", "dXNP0", "dXNP1", "dSPILL", "dSPILL2", "dSPILLS",
                    "dFLD0", "dFLD1"]
        sems = {e: es.enter_context(nc.semaphore("sem_" + e)) for e in SEMNAMES}

        tr = Tracer()
        consts = [
            (identt, d_ident), (lstr, d_lstrict), (idlow, d_idlow),
            (sh64, d_sh64), (shdn, d_shdn), (onesd, d_onesd), (sel64, d_sel64),
            (onesc, d_onesc), (onesf, d_onesf), (onesbf, d_onesbf),
            (selc8, d_selc8), (rt1t, d_rt1), (bqt, d_bq), (bkt, d_bk), (b1t, d_b1),
        ]
        for dst, src in consts:
            tr.step("sync", lambda sy, dst=dst, src=src: sy.dma_start(out=A(dst), in_=src),
                    sem="dCON")
        tr.step("sync", lambda sy: sy.dma_start(out=bo_bc[:, :], in_=d_brow[0:1, :].broadcast_to([128, H])), sem="dCON")
        tr.step("sync", lambda sy: sy.dma_start(out=b2_bc[:, :], in_=d_brow[1:2, :].broadcast_to([128, H])), sem="dCON")
        tr.step("sync", lambda sy: sy.dma_start(out=bv_bc[:, :], in_=d_brow[2:3, 0:256].broadcast_to([128, 256])),
                sem="dCON", tag="consts")
        with nc.Block("p0") as bk:
            tr.emit(bk, sems)
        carry = {"consts": tr.events["consts"]}
        cnts = dict(tr.counts)

        with ExitStack() as s1:
            xq = s1.enter_context(nc.sbuf_tensor("xq", [128, 2, DK, QTR], F32))
            xn = s1.enter_context(nc.sbuf_tensor("xn", [128, DK, S], BF16))
            xnw = s1.enter_context(nc.sbuf_tensor("xnw", [128, DK, OWN], BF16))
            xsqb = s1.enter_context(nc.sbuf_tensor("xsqb", [128, 2, QTR], BF16))
            xqb = s1.enter_context(nc.sbuf_tensor("xqb", [128, 2, QTR], BF16))
            tmpn = s1.enter_context(nc.sbuf_tensor("tmpn", [128, QTR], F32))
            row2 = s1.enter_context(nc.sbuf_tensor("row2", [2, 2, QTR], F32))
            mrow2 = s1.enter_context(nc.sbuf_tensor("mrow2", [2, 2, QTR], F32))
            rva = s1.enter_context(nc.sbuf_tensor("rva", [1, 2, QTR], F32))
            rvb = s1.enter_context(nc.sbuf_tensor("rvb", [1, 2, QTR], F32))
            ps0 = s1.enter_context(nc.psum_tensor("ps0", [128, 2048], F32))
            ps1 = s1.enter_context(nc.psum_tensor("ps1", [128, 512], F32))
            ps2 = s1.enter_context(nc.psum_tensor("ps2", [128, 1024], F32))
            ps3 = s1.enter_context(nc.psum_tensor("ps3", [128, 512], F32))
            R_MSp = [ps0[0:1, 0:512], ps2[0:1, 0:512]]
            R_S2p = [ps0[0:1, 512:1024], ps2[0:1, 512:1024]]
            R_MBCp = [ps0[:, 1024:1536], ps3[:, 0:512]]
            R_RBC = ps0[:, 1536:2048]

            tr = Tracer(cnts, carry)
            def em_r2z(v):
                v.memset(A(row2), 0.0)
                return v.memset(A(mrow2), 0.0)
            tr.step("vector", em_r2z, tag="r2z")
            for q in range(6):
                w = [f"norm_done{q-2}", f"s2mm{q-2}_{DK-1}"] if q >= 2 else []
                src_ap = d_xown[q, :, :, :] if q < 4 else d_xow[q - 4, :, :, :]
                tr.step("sync", lambda sy, src_ap=src_ap, q=q: sy.dma_start(
                    out=xq[:, q % 2, :, :], in_=src_ap),
                    waits=w, tag=f"xq{q}", sem=f"dXQ{q % 2}")

                for dk in range(DK):
                    w = [f"xq{q}"]
                    if dk >= 2:
                        w.append(f"s2mm{q}_{dk-2}")
                    elif q >= 1:
                        w.append(f"s2mm{q-1}_{DK-1}")

                    def em_xsq(sc, q=q, dk=dk):
                        sc.activation(xsqb[:, dk % 2, :], xq[:, q % 2, dk, :], ACT.Square,
                                      bias=0.0, scale=1.0)
                        return sc.activation(xqb[:, dk % 2, :], xq[:, q % 2, dk, :], ACT.Identity,
                                             bias=0.0, scale=1.0)
                    tr.step("scalar", em_xsq, waits=w, tag=f"xsq{q}_{dk}")
                    w2_ = [f"xsq{q}_{dk}"]
                    if dk == 0 and q >= 2:
                        w2_ += [f"ve{q-2}", f"evst{q-2}"]

                    def em_s2(t, q=q, dk=dk):
                        t.matmul(R_S2p[q % 2], onesbf[:, :], xsqb[:, dk % 2, :],
                                 start=(dk == 0), stop=(dk == DK - 1),
                                 skip_group_check=True)
                        return t.matmul(R_MSp[q % 2], onesbf[:, :], xqb[:, dk % 2, :],
                                        start=(dk == 0), stop=(dk == DK - 1),
                                        skip_group_check=True)
                    tr.step("tensor", em_s2, waits=w2_, tag=f"s2mm{q}_{dk}")
                tr.step("vector", lambda v, q=q: v.tensor_scalar_mul(
                    mrow2[0:1, q % 2, :], R_MSp[q % 2], 1.0 / H),
                        waits=[f"s2mm{q}_{DK-1}", "consts", "r2z"]
                        + ([f"mbc{q-2}"] if q >= 2 else []),
                        tag=f"evst{q}")
                tr.step("tensor", lambda t, q=q: t.matmul(
                    R_MBCp[q % 2], onesc[:, :], mrow2[:, q % 2, :],
                    start=True, stop=True, skip_group_check=True),
                        waits=[f"evst{q}"] + ([f"norm_done{q-2}"] if q >= 2 else []),
                        tag=f"mbc{q}")

                def em_ve(v, q=q):
                    p = q % 2
                    v.tensor_copy(rvb[0:1, p, :], R_MBCp[p][0:1, :])
                    v.tensor_mul(rva[0:1, p, :], rvb[0:1, p, :], rvb[0:1, p, :])
                    v.scalar_tensor_tensor(rvb[0:1, p, :], R_S2p[p], 1.0 / H,
                                           rva[0:1, p, :], ALU.mult, ALU.subtract)
                    return v.tensor_scalar_add(rva[0:1, p, :], rvb[0:1, p, :], 1e-5)
                tr.step("vector", em_ve, waits=[f"mbc{q}", f"s2mm{q}_{DK-1}"], tag=f"ve{q}")
                tr.step("scalar", lambda sc, q=q: sc.activation(
                    rvb[0:1, q % 2, :], rva[0:1, q % 2, :], ACT.Sqrt, bias=0.0, scale=1.0),
                        waits=[f"ve{q}"], tag=f"sq{q}")

                def em_rrow(v, q=q):
                    p = q % 2
                    v.reciprocal(row2[0:1, p, :], rvb[0:1, p, :])
                    v.tensor_mul(rvb[0:1, p, :], row2[0:1, p, :], row2[0:1, p, :])
                    v.tensor_mul(rvb[0:1, p, :], rvb[0:1, p, :], rva[0:1, p, :])
                    v.tensor_scalar(rva[0:1, p, :], rvb[0:1, p, :], -0.5, 1.5,
                                    ALU.mult, ALU.add)
                    return v.tensor_mul(row2[0:1, p, :], row2[0:1, p, :], rva[0:1, p, :])
                tr.step("vector", em_rrow, waits=[f"sq{q}", "r2z"], tag=f"rrow{q}")
                tr.step("tensor", lambda t, q=q: t.matmul(
                    R_RBC, onesc[:, :], row2[:, q % 2, :],
                    start=True, stop=True, skip_group_check=True),
                        waits=[f"rrow{q}"] + ([f"norm_done{q-1}"] if q else []),
                        tag=f"rbc{q}")
                if q >= 4:
                    j = q - 4

                    def em_rtr(t, q=q):
                        for i in range(4):
                            t.transpose(ps1[:, i:i + 1],
                                        row2[0:1, q % 2, 128 * i:128 * i + 128],
                                        identt[0:1, 0:1])
                        return t.matmul(ps0[0:1, 0:512], onesbf[:, :], xsqb[:, 0, :],
                                        start=True, stop=True, skip_group_check=True)
                    tr.step("tensor", em_rtr,
                            waits=[f"rrow{q}"] + ([f"rtrev{j-1}"] if j else []), tag=f"rtr{j}")
                    tr.step("vector", lambda v, j=j: v.tensor_copy(rstdtok[:, 4 * j:4 * j + 4],
                                                                   ps1[:, 0:4]),
                            waits=[f"rtr{j}"], tag=f"rtrev{j}")
                for dk in range(DK):
                    def em_norm(v, q=q, dk=dk):
                        v.tensor_sub(tmpn[:, :], xq[:, q % 2, dk, :], R_MBCp[q % 2])
                        if q < 4:
                            return v.tensor_mul(xn[:, dk, QTR * q:QTR * q + QTR],
                                                tmpn[:, :], R_RBC)
                        return v.tensor_mul(xnw[:, dk, QTR * (q - 4):QTR * (q - 4) + QTR],
                                            tmpn[:, :], R_RBC)
                    tr.step("vector", em_norm, waits=[f"rbc{q}"], tag=f"norm{q}_{dk}")
                tr.step("vector", lambda v: None, tag=f"norm_done{q}")
            tr.step("sync", lambda sy: sy.dma_start(out=d_xnsp.rearrange("k p n -> p k n"),
                                                    in_=A(xn)),
                    waits=["norm_done3"], tag="xnspill", sem="dSPILLS")
            tr.step("gpsimd", lambda g: g.dma_start(
                out=d_xnow[:, :, 0:QTR].rearrange("k p n -> p k n"),
                in_=A(xnw)[:, :, 0:QTR]),
                    waits=["norm_done4"], tag="xnowspill0", sem="dSPILL2")
            tr.step("gpsimd", lambda g: g.dma_start(
                out=d_xnow[:, :, QTR:OWN].rearrange("k p n -> p k n"),
                in_=A(xnw)[:, :, QTR:OWN]),
                    waits=["norm_done5"], tag="xnowspill", sem="dSPILL2")
            with nc.Block("p1") as bk:
                tr.emit(bk, sems)
            carry = {"consts": tr.events["consts"], "xnspill": tr.events["xnspill"],
                     "xnowspill0": tr.events["xnowspill0"],
                     "xnowspill": tr.events["xnowspill"]}
            cnts = dict(tr.counts)

        with ExitStack() as s2:
            facc = s2.enter_context(nc.sbuf_tensor("facc", [128, 8, H], F32))
            w1b = s2.enter_context(nc.sbuf_tensor("w1b", [128, FG, DK, 128], BF16))
            w2b = s2.enter_context(nc.sbuf_tensor("w2b", [128, FG, H], BF16))
            mid = s2.enter_context(nc.sbuf_tensor("mid", [128, FG, 512], BF16))
            xnf = s2.enter_context(nc.sbuf_tensor("xnf", [128, 8, 512], BF16))
            xgs = s2.enter_context(nc.sbuf_tensor("xgs", [128, DK, QTR], F32))
            pp = [s2.enter_context(nc.psum_tensor(f"pp{i}", [128, 1024], F32)) for i in range(4)]
            ph1 = [pp[i // 2][:, 512 * (i % 2):512 * (i % 2) + 512] for i in range(8)]

            tr = Tracer(cnts, carry)
            tr.step("gpsimd", lambda g: g.dma_start(out=A(w1b), in_=d_w1[0]), tag="w1_0", sem="dW1")
            tr.step("gpsimd", lambda g: g.dma_start(out=A(w2b), in_=d_w2[0]), tag="w2_0", sem="dW2")

            def bg_quarter(q, cover):
                def em_bgst(t, q=q):
                    for dk in range(DK):
                        for tc4 in range(4):
                            last = t.matmul(
                                pp[0][0:128, 10 * tc4:10 * tc4 + 10],
                                xgs[:, dk, 128 * tc4:128 * tc4 + 128], rt1t[:, dk, :],
                                start=(dk == 0 and tc4 == 0), stop=(dk == DK - 1),
                                skip_group_check=True)
                    return last
                tr.step("tensor", em_bgst, waits=[f"bgx{q}", "consts"] + cover, tag=f"bgst{q}")

                def em_bgrs(v, q=q):
                    return v.tensor_copy(
                        A(rtraw)[:, 4 * q:4 * q + 4, :],
                        pp[0][0:128, 0:40].rearrange("p (a c) -> p a c", c=10))
                tr.step("vector", em_bgrs, waits=[f"bgst{q}"], tag=f"bgrs{q}")

            for u in range(NU):
                fg, hf = u // 2, u % 2
                if u >= 1 and (u - 1) < NQ:
                    bg_quarter(u - 1, [f"gel{u-1}_{FG-1}"])
                if u < NQ:
                    tr.step("sync", lambda sy, u=u: sy.dma_start(out=A(xgs), in_=d_xstats[u]),
                            waits=[f"bgst{u-1}"] if u >= 1 else [], tag=f"bgx{u}", sem="dBGX")

                def xnf_step(gdk):
                    w = ["xnowspill" if hf else "xnowspill0"]
                    if gdk >= 2:
                        w.append(f"h1g{u}_{4*(gdk-2)+3}")
                    elif u >= 1:
                        w.append(f"h1g{u-1}_{4*(gdk+2)+3}")
                    tr.step("gpsimd", lambda g, gdk=gdk, hf=hf: g.dma_start(
                        out=xnf[:, 4 * (gdk % 2):4 * (gdk % 2) + 4, :],
                        in_=d_xnow[4 * gdk:4 * gdk + 4, :, 512 * hf:512 * hf + 512]
                        .rearrange("k p n -> p k n")),
                        waits=w, tag=f"xnf{u}_{gdk}", sem=f"dXNF{gdk % 2}")
                xnf_step(0)
                xnf_step(1)
                for dk in range(DK):
                    gdk = dk // 4

                    def em_h1(t, dk=dk, gdk=gdk):
                        sl = xnf[:, 4 * (gdk % 2) + dk % 4, :]
                        for f in range(FG):
                            last = t.matmul(ph1[f], w1b[:, f, dk, :], sl,
                                            start=(dk == 0), stop=(dk == DK - 1),
                                            skip_group_check=True)
                        return last
                    w = [f"xnf{u}_{gdk}", f"w1_{fg}"]
                    if dk == 0:
                        if u >= 1:
                            w += [f"gel{u-1}_{FG-1}", f"h2e{u-1}_6", f"h2e{u-1}_7"]
                        q = u - 1
                        if 0 <= q < NQ:
                            w.append(f"bgrs{q}")
                    tr.step("tensor", em_h1, waits=w, tag=f"h1g{u}_{dk}")
                    if dk % 4 == 3 and gdk + 2 < 4:
                        xnf_step(gdk + 2)
                for f in range(FG):
                    w = [f"h1g{u}_{DK-1}"]
                    if u >= 1 and f == 0:
                        w.append(f"h2m{u-1}_7")
                    tr.step("scalar", lambda sc, fg=fg, f=f: sc.activation(
                        mid[:, f, :], ph1[f], ACT.Gelu,
                        bias=b1t[:, fg * FG + f:fg * FG + f + 1], scale=1.0),
                        waits=w, tag=f"gel{u}_{f}")
                if hf == 1 and fg + 1 < NFG:
                    tr.step("gpsimd", lambda g, fg=fg: g.dma_start(out=A(w1b), in_=d_w1[fg + 1]),
                            waits=[f"h1g{u}_{DK-1}"], tag=f"w1_{fg+1}", sem="dW1")
                for i in range(8):
                    tc, hh = i // 2, i % 2
                    gtc = 4 * hf + tc

                    def em_h2(t, i=i, tc=tc, hh=hh):
                        for f in range(FG):
                            for c2 in range(2):
                                last = t.matmul(
                                    pp[2 + i % 2][:, 512 * c2:512 * c2 + 512],
                                    mid[:, f, 128 * tc:128 * tc + 128],
                                    w2b[:, f, 1024 * hh + 512 * c2:1024 * hh + 512 * c2 + 512],
                                    start=(f == 0), stop=(f == FG - 1), skip_group_check=True)
                        return last
                    w = [f"gel{u}_{FG-1}", "w2_0" if fg == 0 else f"w2_{fg}"]
                    if i >= 2:
                        w.append(f"h2e{u}_{i-2}")
                    elif u >= 1:
                        w.append(f"h2e{u-1}_{6+i}")
                    tr.step("tensor", em_h2, waits=w, tag=f"h2m{u}_{i}")
                    if u <= 1:
                        def em_h2e(v, gtc=gtc, hh=hh, i=i):
                            return v.tensor_copy(facc[:, gtc, 1024 * hh:1024 * hh + 1024],
                                                 pp[2 + i % 2][:, :])
                    else:
                        def em_h2e(v, gtc=gtc, hh=hh, i=i):
                            sl = facc[:, gtc, 1024 * hh:1024 * hh + 1024]
                            return v.tensor_add(sl, sl, pp[2 + i % 2][:, :])
                    tr.step("vector", em_h2e, waits=[f"h2m{u}_{i}"], tag=f"h2e{u}_{i}")
                if hf == 1 and fg + 1 < NFG:
                    tr.step("gpsimd", lambda g, fg=fg: g.dma_start(out=A(w2b), in_=d_w2[fg + 1]),
                            waits=[f"h2m{u}_7"], tag=f"w2_{fg+1}", sem="dW2")
            bg_quarter(NQ - 1, [f"gel{NU-1}_{FG-1}"])
            tr.step("gpsimd", lambda g: g.dma_start(out=d_facc.rearrange("t p h -> p t h"),
                                                    in_=A(facc)),
                    waits=[f"h2e{NU-1}_7"], tag="faccspill", sem="dSPILL")
            with nc.Block("p2") as bk:
                tr.emit(bk, sems)
            carry = {k: tr.events[k] for k in ["consts", "xnspill", "faccspill"]}
            carry["bglast"] = tr.events[f"bgrs{NQ-1}"]
            cnts = dict(tr.counts)

        with ExitStack() as eqkv:
            qslab = eqkv.enter_context(nc.sbuf_tensor("qslab", [128, 8, S], BF16))
            kslab = eqkv.enter_context(nc.sbuf_tensor("kslab", [128, 2, S], BF16))
            vslab = eqkv.enter_context(nc.sbuf_tensor("vslab", [128, 16, 260], BF16))

            with ExitStack() as s3:
                wks = s3.enter_context(nc.sbuf_tensor("wks", [128, DK, 256], BF16))
                wvs = s3.enter_context(nc.sbuf_tensor("wvs", [128, DK, 256], BF16))
                wqb = s3.enter_context(nc.sbuf_tensor("wqb", [128, 8, DK, 128], BF16))
                xnp = s3.enter_context(nc.sbuf_tensor("xnp", [128, 8, 1024], BF16))
                pp = [s3.enter_context(nc.psum_tensor(f"pq{i}", [128, 1024], F32)) for i in range(4)]
                pv8 = [pp[2 + tc // 4][:, 256 * (tc % 4):256 * (tc % 4) + 256] for tc in range(8)]
                pq8 = [pp[i // 2][:, 512 * (i % 2):512 * (i % 2) + 512] for i in range(8)]

                tr = Tracer(cnts, carry)
                tr.step("gpsimd", lambda g: g.dma_start(out=A(wks), in_=d_wk), sem="dWA")
                tr.step("gpsimd", lambda g: g.dma_start(out=A(wvs), in_=d_wv), sem="dWA")
                tr.step("gpsimd", lambda g: g.dma_start(out=A(wqb), in_=d_wq), sem="dWA",
                        tag="wABurst")
                tr.step("vector", lambda v: v.memset(
                    A(vslab).rearrange("p k (g c) -> p k g c", c=65)[:, :, :, 64:65], 1.0),
                    tag="vones")

                for hf in range(2):
                    def kx_step(gdk, hf=hf):
                        w = ["xnspill"]
                        if gdk >= 2:
                            w.append(f"kv{hf}_{4*(gdk-2)+3}")
                        elif hf == 1:
                            w.append(f"kv0_{4*(gdk+2)+3}")
                        tr.step("gpsimd", lambda g, gdk=gdk, hf=hf: g.dma_start(
                            out=xnp[:, 4 * (gdk % 2):4 * (gdk % 2) + 4, :],
                            in_=d_xnsp[4 * gdk:4 * gdk + 4, :, 1024 * hf:1024 * hf + 1024]
                            .rearrange("k p n -> p k n")),
                            waits=w, tag=f"kx{hf}_{gdk}", sem=f"dXNP{gdk % 2}")
                    kx_step(0)
                    kx_step(1)
                    for dk in range(DK):
                        gdk = dk // 4

                        def em_kv(t, hf=hf, dk=dk, gdk=gdk):
                            sl = xnp[:, 4 * (gdk % 2) + dk % 4, :]
                            for oc in range(2):
                                for c2 in range(2):
                                    t.matmul(pp[oc][:, 512 * c2:512 * c2 + 512],
                                             wks[:, dk, 128 * oc:128 * oc + 128],
                                             sl[:, 512 * c2:512 * c2 + 512],
                                             start=(dk == 0), stop=(dk == DK - 1),
                                             skip_group_check=True)
                            for tc in range(8):
                                last = t.matmul(pv8[tc], sl[:, 128 * tc:128 * tc + 128],
                                                wvs[:, dk, :],
                                                start=(dk == 0 and tc % 2 == 0),
                                                stop=(dk == DK - 1),
                                                skip_group_check=True)
                            return last
                        kw = [f"kx{hf}_{gdk}", "wABurst"]
                        if dk == 0:
                            if hf == 0:
                                kw.append("bglast")
                            else:
                                kw += [f"kev0_{oc}" for oc in range(2)]
                                kw += [f"vev0_{tc}" for tc in range(8)]
                        tr.step("tensor", em_kv, waits=kw, tag=f"kv{hf}_{dk}")
                        if dk % 4 == 3 and gdk + 2 < 4:
                            kx_step(gdk + 2)
                    for oc in range(2):
                        tr.step("vector", lambda v, hf=hf, oc=oc: v.tensor_scalar(
                            kslab[:, oc, OWN * hf:OWN * hf + OWN], pp[oc][:, :],
                            bkt[:, oc:oc + 1], None, ALU.add),
                            waits=[f"kv{hf}_{DK-1}"], tag=f"kev{hf}_{oc}")
                    for tc in range(8):
                        def em_vev(v, hf=hf, tc=tc):
                            kt = 8 * hf + tc
                            return v.tensor_tensor(
                                A(vslab)[:, kt:kt + 1, :].rearrange(
                                    "p o (g c) -> p (o g) c", c=65)[:, :, 0:64],
                                pv8[tc].rearrange("p (g c) -> p g c", c=64),
                                bv_bc[:, :].rearrange("p (g c) -> p g c", c=64),
                                ALU.add)
                        tr.step("vector", em_vev, waits=[f"kv{hf}_{DK-1}", "vones"],
                                tag=f"vev{hf}_{tc}")

                for tq in range(4):
                    def qx_step(gdk, tq=tq):
                        w = []
                        if gdk >= 2:
                            w.append(f"qprj{tq}_{4*(gdk-2)+3}")
                        elif tq == 0:
                            w.append(f"kv1_{4*(gdk+2)+3}")
                        else:
                            w.append(f"qprj{tq-1}_{4*(gdk+2)+3}")
                        tr.step("gpsimd", lambda g, gdk=gdk, tq=tq: g.dma_start(
                            out=xnp[:, 4 * (gdk % 2):4 * (gdk % 2) + 4, 0:512],
                            in_=d_xnsp[4 * gdk:4 * gdk + 4, :, 512 * tq:512 * tq + 512]
                            .rearrange("k p n -> p k n")),
                            waits=w, tag=f"qx{tq}_{gdk}", sem=f"dXNP{gdk % 2}")
                    qx_step(0)
                    qx_step(1)
                    for dk in range(DK):
                        gdk = dk // 4

                        def em_q(t, dk=dk, gdk=gdk):
                            sl = xnp[:, 4 * (gdk % 2) + dk % 4, 0:512]
                            for j in range(8):
                                last = t.matmul(pq8[j], wqb[:, j, dk, :], sl,
                                                start=(dk == 0), stop=(dk == DK - 1),
                                                skip_group_check=True)
                            return last
                        qw = [f"qx{tq}_{gdk}", "wABurst"]
                        if dk == 0:
                            if tq == 0:
                                qw += [f"kev1_{oc}" for oc in range(2)]
                                qw += [f"vev1_{tc}" for tc in range(8)]
                            else:
                                qw += [f"qe{tq-1}_{j}" for j in range(8)]
                        tr.step("tensor", em_q, waits=qw, tag=f"qprj{tq}_{dk}")
                        if dk % 4 == 3 and gdk + 2 < 4:
                            qx_step(gdk + 2)
                    for j in range(8):
                        def em_qe(v, j=j, tq=tq):
                            v.tensor_scalar(
                                qslab[:, j, 256 * tq:256 * tq + 256],
                                pq8[j].rearrange("p (a two) -> p a two", two=2)[:, :, 0],
                                bqt[:, j:j + 1], None, ALU.add)
                            return v.tensor_scalar(
                                qslab[:, j, 1024 + 256 * tq:1024 + 256 * tq + 256],
                                pq8[j].rearrange("p (a two) -> p a two", two=2)[:, :, 1],
                                bqt[:, j:j + 1], None, ALU.add)
                        tr.step("vector", em_qe, waits=[f"qprj{tq}_{DK-1}"], tag=f"qe{tq}_{j}")
                with nc.Block("p3") as bk:
                    tr.emit(bk, sems)
                carry = {k: tr.events[k] for k in ["consts", "faccspill", "bglast"]}
                carry["qlast"] = tr.events["qe3_7"]
                carry["qhalf"] = tr.events["qe1_7"]
                carry["kvlast"] = tr.events["vev1_7"]
                cnts = dict(tr.counts)

            attns = eAW.enter_context(nc.sbuf_tensor("attns", [128, 16, OWN], BF16, side="right"))
            wos = eAW.enter_context(nc.sbuf_tensor("wos", [128, 16, H], BF16, side="right"))
            with ExitStack() as s4:
                probs = s4.enter_context(nc.sbuf_tensor("probs", [128, 2, 1024], BF16))
                avsb = s4.enter_context(nc.sbuf_tensor("avsb", [65, 2, 1024], F32))
                avtmp = s4.enter_context(nc.sbuf_tensor("avtmp", [64, 512], F32))
                rr = s4.enter_context(nc.sbuf_tensor("rr", [65, 512], F32))
                rbsb = s4.enter_context(nc.sbuf_tensor("rbsb", [64, 512], F32))
                ps_s = [s4.enter_context(nc.psum_tensor(f"ps_s{i}", [128, 1024], F32))
                        for i in range(2)]
                ps_av = s4.enter_context(nc.psum_tensor("ps_av", [128, 1024], F32))
                ps_ep = s4.enter_context(nc.psum_tensor("ps_ep", [128, 1024], F32))

                tr = Tracer(cnts, carry)
                for i in range(4):
                    tr.step("gpsimd", lambda g, i=i: g.dma_start(
                        out=wos[:, 4 * i:4 * i + 4, :], in_=d_wo[:, 4 * i:4 * i + 4, :]),
                        tag=f"wo{i}", sem="dWOS")
                tr.step("vector", lambda v: v.memset(rr[0:64, :], 0.0), tag="rrz")
                vv = A(vslab).rearrange("p k (g c) -> p k g c", c=65)
                for pr in range(8):
                    g_, ckv = pr // 2, pr % 2
                    hlA = 8 * ckv + g_
                    hlB = 8 * ckv + 4 + g_
                    for qb in range(4):
                        par = qb // 2
                        mb = 512 * (qb % 2)
                        un = pr * 4 + qb

                        def em_sc(t, ckv=ckv, pr=pr, qb=qb, kt=0):
                            psc = ps_s[kt % 2]
                            t.matmul(psc[:, 0:512],
                                     kslab[0:64, ckv, 128 * kt:128 * kt + 128],
                                     qslab[0:64, pr, 512 * qb:512 * qb + 512],
                                     start=True, stop=True,
                                     tile_position=(0, 0), skip_group_check=True)
                            return t.matmul(psc[:, 512:1024],
                                            kslab[64:128, ckv, 128 * kt:128 * kt + 128],
                                            qslab[64:128, pr, 512 * qb:512 * qb + 512],
                                            start=True, stop=True,
                                            tile_position=(64, 0),
                                            skip_group_check=True)

                        def em_exp(sc, kt=0):
                            return sc.activation(probs[:, kt % 2, :], ps_s[kt % 2][:, :],
                                                 ACT.Exp, bias=0.0, scale=0.125)

                        def em_av(t, ckv=ckv, kt=0):
                            t.matmul(ps_av[0:65, 0:512], vv[:, kt, 2 * ckv, :],
                                     probs[:, kt % 2, 0:512],
                                     start=(kt == 0), stop=(kt == 15), skip_group_check=True)
                            return t.matmul(ps_av[0:65, 512:1024], vv[:, kt, 2 * ckv + 1, :],
                                            probs[:, kt % 2, 512:1024],
                                            start=(kt == 0), stop=(kt == 15),
                                            skip_group_check=True)

                        def emit_av(kt):
                            aw = [f"exp{un}_{kt}"]
                            if kt == 0 and un >= 1:
                                aw.append(f"evac{un-1}")
                            tr.step("tensor", partial(em_av, kt=kt), waits=aw,
                                    tag=f"av{un}_{kt}")

                        for kt in range(16):
                            w = []
                            if un == 0 and kt == 0:
                                w += ["qhalf", "kvlast"]
                            if un == 1 and kt == 0:
                                w.append("qlast")
                            if kt >= 2:
                                w.append(f"exp{un}_{kt-2}")
                            elif un >= 1:
                                w.append(f"exp{un-1}_{14 + kt}")
                            tr.step("tensor", partial(em_sc, kt=kt), waits=w,
                                    tag=f"sc{un}_{kt}")
                            ew = [f"sc{un}_{kt}"]
                            if kt >= 2:
                                ew.append(f"av{un}_{kt-2}")
                            elif un >= 1:
                                ew.append(f"av{un-1}_{14 + kt}")
                            tr.step("scalar", partial(em_exp, kt=kt), waits=ew,
                                    tag=f"exp{un}_{kt}")
                            if kt >= 1:
                                emit_av(kt - 1)
                        emit_av(15)
                        tr.step("vector", lambda v, un=un: v.tensor_copy(
                            avsb[:, un % 2, :], ps_av[0:65, 0:1024]),
                            waits=[f"av{un}_15"], tag=f"evac{un}")
                        for hx, hl in ((0, hlA), (1, hlB)):
                            cA = 512 * hx
                            tr.step("vector", lambda v, cA=cA, un=un: v.reciprocal(
                                rr[64:65, :], avsb[64:65, un % 2, cA:cA + 512]),
                                waits=([f"evac{un}", "rrz"] if hx == 0 else [f"rb{un}_0"]),
                                tag=f"rec{un}_{hx}")
                            tr.step("tensor", lambda t, hx=hx: t.matmul(
                                ps_ep[0:128, 512 * hx:512 * hx + 512], sel64[:, :], rr[:, :],
                                start=True, stop=True, skip_group_check=True),
                                waits=[f"rec{un}_{hx}"], tag=f"rb{un}_{hx}")
                            tr.step("vector", lambda v, hx=hx: v.tensor_copy(
                                rbsb[:, :], ps_ep[0:64, 512 * hx:512 * hx + 512]),
                                waits=[f"rb{un}_{hx}"], tag=f"rbe{un}_{hx}")
                            if par == 0:
                                tr.step("vector", lambda v, hl=hl, mb=mb, cA=cA, un=un:
                                        v.tensor_mul(
                                            attns[0:64, hl, mb:mb + 512],
                                            avsb[0:64, un % 2, cA:cA + 512], rbsb[:, :]),
                                        waits=[f"rbe{un}_{hx}"], tag=f"ep{un}_{hx}")
                            else:
                                tr.step("vector", lambda v, cA=cA, un=un: v.tensor_mul(
                                    avtmp[:, :], avsb[0:64, un % 2, cA:cA + 512], rbsb[:, :]),
                                    waits=[f"rbe{un}_{hx}"], tag=f"avt{un}_{hx}")
                                tr.step("tensor", lambda t, hx=hx: t.matmul(
                                    ps_ep[:, 512 * hx:512 * hx + 512], sh64[:, :], avtmp[:, :],
                                    start=True, stop=True, skip_group_check=True),
                                    waits=[f"avt{un}_{hx}"], tag=f"sh{un}_{hx}")
                                tr.step("vector", lambda v, hl=hl, mb=mb, hx=hx: v.tensor_copy(
                                    attns[64:128, hl, mb:mb + 512],
                                    ps_ep[64:128, 512 * hx:512 * hx + 512]),
                                    waits=[f"sh{un}_{hx}"], tag=f"ep{un}_{hx}")
                with nc.Block("p4") as bk:
                    tr.emit(bk, sems)
                carry = {k: tr.events[k] for k in ["consts", "faccspill", "bglast"]}
                carry["attdone"] = tr.events["ep31_1"]
                carry["wodone"] = tr.events["wo3"]
                cnts = dict(tr.counts)

        with ExitStack() as s5:
            r64 = [s5.enter_context(nc.sbuf_tensor(f"r64_{i}", [128, 64], F32)) for i in range(6)]
            rw = s5.enter_context(nc.sbuf_tensor("rw", [128, 64, 8], F32))
            rw2 = s5.enter_context(nc.sbuf_tensor("rw2", [128, 64, 8], F32))
            sbc = s5.enter_context(nc.sbuf_tensor("sbc", [128, 8], F32))
            s8 = s5.enter_context(nc.sbuf_tensor("s8", [2, 8], F32))
            ea = s5.enter_context(nc.sbuf_tensor("ea", [64, 128], F32))
            eb = s5.enter_context(nc.sbuf_tensor("eb", [64, 128], F32))
            ind = s5.enter_context(nc.sbuf_tensor("ind", [64, 128], F32))
            sca = s5.enter_context(nc.sbuf_tensor("sca", [64, 8, 128], F32))
            scb = s5.enter_context(nc.sbuf_tensor("scb", [64, 8, 128], F32))
            zer = s5.enter_context(nc.sbuf_tensor("zer", [64, 128], F32))
            t01 = s5.enter_context(nc.sbuf_tensor("t01", [64, 16], F32))
            tsb = s5.enter_context(nc.sbuf_tensor("tsb", [128, 8], F32))
            offs = s5.enter_context(nc.sbuf_tensor("offs", [128, 8], F32))
            off1 = s5.enter_context(nc.sbuf_tensor("off1", [64, 8], F32))
            ppx = [s5.enter_context(nc.sbuf_tensor(f"ppx{i}", [64, 128], F32)) for i in range(4)]
            own8 = s5.enter_context(nc.sbuf_tensor("own8", [8, 4, 128], F32))
            uo = s5.enter_context(nc.sbuf_tensor("uo", [8, 8, 128], F32))
            po = s5.enter_context(nc.sbuf_tensor("po", [8, 8, 128], F32))
            utmp8 = s5.enter_context(nc.sbuf_tensor("utmp8", [64, 8, 128], F32))
            o8 = [s5.enter_context(nc.sbuf_tensor(f"o8_{i}", [8, 128], F32)) for i in range(5)]
            ps_a = s5.enter_context(nc.psum_tensor("ps_a", [128, 1024], F32))
            ps_b = s5.enter_context(nc.psum_tensor("ps_b", [128, 1024], F32))
            ps_c = s5.enter_context(nc.psum_tensor("ps_c", [128, 1024], F32))

            tr = Tracer(cnts, carry)
            tr.pad_vector = True

            def drain(t):
                return t.matmul(ps_b[0:1, 512:1024], onesbf[:, :], wos[:, 0, 0:512],
                                start=True, stop=True, skip_group_check=True)

            def em_srow(t):
                for dk in range(DK):
                    t.matmul(ps_a[0:1, 0:10], onesf[:, :], rt1t[:, dk, :],
                             start=(dk == 0), stop=(dk == DK - 1),
                             skip_group_check=True)
                return drain(t)
            tr.step("tensor", em_srow, waits=["consts", "attdone"], tag="srow")

            def em_s8(v):
                v.memset(A(s8), 0.0)
                return v.tensor_copy(s8[0:1, :], ps_a[0:1, 0:8])
            tr.step("vector", em_s8, waits=["srow"], tag="s8ev")
            tr.step("tensor", lambda t: t.matmul(ps_b[0:128, 0:8], onesc[:, :], s8[:, :],
                                                 start=True, stop=True, skip_group_check=True),
                    waits=["s8ev"], tag="sbcmm")
            tr.step("vector", lambda v: v.tensor_copy(sbc[:, :], ps_b[0:128, 0:8]),
                    waits=["sbcmm"], tag="sbcev")
            tr.step("vector", lambda v: v.tensor_scalar_mul(r64[0][:, :], rtraw[:, :, 8], 1.0 / H),
                    waits=["bglast"], tag="mtok")

            def em_u(v):
                for e in range(8):
                    v.tensor_scalar(r64[1][:, :], r64[0][:, :], sbc[:, e:e + 1], None, ALU.mult)
                    v.tensor_sub(rw[:, :, e], rtraw[:, :, e], r64[1][:, :])
                return None
            tr.step("vector", em_u, waits=["mtok", "sbcev"], tag="u_done")

            def em_top(v):
                v.tensor_max(r64[0][:, :], rw[:, :, 0], rw[:, :, 1])
                for e in range(2, 8):
                    v.tensor_max(r64[0][:, :], r64[0][:, :], rw[:, :, e])
                for e in range(8):
                    v.tensor_tensor(r64[1][:, :], rw[:, :, e], r64[0][:, :], ALU.is_equal)
                    v.tensor_scalar_mul(r64[2][:, :], r64[1][:, :], float(8 - e))
                    if e == 0:
                        v.tensor_copy(r64[3][:, :], r64[2][:, :])
                    else:
                        v.tensor_max(r64[3][:, :], r64[3][:, :], r64[2][:, :])
                v.tensor_scalar(r64[4][:, :], r64[3][:, :], -1.0, 8.0, ALU.mult, ALU.add)
                for e in range(8):
                    v.tensor_scalar(r64[1][:, :], r64[4][:, :], float(e), None, ALU.is_equal)
                    v.scalar_tensor_tensor(rw2[:, :, e], r64[1][:, :], -1e30, rw[:, :, e],
                                           ALU.mult, ALU.add)
                v.tensor_max(r64[0][:, :], rw2[:, :, 0], rw2[:, :, 1])
                for e in range(2, 8):
                    v.tensor_max(r64[0][:, :], r64[0][:, :], rw2[:, :, e])
                for e in range(8):
                    v.tensor_tensor(r64[1][:, :], rw2[:, :, e], r64[0][:, :], ALU.is_equal)
                    v.tensor_scalar_mul(r64[2][:, :], r64[1][:, :], float(8 - e))
                    if e == 0:
                        v.tensor_copy(r64[5][:, :], r64[2][:, :])
                    else:
                        v.tensor_max(r64[5][:, :], r64[5][:, :], r64[2][:, :])
                return v.tensor_scalar(r64[5][:, :], r64[5][:, :], -1.0, 8.0, ALU.mult, ALU.add)
            tr.step("vector", em_top, waits=["u_done"], tag="top_done")

            def em_etr(t):
                t.transpose(ps_a[0:64, 0:128], r64[4][:, :], identt[:, :])
                t.transpose(ps_a[0:64, 512:640], r64[5][:, :], identt[:, :])
                return drain(t)
            tr.step("tensor", em_etr, waits=["top_done"], tag="etr")

            def em_eev(v):
                v.tensor_copy(ea[:, :], ps_a[0:64, 0:128])
                v.tensor_copy(eb[:, :], ps_a[0:64, 512:640])
                return v.memset(zer[:, :], 0.0)
            tr.step("vector", em_eev, waits=["etr"], tag="eev")

            def em_scan(v):
                for e in range(8):
                    v.tensor_scalar(ind[:, :], ea[:, :], float(e), None, ALU.is_equal)
                    v.tensor_tensor_scan(sca[:, e, :], ind[:, :], zer[:, :], 0.0,
                                         ALU.add, ALU.add)
                    v.tensor_scalar(ind[:, :], eb[:, :], float(e), None, ALU.is_equal)
                    v.tensor_tensor_scan(scb[:, e, :], ind[:, :], zer[:, :], 0.0,
                                         ALU.add, ALU.add)
                for e in range(8):
                    v.tensor_copy(t01[:, e:e + 1], sca[:, e, 127:128])
                    v.tensor_copy(t01[:, 8 + e:8 + e + 1], scb[:, e, 127:128])
                return None
            tr.step("vector", em_scan, waits=["eev"], tag="scans")

            def em_tmm(t):
                t.matmul(ps_b[0:128, 8:16], idlow[:, :], t01[:, 0:8], start=True, stop=False,
                         skip_group_check=True)
                t.matmul(ps_b[0:128, 8:16], sh64[:, :], t01[:, 8:16], start=False,
                         stop=True, skip_group_check=True)
                return drain(t)
            tr.step("tensor", em_tmm, waits=["scans"], tag="tmm")
            tr.step("vector", lambda v: v.tensor_copy(tsb[:, :], ps_b[0:128, 8:16]),
                    waits=["tmm"], tag="tev")
            def em_offmm(t):
                t.matmul(ps_b[0:128, 16:24], lstr[:, :], tsb[:, :],
                         start=True, stop=True, skip_group_check=True)
                return drain(t)
            tr.step("tensor", em_offmm, waits=["tev"], tag="offmm")
            tr.step("vector", lambda v: v.tensor_copy(offs[:, :], ps_b[0:128, 16:24]),
                    waits=["offmm"], tag="offev")
            def em_off1mm(t):
                t.matmul(ps_b[0:64, 24:32], shdn[:, :], offs[:, :],
                         start=True, stop=True, skip_group_check=True)
                return drain(t)
            tr.step("tensor", em_off1mm, waits=["offev"], tag="off1mm")
            tr.step("vector", lambda v: v.tensor_copy(off1[:, :], ps_b[0:64, 24:32]),
                    waits=["off1mm"], tag="off1ev")

            def em_pp(v):
                for e in range(8):
                    v.tensor_scalar(ind[:, :], ea[:, :], float(e), None, ALU.is_equal)
                    v.tensor_scalar(ppx[2][:, :], sca[:, e, :], offs[0:64, e:e + 1], None, ALU.add)
                    v.tensor_mul(ppx[3][:, :], ppx[2][:, :], ind[:, :])
                    if e == 0:
                        v.tensor_copy(ppx[0][:, :], ppx[3][:, :])
                    else:
                        v.tensor_add(ppx[0][:, :], ppx[0][:, :], ppx[3][:, :])
                for e in range(8):
                    v.tensor_scalar(ind[:, :], eb[:, :], float(e), None, ALU.is_equal)
                    v.tensor_scalar(ppx[2][:, :], scb[:, e, :], off1[:, e:e + 1], None, ALU.add)
                    v.tensor_mul(ppx[3][:, :], ppx[2][:, :], ind[:, :])
                    if e == 0:
                        v.tensor_copy(ppx[1][:, :], ppx[3][:, :])
                    else:
                        v.tensor_add(ppx[1][:, :], ppx[1][:, :], ppx[3][:, :])
                v.tensor_scalar(ppx[2][:, :], ppx[0][:, :], float(CAP), None, ALU.is_le)
                v.tensor_scalar(ppx[3][:, :], ppx[1][:, :], float(CAP), None, ALU.is_le)
                return None
            tr.step("vector", em_pp, waits=["off1ev"], tag="pp_done")

            def em_ownx(t):
                t.matmul(ps_a[0:8, 256:384], selc8[:, :], ppx[2][:, :], start=True, stop=True,
                         skip_group_check=True)
                t.matmul(ps_a[0:8, 640:768], selc8[:, :], ppx[3][:, :], start=True, stop=True,
                         skip_group_check=True)
                t.matmul(ps_c[0:8, 512:640], selc8[:, :], ea[:, :], start=True, stop=True,
                         skip_group_check=True)
                t.transpose(ps_b[0:8, 128:256], rstdtok[:, :], identt[:, :])
                return drain(t)
            tr.step("tensor", em_ownx, waits=["pp_done"], tag="ownx")

            def em_ownev(v):
                v.tensor_copy(own8[:, 0, :], ps_a[0:8, 256:384])
                v.tensor_copy(own8[:, 1, :], ps_a[0:8, 640:768])
                v.tensor_copy(own8[:, 2, :], ps_c[0:8, 512:640])
                return v.tensor_copy(own8[:, 3, :], ps_b[0:8, 128:256])
            tr.step("vector", em_ownev, waits=["ownx"], tag="ownev")
            def em_utr(t):
                for e in range(8):
                    t.transpose(ps_c[0:64, 128 * e:128 * e + 128], rw[:, :, e], identt[:, :])
                return drain(t)
            tr.step("tensor", em_utr, waits=["ownev"], tag="utr_all")
            tr.step("vector", lambda v: v.tensor_copy(
                A(utmp8).rearrange("p a c -> p (a c)"), ps_c[0:64, 0:1024]),
                waits=["utr_all"], tag="utev_all")

            def em_uxm(t):
                for e in range(8):
                    t.matmul(ps_a[0:8, 128 * e:128 * e + 128], selc8[:, :], utmp8[:, e, :],
                             start=(e % 4 == 0), stop=True, skip_group_check=True)
                return drain(t)
            tr.step("tensor", em_uxm, waits=["utev_all"], tag="uxm_all")
            tr.step("vector", lambda v: v.tensor_copy(
                A(uo).rearrange("p a c -> p (a c)"), ps_a[0:8, 0:1024]),
                waits=["uxm_all"], tag="uoev7")

            def em_pmath(v):
                for e in range(8):
                    v.tensor_mul(po[:, e, :], uo[:, e, :], own8[:, 3, :])
                return None
            tr.step("vector", em_pmath, waits=["uoev7", "ownev"], tag="l_done")
            tr.step("scalar", lambda sc: sc.activation(A(uo).rearrange("p a b -> p (a b)"),
                                                       A(po).rearrange("p a b -> p (a b)"),
                                                       ACT.Exp, bias=0.0, scale=1.0),
                    waits=["l_done"], tag="pexp")

            def em_pvals(v):
                v.tensor_add(o8[0][:, :], uo[:, 0, :], uo[:, 1, :])
                for e in range(2, 8):
                    v.tensor_add(o8[0][:, :], o8[0][:, :], uo[:, e, :])
                v.reciprocal(o8[1][:, :], o8[0][:, :])
                v.tensor_max(o8[2][:, :], uo[:, 0, :], uo[:, 1, :])
                for e in range(2, 8):
                    v.tensor_max(o8[2][:, :], o8[2][:, :], uo[:, e, :])
                v.tensor_mul(o8[2][:, :], o8[2][:, :], o8[1][:, :])
                for e in range(8):
                    v.tensor_scalar(o8[3][:, :], own8[:, 2, :], float(e), None, ALU.is_equal)
                    v.tensor_scalar(o8[3][:, :], o8[3][:, :], -1.0, 1.0, ALU.mult, ALU.add)
                    v.tensor_mul(po[:, e, :], uo[:, e, :], o8[3][:, :])
                v.tensor_max(o8[3][:, :], po[:, 0, :], po[:, 1, :])
                for e in range(2, 8):
                    v.tensor_max(o8[3][:, :], o8[3][:, :], po[:, e, :])
                v.tensor_mul(o8[3][:, :], o8[3][:, :], o8[1][:, :])
                v.tensor_mul(o8[4][:, :], own8[:, 0, :], o8[2][:, :])
                v.tensor_mul(o8[0][:, :], own8[:, 1, :], o8[3][:, :])
                return v.tensor_add(o8[4][:, :], o8[4][:, :], o8[0][:, :])
            tr.step("vector", em_pvals, waits=["pexp"], tag="pvals")
            def em_gtr(t):
                t.transpose(ps_c[0:128, 256:264], o8[4][:, :], identt[0:8, 0:8])
                return drain(t)
            tr.step("tensor", em_gtr, waits=["pvals"], tag="g_tr")
            tr.step("vector", lambda v: v.tensor_copy(gtok[:, :], ps_c[0:128, 256:264]),
                    waits=["g_tr"], tag="g_ev")
            with nc.Block("p5a") as bk:
                tr.emit(bk, sems)
            carry = {k: tr.events[k] for k in ["consts", "faccspill", "attdone", "wodone", "g_ev"]}
            cnts = dict(tr.counts)

        with ExitStack() as s6:
            acc = s6.enter_context(nc.sbuf_tensor("acc", [128, 8, H], F32))
            cmb = s6.enter_context(nc.sbuf_tensor("cmb", [128, 512], F32))
            xst = s6.enter_context(nc.sbuf_tensor("xst", [128, 1, H], F32))
            fst = s6.enter_context(nc.sbuf_tensor("fst", [128, 1, H], F32))
            pw = [s6.enter_context(nc.psum_tensor(f"pw{i}", [128, 1024], F32)) for i in range(4)]
            pwv = [pw[i // 2][:, 512 * (i % 2):512 * (i % 2) + 512] for i in range(4)]
            pwv2 = [pw[2 + i // 2][:, 512 * (i % 2):512 * (i % 2) + 512] for i in range(4)]
            tr = Tracer(cnts, carry)
            for tc in range(8):
                tr.step("sync", lambda sy, tc=tc: sy.dma_start(
                    out=xst[:, 0, :], in_=d_xacc[:, tc, :]),
                    waits=[f"cmb{tc-1}_3"] if tc >= 1 else ["attdone"],
                    tag=f"xld{tc}", sem="dXLD0")
                tr.step("gpsimd", lambda g, tc=tc: g.dma_start(
                    out=fst[:, 0, :], in_=d_facc[tc, :, :]),
                    waits=[f"cmb{tc-1}_3"] if tc >= 1 else ["faccspill", "attdone"],
                    tag=f"fld{tc}", sem="dFLD0")

                def em_wo(t, tc=tc):
                    ps = pwv if tc % 2 == 0 else pwv2
                    slw = A(attns)[:, 2 * tc:2 * tc + 2, :].rearrange(
                        "p h (s a) -> p h s a", a=16)
                    for ac in range(16):
                        lhs = slw[:, :, :, ac:ac + 1]
                        for hq in range(4):
                            last = t.matmul(ps[hq], lhs,
                                            wos[:, ac, 512 * hq:512 * hq + 512],
                                            start=(ac == 0), stop=(ac == 15),
                                            skip_group_check=True)
                    return last
                if tc == 0:
                    ww = ["attdone", "wodone", "g_ev"]
                elif tc == 1:
                    ww = ["g_ev"]
                else:
                    ww = [f"cmb{tc-2}_3"]
                tr.step("tensor", em_wo, waits=ww, tag=f"wo_mm{tc}")

                for hq in range(4):
                    def em_cmb(v, tc=tc, hq=hq):
                        ps = (pwv if tc % 2 == 0 else pwv2)[hq]
                        asl = acc[:, tc, 512 * hq:512 * hq + 512]
                        hsl = slice(512 * hq, 512 * hq + 512)
                        v.scalar_tensor_tensor(cmb[:, :], b2_bc[:, hsl],
                                               gtok[:, tc:tc + 1], ps, ALU.mult, ALU.add)
                        v.scalar_tensor_tensor(asl, fst[:, 0, hsl], gtok[:, tc:tc + 1],
                                               cmb[:, :], ALU.mult, ALU.add)
                        v.tensor_add(cmb[:, :], xst[:, 0, hsl], bo_bc[:, hsl])
                        return v.tensor_add(asl, asl, cmb[:, :])
                    tr.step("vector", em_cmb,
                            waits=[f"wo_mm{tc}", f"xld{tc}", f"fld{tc}", "g_ev"],
                            tag=f"cmb{tc}_{hq}")
            tr.step("sync", lambda sy: sy.dma_start(out=d_out[:, :, :], in_=A(acc)),
                    waits=["cmb7_3"], tag="stored", sem="dOUT")
            tr.step("gpsimd", lambda g: None, waits=["stored"], tag="final", sem="dOUT")
            with nc.Block("p5b") as bk:
                tr.emit(bk, sems)
        eAW.close()

    return nc


_PROG = None


def _get_prog():
    global _PROG
    if _PROG is None:
        _PROG = build_program()
    return _PROG


def kernel(L0_attn_X, L0_attn_Wq, L0_attn_bq, L0_attn_Wk, L0_attn_bk, L0_attn_Wv,
           L0_attn_bv, L0_attn_Wo, L0_attn_bo, L0_ffn_router_w, L0_ffn_w1, L0_ffn_b1,
           L0_ffn_w2, L0_ffn_b2):
    X = np.ascontiguousarray(np.asarray(L0_attn_X, dtype=np.float32)).reshape(NTOK, H)
    Wq = np.asarray(L0_attn_Wq, np.float32)
    Wk = np.asarray(L0_attn_Wk, np.float32)
    Wv = np.asarray(L0_attn_Wv, np.float32)
    Wo = np.asarray(L0_attn_Wo, np.float32)
    bq = np.asarray(L0_attn_bq, np.float32)
    bk = np.asarray(L0_attn_bk, np.float32)
    bv = np.asarray(L0_attn_bv, np.float32)
    bo = np.asarray(L0_attn_bo, np.float32)
    rt = np.asarray(L0_ffn_router_w, np.float32)
    w1 = np.asarray(L0_ffn_w1, np.float32)
    b1 = np.asarray(L0_ffn_b1, np.float32)
    w2 = np.asarray(L0_ffn_w2, np.float32)
    b2 = np.asarray(L0_ffn_b2, np.float32)

    XT = np.ascontiguousarray(X.T)
    xstats = np.ascontiguousarray(XT.reshape(DK, 128, NQ, QTR).transpose(2, 1, 0, 3))
    wo_h = np.ascontiguousarray(Wo.T.reshape(16, 128, H).transpose(1, 0, 2))
    w1_h = np.ascontiguousarray(
        w1.T.reshape(DK, 128, NFG, FG, 128).transpose(2, 1, 3, 0, 4))
    w2_h = np.ascontiguousarray(w2.T.reshape(NFG, FG, 128, H).transpose(0, 2, 1, 3))
    rt1 = np.zeros((H, 10), np.float32)
    rt1[:, 0:8] = rt.T
    rt1[:, 8] = 1.0
    rt1_h = np.ascontiguousarray(rt1.reshape(DK, 128, 10).transpose(1, 0, 2))
    b1_t = np.ascontiguousarray(b1.reshape(64, 128).T)
    ident = np.eye(128, dtype=np.float32)
    lstrict = np.ascontiguousarray(np.tril(np.ones((128, 128), np.float32), -1).T)
    idlow = np.zeros((64, 128), np.float32)
    idlow[np.arange(64), np.arange(64)] = 1.0
    shift64 = np.zeros((64, 128), np.float32)
    shift64[np.arange(64), np.arange(64) + 64] = 1.0
    shiftdn = np.zeros((128, 64), np.float32)
    shiftdn[np.arange(64) + 64, np.arange(64)] = 1.0
    onesd = np.zeros((2, 128), np.float32)
    onesd[1, :] = 1.0
    sel64 = np.zeros((65, 128), np.float32)
    sel64[64, :] = 1.0
    onesc = np.zeros((2, 128), np.float32)
    onesc[0, :] = 1.0
    onesf = np.ones((128, 1), np.float32)
    onesbf = np.ones((128, 1), BF)

    nc = _get_prog()
    in_maps = []
    for c in range(8):
        b = c // 2
        cp = c % 2
        seq = XT[:, S * b:S * (b + 1)]
        xown = np.ascontiguousarray(seq.reshape(DK, 128, 4, QTR).transpose(2, 1, 0, 3))
        ownT = seq[:, OWN * cp:OWN * (cp + 1)]
        xow = np.ascontiguousarray(ownT.reshape(DK, 128, 2, QTR).transpose(2, 1, 0, 3))
        xacc = np.ascontiguousarray(
            X[OWN * c:OWN * (c + 1)].reshape(8, 128, H).transpose(1, 0, 2))
        qcols = np.empty(1024, np.int64)
        i = 0
        for ch in range(8):
            g_, ckv = ch // 2, ch % 2
            for kvpar in range(2):
                kvg = 4 * cp + 2 * ckv + kvpar
                for d in range(64):
                    qcols[i] = kvg * 256 + g_ * 64 + d
                    i += 1
        WqTs = np.ascontiguousarray(Wq.T[:, qcols])
        wq_h = np.ascontiguousarray(
            WqTs.reshape(DK, 128, 8, 128).transpose(1, 2, 0, 3))
        bq_t = np.ascontiguousarray(bq[qcols].reshape(8, 128).T)
        kvrows = np.arange(4 * cp * 64, (4 * cp + 4) * 64)
        WkTs = np.ascontiguousarray(Wk[kvrows, :].T)
        wk_h = np.ascontiguousarray(WkTs.reshape(DK, 128, 256).transpose(1, 0, 2))
        WvTs = np.ascontiguousarray(Wv[kvrows, :].T)
        wv_h = np.ascontiguousarray(WvTs.reshape(DK, 128, 256).transpose(1, 0, 2))
        bk_t = np.ascontiguousarray(bk[kvrows].reshape(2, 128).T)
        brow = np.zeros((3, H), np.float32)
        brow[0] = bo
        brow[1] = b2
        brow[2, 0:256] = bv[kvrows]
        selc8 = np.zeros((64, 8), np.float32)
        for j in range(8):
            selc8[8 * c + j, j] = 1.0
        in_maps.append({
            "xownseq": xown, "xowhalf": xow, "xstats": xstats, "xacc": xacc,
            "wq": wq_h, "wk": wk_h, "wv": wv_h, "wo": wo_h,
            "w1": w1_h, "w2": w2_h, "rt1": rt1_h,
            "bqt": bq_t, "bkt": bk_t, "b1t": b1_t, "brow": brow,
            "identc": ident, "lstrictc": lstrict, "idlowc": idlow,
            "shift64c": shift64, "shiftdnc": shiftdn, "onesdc": onesd,
            "sel64c": sel64, "onescc": onesc, "onesfc": onesf,
            "onesbfc": onesbf, "selc8c": selc8,
        })
    res = run_bass_kernel_spmd(nc, in_maps, core_ids=list(range(8)))
    parts = []
    for c in range(8):
        o = res.results[c]["outp"]
        parts.append(np.ascontiguousarray(np.asarray(o).transpose(1, 0, 2)).reshape(OWN, H))
    return np.concatenate(parts, axis=0).reshape(Bb, S, H)


# version 14
# speedup vs baseline: 1.1393x; 1.1386x over previous
import numpy as np
import ml_dtypes
from contextlib import ExitStack
from functools import partial

import concourse.bass as bass
import concourse.mybir as mybir
from concourse.bass_utils import run_bass_kernel_spmd

F32 = mybir.dt.float32
BF16 = mybir.dt.bfloat16
ALU = mybir.AluOpType
ACT = mybir.ActivationFunctionType
AX = mybir.AxisListType
BF = ml_dtypes.bfloat16

H = 2048
S = 2048
Bb = 4
NTOK = Bb * S
OWN = 1024
CAP = 512
FF = 8192
DK = 16
QTR = 512
NQ = NTOK // QTR
FG = 8
NFG = FF // 128 // FG
NU = 2 * NFG


def A(th):
    return th[(slice(None),) * len(th.shape)]


class VPad:

    def __init__(self, v, n=6):
        self._v = v
        self._n = n

    def __getattr__(self, name):
        f = getattr(self._v, name)
        if name in ("nop", "wait_ge") or not callable(f):
            return f

        def wrapped(*a, **k):
            r = f(*a, **k)
            for _ in range(self._n):
                self._v.nop()
            return r
        return wrapped


class Tracer:
    ENGINES = ("sync", "gpsimd", "tensor", "scalar", "vector")

    def __init__(self, counts=None, events=None):
        self.steps = []
        self.counts = dict(counts) if counts else {}
        self.events = dict(events) if events else {}

    def step(self, engine, emit_fn, waits=(), tag=None, sem=None):
        if sem is None:
            assert engine not in ("sync", "gpsimd"), "DMA steps need an explicit stream sem"
            sem = engine
        inc = 16 if engine in ("sync", "gpsimd") else 1
        for w in waits:
            assert w in self.events, f"wait on unknown/future event {w!r}"
        self.counts[sem] = self.counts.get(sem, 0) + inc
        if tag is not None:
            assert tag not in self.events, f"dup tag {tag}"
            self.events[tag] = (sem, self.counts[sem])
        self.steps.append((engine, tuple(waits), emit_fn, inc, sem))

    def emit(self, bk, sems):
        per = {e: [] for e in self.ENGINES}
        for st in self.steps:
            per[st[0]].append(st)

        def mk(engine):
            def _(eng):
                if engine == "vector" and getattr(self, "pad_vector", False):
                    eng = VPad(eng)
                for _e, waits, emit_fn, inc, sem in per[engine]:
                    done = set()
                    for w in waits:
                        ws, wv = self.events[w]
                        if ws == engine or (ws, wv) in done:
                            continue
                        done.add((ws, wv))
                        eng.wait_ge(sems[ws], wv)
                    last = emit_fn(eng)
                    if last is None:
                        last = eng.nop()
                    last.then_inc(sems[sem], inc)
            return _

        for e in self.ENGINES:
            if per[e]:
                getattr(bk, e)(mk(e))


def build_program():
    nc = bass.Bass("TRN2", target_bir_lowering=False, detect_race_conditions=False)

    def din(name, shape, dt=F32):
        return nc.dram_tensor(name, shape, dt, kind="ExternalInput").ap()

    d_xown = din("xownseq", [4, 128, DK, QTR])
    d_xow = din("xowhalf", [2, 128, DK, QTR])
    d_xstats = din("xstats", [NQ, 128, DK, QTR])
    d_xacc = din("xacc", [128, 8, H])
    d_wq = din("wq", [128, 8, DK, 128])
    d_wk = din("wk", [128, DK, 256])
    d_wv = din("wv", [128, DK, 256])
    d_wo = din("wo", [128, 16, H])
    d_w1 = din("w1", [NFG, 128, FG, DK, 128])
    d_w2 = din("w2", [NFG, 128, FG, H])
    d_rt1 = din("rt1", [128, DK, 10])
    d_bq = din("bqt", [128, 8])
    d_bk = din("bkt", [128, 2])
    d_b1 = din("b1t", [128, 64])
    d_brow = din("brow", [3, H])
    d_ident = din("identc", [128, 128])
    d_lstrict = din("lstrictc", [128, 128])
    d_idlow = din("idlowc", [64, 128])
    d_sh64 = din("shift64c", [64, 128])
    d_shdn = din("shiftdnc", [128, 64])
    d_onesd = din("onesdc", [2, 128])
    d_sel64 = din("sel64c", [65, 128])
    d_onesc = din("onescc", [2, 128])
    d_onesf = din("onesfc", [128, 1])
    d_onesbf = din("onesbfc", [128, 1], BF16)
    d_selc8 = din("selc8c", [64, 8])
    d_out = nc.dram_tensor("outp", [128, 8, H], F32, kind="ExternalOutput").ap()
    d_xnsp = nc.dram_tensor("xnsp", [DK, 128, S], BF16, kind="Internal", addr_space="Local").ap()
    d_xnow = nc.dram_tensor("xnow", [DK, 128, OWN], BF16, kind="Internal", addr_space="Local").ap()
    d_facc = nc.dram_tensor("faccsp", [8, 128, H], F32, kind="Internal", addr_space="Local").ap()
    d_rrow = nc.dram_tensor("rrowsp", [1, 512], F32, kind="Internal", addr_space="Local").ap()

    es = ExitStack()
    eAW = ExitStack()
    with es:
        sb = lambda name, shape, dt: es.enter_context(nc.sbuf_tensor(name, shape, dt))
        identt = sb("identt", [128, 128], F32)
        lstr = sb("lstrt", [128, 128], F32)
        idlow = sb("idlowt", [64, 128], F32)
        sh64 = sb("sh64t", [64, 128], F32)
        shdn = sb("shdnt", [128, 64], F32)
        onesd = sb("onesdt", [2, 128], F32)
        sel64 = sb("sel64t", [65, 128], F32)
        onesc = sb("onesct", [2, 128], F32)
        onesf = sb("onesft", [128, 1], F32)
        onesbf = sb("onesbft", [128, 1], BF16)
        selc8 = sb("selc8t", [64, 8], F32)
        rt1t = sb("rt1t", [128, DK, 10], F32)
        bqt = sb("bqtt", [128, 8], F32)
        bkt = sb("bktt", [128, 2], F32)
        b1t = sb("b1tt", [128, 64], F32)
        bo_bc = sb("bo_bc", [128, H], F32)
        b2_bc = sb("b2_bc", [128, H], F32)
        bv_bc = sb("bv_bc", [128, 256], F32)
        rtraw = sb("rtraw", [128, 64, 10], F32)
        rstdtok = sb("rstdtok", [128, 8], F32)
        gtok = sb("gtok", [128, 8], F32)

        SEMNAMES = ["tensor", "scalar", "vector", "dCON", "dXQ0", "dXQ1", "dBGX", "dXLD0",
                    "dXLD1", "dOUT", "dWA", "dW1", "dW2", "dWOS",
                    "dXNF0", "dXNF1", "dXNP0", "dXNP1", "dSPILL", "dSPILL2", "dSPILLS",
                    "dFLD0", "dFLD1", "dEP0", "dEP1"]
        sems = {e: es.enter_context(nc.semaphore("sem_" + e)) for e in SEMNAMES}

        tr = Tracer()
        consts = [
            (identt, d_ident), (lstr, d_lstrict), (idlow, d_idlow),
            (sh64, d_sh64), (shdn, d_shdn), (onesd, d_onesd), (sel64, d_sel64),
            (onesc, d_onesc), (onesf, d_onesf), (onesbf, d_onesbf),
            (selc8, d_selc8), (rt1t, d_rt1), (bqt, d_bq), (bkt, d_bk), (b1t, d_b1),
        ]
        for dst, src in consts:
            tr.step("sync", lambda sy, dst=dst, src=src: sy.dma_start(out=A(dst), in_=src),
                    sem="dCON")
        tr.step("sync", lambda sy: sy.dma_start(out=bo_bc[:, :], in_=d_brow[0:1, :].broadcast_to([128, H])), sem="dCON")
        tr.step("sync", lambda sy: sy.dma_start(out=b2_bc[:, :], in_=d_brow[1:2, :].broadcast_to([128, H])), sem="dCON")
        tr.step("sync", lambda sy: sy.dma_start(out=bv_bc[:, :], in_=d_brow[2:3, 0:256].broadcast_to([128, 256])),
                sem="dCON", tag="consts")
        with nc.Block("p0") as bk:
            tr.emit(bk, sems)
        carry = {"consts": tr.events["consts"]}
        cnts = dict(tr.counts)

        with ExitStack() as s1:
            xq = s1.enter_context(nc.sbuf_tensor("xq", [128, 2, DK, QTR], F32))
            xn = s1.enter_context(nc.sbuf_tensor("xn", [128, DK, S], BF16))
            xnw = s1.enter_context(nc.sbuf_tensor("xnw", [128, DK, OWN], BF16))
            xsqb = s1.enter_context(nc.sbuf_tensor("xsqb", [128, 2, QTR], BF16))
            xqb = s1.enter_context(nc.sbuf_tensor("xqb", [128, 2, QTR], BF16))
            tmpn = s1.enter_context(nc.sbuf_tensor("tmpn", [128, QTR], F32))
            row2 = s1.enter_context(nc.sbuf_tensor("row2", [2, 2, QTR], F32))
            mrow2 = s1.enter_context(nc.sbuf_tensor("mrow2", [2, 2, QTR], F32))
            rva = s1.enter_context(nc.sbuf_tensor("rva", [1, 2, QTR], F32))
            rvb = s1.enter_context(nc.sbuf_tensor("rvb", [1, 2, QTR], F32))
            ps0 = s1.enter_context(nc.psum_tensor("ps0", [128, 2048], F32))
            ps1 = s1.enter_context(nc.psum_tensor("ps1", [128, 512], F32))
            ps2 = s1.enter_context(nc.psum_tensor("ps2", [128, 1024], F32))
            ps3 = s1.enter_context(nc.psum_tensor("ps3", [128, 512], F32))
            R_MSp = [ps0[0:1, 0:512], ps2[0:1, 0:512]]
            R_S2p = [ps0[0:1, 512:1024], ps2[0:1, 512:1024]]
            R_MBCp = [ps0[:, 1024:1536], ps3[:, 0:512]]
            R_RBC = ps0[:, 1536:2048]

            tr = Tracer(cnts, carry)
            def em_r2z(v):
                v.memset(A(row2), 0.0)
                return v.memset(A(mrow2), 0.0)
            tr.step("vector", em_r2z, tag="r2z")
            for q in range(6):
                w = [f"norm_done{q-2}", f"s2mm{q-2}_{DK-1}"] if q >= 2 else []
                src_ap = d_xown[q, :, :, :] if q < 4 else d_xow[q - 4, :, :, :]
                tr.step("sync", lambda sy, src_ap=src_ap, q=q: sy.dma_start(
                    out=xq[:, q % 2, :, :], in_=src_ap),
                    waits=w, tag=f"xq{q}", sem=f"dXQ{q % 2}")

                for dk in range(DK):
                    w = [f"xq{q}"]
                    if dk >= 2:
                        w.append(f"s2mm{q}_{dk-2}")
                    elif q >= 1:
                        w.append(f"s2mm{q-1}_{DK-1}")

                    def em_xsq(sc, q=q, dk=dk):
                        sc.activation(xsqb[:, dk % 2, :], xq[:, q % 2, dk, :], ACT.Square,
                                      bias=0.0, scale=1.0)
                        return sc.activation(xqb[:, dk % 2, :], xq[:, q % 2, dk, :], ACT.Identity,
                                             bias=0.0, scale=1.0)
                    tr.step("scalar", em_xsq, waits=w, tag=f"xsq{q}_{dk}")
                    w2_ = [f"xsq{q}_{dk}"]
                    if dk == 0 and q >= 2:
                        w2_ += [f"ve{q-2}", f"evst{q-2}"]

                    def em_s2(t, q=q, dk=dk):
                        t.matmul(R_S2p[q % 2], onesbf[:, :], xsqb[:, dk % 2, :],
                                 start=(dk == 0), stop=(dk == DK - 1),
                                 skip_group_check=True)
                        return t.matmul(R_MSp[q % 2], onesbf[:, :], xqb[:, dk % 2, :],
                                        start=(dk == 0), stop=(dk == DK - 1),
                                        skip_group_check=True)
                    tr.step("tensor", em_s2, waits=w2_, tag=f"s2mm{q}_{dk}")
                tr.step("vector", lambda v, q=q: v.tensor_scalar_mul(
                    mrow2[0:1, q % 2, :], R_MSp[q % 2], 1.0 / H),
                        waits=[f"s2mm{q}_{DK-1}", "consts", "r2z"]
                        + ([f"mbc{q-2}"] if q >= 2 else []),
                        tag=f"evst{q}")
                tr.step("tensor", lambda t, q=q: t.matmul(
                    R_MBCp[q % 2], onesc[:, :], mrow2[:, q % 2, :],
                    start=True, stop=True, skip_group_check=True),
                        waits=[f"evst{q}"] + ([f"norm_done{q-2}"] if q >= 2 else []),
                        tag=f"mbc{q}")

                def em_ve(v, q=q):
                    p = q % 2
                    v.tensor_copy(rvb[0:1, p, :], R_MBCp[p][0:1, :])
                    v.tensor_mul(rva[0:1, p, :], rvb[0:1, p, :], rvb[0:1, p, :])
                    v.scalar_tensor_tensor(rvb[0:1, p, :], R_S2p[p], 1.0 / H,
                                           rva[0:1, p, :], ALU.mult, ALU.subtract)
                    return v.tensor_scalar_add(rva[0:1, p, :], rvb[0:1, p, :], 1e-5)
                tr.step("vector", em_ve, waits=[f"mbc{q}", f"s2mm{q}_{DK-1}"], tag=f"ve{q}")
                tr.step("scalar", lambda sc, q=q: sc.activation(
                    rvb[0:1, q % 2, :], rva[0:1, q % 2, :], ACT.Sqrt, bias=0.0, scale=1.0),
                        waits=[f"ve{q}"], tag=f"sq{q}")

                def em_rrow(v, q=q):
                    p = q % 2
                    v.reciprocal(row2[0:1, p, :], rvb[0:1, p, :])
                    v.tensor_mul(rvb[0:1, p, :], row2[0:1, p, :], row2[0:1, p, :])
                    v.tensor_mul(rvb[0:1, p, :], rvb[0:1, p, :], rva[0:1, p, :])
                    v.tensor_scalar(rva[0:1, p, :], rvb[0:1, p, :], -0.5, 1.5,
                                    ALU.mult, ALU.add)
                    return v.tensor_mul(row2[0:1, p, :], row2[0:1, p, :], rva[0:1, p, :])
                tr.step("vector", em_rrow, waits=[f"sq{q}", "r2z"], tag=f"rrow{q}")
                tr.step("tensor", lambda t, q=q: t.matmul(
                    R_RBC, onesc[:, :], row2[:, q % 2, :],
                    start=True, stop=True, skip_group_check=True),
                        waits=[f"rrow{q}"] + ([f"norm_done{q-1}"] if q else []),
                        tag=f"rbc{q}")
                if q >= 4:
                    j = q - 4

                    def em_rtr(t, q=q):
                        for i in range(4):
                            t.transpose(ps1[:, i:i + 1],
                                        row2[0:1, q % 2, 128 * i:128 * i + 128],
                                        identt[0:1, 0:1])
                        return t.matmul(ps0[0:1, 0:512], onesbf[:, :], xsqb[:, 0, :],
                                        start=True, stop=True, skip_group_check=True)
                    tr.step("tensor", em_rtr,
                            waits=[f"rrow{q}"] + ([f"rtrev{j-1}"] if j else []), tag=f"rtr{j}")
                    tr.step("vector", lambda v, j=j: v.tensor_copy(rstdtok[:, 4 * j:4 * j + 4],
                                                                   ps1[:, 0:4]),
                            waits=[f"rtr{j}"], tag=f"rtrev{j}")
                for dk in range(DK):
                    def em_norm(v, q=q, dk=dk):
                        v.tensor_sub(tmpn[:, :], xq[:, q % 2, dk, :], R_MBCp[q % 2])
                        if q < 4:
                            return v.tensor_mul(xn[:, dk, QTR * q:QTR * q + QTR],
                                                tmpn[:, :], R_RBC)
                        return v.tensor_mul(xnw[:, dk, QTR * (q - 4):QTR * (q - 4) + QTR],
                                            tmpn[:, :], R_RBC)
                    tr.step("vector", em_norm, waits=[f"rbc{q}"], tag=f"norm{q}_{dk}")
                tr.step("vector", lambda v: None, tag=f"norm_done{q}")
            tr.step("sync", lambda sy: sy.dma_start(out=d_xnsp.rearrange("k p n -> p k n"),
                                                    in_=A(xn)),
                    waits=["norm_done3"], tag="xnspill", sem="dSPILLS")
            tr.step("gpsimd", lambda g: g.dma_start(
                out=d_xnow[:, :, 0:QTR].rearrange("k p n -> p k n"),
                in_=A(xnw)[:, :, 0:QTR]),
                    waits=["norm_done4"], tag="xnowspill0", sem="dSPILL2")
            tr.step("gpsimd", lambda g: g.dma_start(
                out=d_xnow[:, :, QTR:OWN].rearrange("k p n -> p k n"),
                in_=A(xnw)[:, :, QTR:OWN]),
                    waits=["norm_done5"], tag="xnowspill", sem="dSPILL2")
            with nc.Block("p1") as bk:
                tr.emit(bk, sems)
            carry = {"consts": tr.events["consts"], "xnspill": tr.events["xnspill"],
                     "xnowspill0": tr.events["xnowspill0"],
                     "xnowspill": tr.events["xnowspill"]}
            cnts = dict(tr.counts)

        with ExitStack() as s2:
            facc = s2.enter_context(nc.sbuf_tensor("facc", [128, 8, H], F32))
            w1b = s2.enter_context(nc.sbuf_tensor("w1b", [128, FG, DK, 128], BF16))
            w2b = s2.enter_context(nc.sbuf_tensor("w2b", [128, FG, H], BF16))
            mid = s2.enter_context(nc.sbuf_tensor("mid", [128, FG, 512], BF16))
            xnf = s2.enter_context(nc.sbuf_tensor("xnf", [128, 8, 512], BF16))
            xgs = s2.enter_context(nc.sbuf_tensor("xgs", [128, DK, QTR], F32))
            pp = [s2.enter_context(nc.psum_tensor(f"pp{i}", [128, 1024], F32)) for i in range(4)]
            ph1 = [pp[i // 2][:, 512 * (i % 2):512 * (i % 2) + 512] for i in range(8)]

            tr = Tracer(cnts, carry)
            tr.step("gpsimd", lambda g: g.dma_start(out=A(w1b), in_=d_w1[0]), tag="w1_0", sem="dW1")
            tr.step("gpsimd", lambda g: g.dma_start(out=A(w2b), in_=d_w2[0]), tag="w2_0", sem="dW2")

            def bg_quarter(q, cover):
                def em_bgst(t, q=q):
                    for dk in range(DK):
                        for tc4 in range(4):
                            last = t.matmul(
                                pp[0][0:128, 10 * tc4:10 * tc4 + 10],
                                xgs[:, dk, 128 * tc4:128 * tc4 + 128], rt1t[:, dk, :],
                                start=(dk == 0 and tc4 == 0), stop=(dk == DK - 1),
                                skip_group_check=True)
                    return last
                tr.step("tensor", em_bgst, waits=[f"bgx{q}", "consts"] + cover, tag=f"bgst{q}")

                def em_bgrs(v, q=q):
                    return v.tensor_copy(
                        A(rtraw)[:, 4 * q:4 * q + 4, :],
                        pp[0][0:128, 0:40].rearrange("p (a c) -> p a c", c=10))
                tr.step("vector", em_bgrs, waits=[f"bgst{q}"], tag=f"bgrs{q}")

            for u in range(NU):
                fg, hf = u // 2, u % 2
                if u >= 1 and (u - 1) < NQ:
                    bg_quarter(u - 1, [f"gel{u-1}_{FG-1}"])
                if u < NQ:
                    tr.step("sync", lambda sy, u=u: sy.dma_start(out=A(xgs), in_=d_xstats[u]),
                            waits=[f"bgst{u-1}"] if u >= 1 else [], tag=f"bgx{u}", sem="dBGX")

                def xnf_step(gdk):
                    w = ["xnowspill" if hf else "xnowspill0"]
                    if gdk >= 2:
                        w.append(f"h1g{u}_{4*(gdk-2)+3}")
                    elif u >= 1:
                        w.append(f"h1g{u-1}_{4*(gdk+2)+3}")
                    tr.step("gpsimd", lambda g, gdk=gdk, hf=hf: g.dma_start(
                        out=xnf[:, 4 * (gdk % 2):4 * (gdk % 2) + 4, :],
                        in_=d_xnow[4 * gdk:4 * gdk + 4, :, 512 * hf:512 * hf + 512]
                        .rearrange("k p n -> p k n")),
                        waits=w, tag=f"xnf{u}_{gdk}", sem=f"dXNF{gdk % 2}")
                xnf_step(0)
                xnf_step(1)
                for dk in range(DK):
                    gdk = dk // 4

                    def em_h1(t, dk=dk, gdk=gdk):
                        sl = xnf[:, 4 * (gdk % 2) + dk % 4, :]
                        for f in range(FG):
                            last = t.matmul(ph1[f], w1b[:, f, dk, :], sl,
                                            start=(dk == 0), stop=(dk == DK - 1),
                                            skip_group_check=True)
                        return last
                    w = [f"xnf{u}_{gdk}", f"w1_{fg}"]
                    if dk == 0:
                        if u >= 1:
                            w += [f"gel{u-1}_{FG-1}", f"h2e{u-1}_6", f"h2e{u-1}_7"]
                        q = u - 1
                        if 0 <= q < NQ:
                            w.append(f"bgrs{q}")
                    tr.step("tensor", em_h1, waits=w, tag=f"h1g{u}_{dk}")
                    if dk % 4 == 3 and gdk + 2 < 4:
                        xnf_step(gdk + 2)
                for f in range(FG):
                    w = [f"h1g{u}_{DK-1}"]
                    if u >= 1 and f == 0:
                        w.append(f"h2m{u-1}_7")
                    tr.step("scalar", lambda sc, fg=fg, f=f: sc.activation(
                        mid[:, f, :], ph1[f], ACT.Gelu,
                        bias=b1t[:, fg * FG + f:fg * FG + f + 1], scale=1.0),
                        waits=w, tag=f"gel{u}_{f}")
                if hf == 1 and fg + 1 < NFG:
                    tr.step("gpsimd", lambda g, fg=fg: g.dma_start(out=A(w1b), in_=d_w1[fg + 1]),
                            waits=[f"h1g{u}_{DK-1}"], tag=f"w1_{fg+1}", sem="dW1")
                for i in range(8):
                    tc, hh = i // 2, i % 2
                    gtc = 4 * hf + tc

                    def em_h2(t, i=i, tc=tc, hh=hh):
                        for f in range(FG):
                            for c2 in range(2):
                                last = t.matmul(
                                    pp[2 + i % 2][:, 512 * c2:512 * c2 + 512],
                                    mid[:, f, 128 * tc:128 * tc + 128],
                                    w2b[:, f, 1024 * hh + 512 * c2:1024 * hh + 512 * c2 + 512],
                                    start=(f == 0), stop=(f == FG - 1), skip_group_check=True)
                        return last
                    w = [f"gel{u}_{FG-1}", "w2_0" if fg == 0 else f"w2_{fg}"]
                    if i >= 2:
                        w.append(f"h2e{u}_{i-2}")
                    elif u >= 1:
                        w.append(f"h2e{u-1}_{6+i}")
                    tr.step("tensor", em_h2, waits=w, tag=f"h2m{u}_{i}")
                    if u <= 1:
                        def em_h2e(v, gtc=gtc, hh=hh, i=i):
                            return v.tensor_copy(facc[:, gtc, 1024 * hh:1024 * hh + 1024],
                                                 pp[2 + i % 2][:, :])
                    else:
                        def em_h2e(v, gtc=gtc, hh=hh, i=i):
                            sl = facc[:, gtc, 1024 * hh:1024 * hh + 1024]
                            return v.tensor_add(sl, sl, pp[2 + i % 2][:, :])
                    tr.step("vector", em_h2e, waits=[f"h2m{u}_{i}"], tag=f"h2e{u}_{i}")
                if hf == 1 and fg + 1 < NFG:
                    tr.step("gpsimd", lambda g, fg=fg: g.dma_start(out=A(w2b), in_=d_w2[fg + 1]),
                            waits=[f"h2m{u}_7"], tag=f"w2_{fg+1}", sem="dW2")
            bg_quarter(NQ - 1, [f"gel{NU-1}_{FG-1}"])
            tr.step("gpsimd", lambda g: g.dma_start(out=d_facc.rearrange("t p h -> p t h"),
                                                    in_=A(facc)),
                    waits=[f"h2e{NU-1}_7"], tag="faccspill", sem="dSPILL")
            with nc.Block("p2") as bk:
                tr.emit(bk, sems)
            carry = {k: tr.events[k] for k in ["consts", "xnspill", "faccspill"]}
            carry["bglast"] = tr.events[f"bgrs{NQ-1}"]
            cnts = dict(tr.counts)

        with ExitStack() as eqkv:
            qslab = eqkv.enter_context(nc.sbuf_tensor("qslab", [128, 8, S], BF16))
            kslab = eqkv.enter_context(nc.sbuf_tensor("kslab", [128, 2, S], BF16))
            vslab = eqkv.enter_context(nc.sbuf_tensor("vslab", [128, 16, 260], BF16))

            with ExitStack() as s3:
                wks = s3.enter_context(nc.sbuf_tensor("wks", [128, DK, 256], BF16))
                wvs = s3.enter_context(nc.sbuf_tensor("wvs", [128, DK, 256], BF16))
                wqb = s3.enter_context(nc.sbuf_tensor("wqb", [128, 8, DK, 128], BF16))
                xnp = s3.enter_context(nc.sbuf_tensor("xnp", [128, 8, 1024], BF16))
                pp = [s3.enter_context(nc.psum_tensor(f"pq{i}", [128, 1024], F32)) for i in range(4)]
                pv8 = [pp[2 + tc // 4][:, 256 * (tc % 4):256 * (tc % 4) + 256] for tc in range(8)]
                pq8 = [pp[i // 2][:, 512 * (i % 2):512 * (i % 2) + 512] for i in range(8)]

                tr = Tracer(cnts, carry)
                tr.step("gpsimd", lambda g: g.dma_start(out=A(wks), in_=d_wk), sem="dWA")
                tr.step("gpsimd", lambda g: g.dma_start(out=A(wvs), in_=d_wv), sem="dWA")
                tr.step("gpsimd", lambda g: g.dma_start(out=A(wqb), in_=d_wq), sem="dWA",
                        tag="wABurst")
                tr.step("vector", lambda v: v.memset(
                    A(vslab).rearrange("p k (g c) -> p k g c", c=65)[:, :, :, 64:65], 1.0),
                    tag="vones")

                for hf in range(2):
                    def kx_step(gdk, hf=hf):
                        w = ["xnspill"]
                        if gdk >= 2:
                            w.append(f"kv{hf}_{4*(gdk-2)+3}")
                        elif hf == 1:
                            w.append(f"kv0_{4*(gdk+2)+3}")
                        tr.step("gpsimd", lambda g, gdk=gdk, hf=hf: g.dma_start(
                            out=xnp[:, 4 * (gdk % 2):4 * (gdk % 2) + 4, :],
                            in_=d_xnsp[4 * gdk:4 * gdk + 4, :, 1024 * hf:1024 * hf + 1024]
                            .rearrange("k p n -> p k n")),
                            waits=w, tag=f"kx{hf}_{gdk}", sem=f"dXNP{gdk % 2}")
                    kx_step(0)
                    kx_step(1)
                    for dk in range(DK):
                        gdk = dk // 4

                        def em_kv(t, hf=hf, dk=dk, gdk=gdk):
                            sl = xnp[:, 4 * (gdk % 2) + dk % 4, :]
                            for oc in range(2):
                                for c2 in range(2):
                                    t.matmul(pp[oc][:, 512 * c2:512 * c2 + 512],
                                             wks[:, dk, 128 * oc:128 * oc + 128],
                                             sl[:, 512 * c2:512 * c2 + 512],
                                             start=(dk == 0), stop=(dk == DK - 1),
                                             skip_group_check=True)
                            for tc in range(8):
                                last = t.matmul(pv8[tc], sl[:, 128 * tc:128 * tc + 128],
                                                wvs[:, dk, :],
                                                start=(dk == 0 and tc % 2 == 0),
                                                stop=(dk == DK - 1),
                                                skip_group_check=True)
                            return last
                        kw = [f"kx{hf}_{gdk}", "wABurst"]
                        if dk == 0:
                            if hf == 0:
                                kw.append("bglast")
                            else:
                                kw += [f"kev0_{oc}" for oc in range(2)]
                                kw += [f"vev0_{tc}" for tc in range(8)]
                        tr.step("tensor", em_kv, waits=kw, tag=f"kv{hf}_{dk}")
                        if dk % 4 == 3 and gdk + 2 < 4:
                            kx_step(gdk + 2)
                    for oc in range(2):
                        tr.step("vector", lambda v, hf=hf, oc=oc: v.tensor_scalar(
                            kslab[:, oc, OWN * hf:OWN * hf + OWN], pp[oc][:, :],
                            bkt[:, oc:oc + 1], None, ALU.add),
                            waits=[f"kv{hf}_{DK-1}"], tag=f"kev{hf}_{oc}")
                    for tc in range(8):
                        def em_vev(v, hf=hf, tc=tc):
                            kt = 8 * hf + tc
                            return v.tensor_tensor(
                                A(vslab)[:, kt:kt + 1, :].rearrange(
                                    "p o (g c) -> p (o g) c", c=65)[:, :, 0:64],
                                pv8[tc].rearrange("p (g c) -> p g c", c=64),
                                bv_bc[:, :].rearrange("p (g c) -> p g c", c=64),
                                ALU.add)
                        tr.step("vector", em_vev, waits=[f"kv{hf}_{DK-1}", "vones"],
                                tag=f"vev{hf}_{tc}")

                for tq in range(4):
                    def qx_step(gdk, tq=tq):
                        w = []
                        if gdk >= 2:
                            w.append(f"qprj{tq}_{4*(gdk-2)+3}")
                        elif tq == 0:
                            w.append(f"kv1_{4*(gdk+2)+3}")
                        else:
                            w.append(f"qprj{tq-1}_{4*(gdk+2)+3}")
                        tr.step("gpsimd", lambda g, gdk=gdk, tq=tq: g.dma_start(
                            out=xnp[:, 4 * (gdk % 2):4 * (gdk % 2) + 4, 0:512],
                            in_=d_xnsp[4 * gdk:4 * gdk + 4, :, 512 * tq:512 * tq + 512]
                            .rearrange("k p n -> p k n")),
                            waits=w, tag=f"qx{tq}_{gdk}", sem=f"dXNP{gdk % 2}")
                    qx_step(0)
                    qx_step(1)
                    for dk in range(DK):
                        gdk = dk // 4

                        def em_q(t, dk=dk, gdk=gdk):
                            sl = xnp[:, 4 * (gdk % 2) + dk % 4, 0:512]
                            for j in range(8):
                                last = t.matmul(pq8[j], wqb[:, j, dk, :], sl,
                                                start=(dk == 0), stop=(dk == DK - 1),
                                                skip_group_check=True)
                            return last
                        qw = [f"qx{tq}_{gdk}", "wABurst"]
                        if dk == 0:
                            if tq == 0:
                                qw += [f"kev1_{oc}" for oc in range(2)]
                                qw += [f"vev1_{tc}" for tc in range(8)]
                            else:
                                qw += [f"qe{tq-1}_{j}" for j in range(8)]
                        tr.step("tensor", em_q, waits=qw, tag=f"qprj{tq}_{dk}")
                        if dk % 4 == 3 and gdk + 2 < 4:
                            qx_step(gdk + 2)
                    for j in range(8):
                        def em_qe(v, j=j, tq=tq):
                            v.tensor_scalar(
                                qslab[:, j, 256 * tq:256 * tq + 256],
                                pq8[j].rearrange("p (a two) -> p a two", two=2)[:, :, 0],
                                bqt[:, j:j + 1], None, ALU.add)
                            return v.tensor_scalar(
                                qslab[:, j, 1024 + 256 * tq:1024 + 256 * tq + 256],
                                pq8[j].rearrange("p (a two) -> p a two", two=2)[:, :, 1],
                                bqt[:, j:j + 1], None, ALU.add)
                        tr.step("vector", em_qe, waits=[f"qprj{tq}_{DK-1}"], tag=f"qe{tq}_{j}")
                with nc.Block("p3") as bk:
                    tr.emit(bk, sems)
                carry = {k: tr.events[k] for k in ["consts", "faccspill", "bglast"]}
                carry["qlast"] = tr.events["qe3_7"]
                carry["qhalf"] = tr.events["qe1_7"]
                carry["kvlast"] = tr.events["vev1_7"]
                cnts = dict(tr.counts)

            attns = eAW.enter_context(nc.sbuf_tensor("attns", [128, 16, OWN], BF16, side="right"))
            wos = eAW.enter_context(nc.sbuf_tensor("wos", [128, 16, H], BF16, side="right"))
            with ExitStack() as s4:
                probs = s4.enter_context(nc.sbuf_tensor("probs", [128, 2, 1024], BF16))
                avsb = s4.enter_context(nc.sbuf_tensor("avsb", [65, 2, 1024], F32))
                avtmp = s4.enter_context(nc.sbuf_tensor("avtmp", [64, 512], F32))
                rr = s4.enter_context(nc.sbuf_tensor("rr", [65, 512], F32))
                rbsb = s4.enter_context(nc.sbuf_tensor("rbsb", [64, 512], F32))
                ps_s = [s4.enter_context(nc.psum_tensor(f"ps_s{i}", [128, 1024], F32))
                        for i in range(3)]
                ps_av = s4.enter_context(nc.psum_tensor("ps_av", [128, 1024], F32))

                tr = Tracer(cnts, carry)
                for i in range(4):
                    tr.step("gpsimd", lambda g, i=i: g.dma_start(
                        out=wos[:, 4 * i:4 * i + 4, :], in_=d_wo[:, 4 * i:4 * i + 4, :]),
                        tag=f"wo{i}", sem="dWOS")
                tr.step("vector", lambda v: v.memset(rr[0:64, :], 0.0), tag="rrz")
                vv = A(vslab).rearrange("p k (g c) -> p k g c", c=65)
                for pr in range(8):
                    g_, ckv = pr // 2, pr % 2
                    hlA = 8 * ckv + g_
                    hlB = 8 * ckv + 4 + g_
                    for qb in range(4):
                        par = qb // 2
                        mb = 512 * (qb % 2)
                        un = pr * 4 + qb

                        def em_sc(t, ckv=ckv, pr=pr, qb=qb, kt=0):
                            psc = ps_s[kt % 3]
                            t.matmul(psc[:, 0:512],
                                     kslab[0:64, ckv, 128 * kt:128 * kt + 128],
                                     qslab[0:64, pr, 512 * qb:512 * qb + 512],
                                     start=True, stop=True,
                                     tile_position=(0, 0), skip_group_check=True)
                            return t.matmul(psc[:, 512:1024],
                                            kslab[64:128, ckv, 128 * kt:128 * kt + 128],
                                            qslab[64:128, pr, 512 * qb:512 * qb + 512],
                                            start=True, stop=True,
                                            tile_position=(64, 0),
                                            skip_group_check=True)

                        def em_exp(sc, kt=0):
                            return sc.activation(probs[:, kt % 2, :], ps_s[kt % 3][:, :],
                                                 ACT.Exp, bias=0.0, scale=0.125)

                        def em_av(t, ckv=ckv, kt=0):
                            t.matmul(ps_av[0:65, 0:512], vv[:, kt, 2 * ckv, :],
                                     probs[:, kt % 2, 0:512],
                                     start=(kt == 0), stop=(kt == 15), skip_group_check=True)
                            return t.matmul(ps_av[0:65, 512:1024], vv[:, kt, 2 * ckv + 1, :],
                                            probs[:, kt % 2, 512:1024],
                                            start=(kt == 0), stop=(kt == 15),
                                            skip_group_check=True)

                        def emit_av(kt):
                            aw = [f"exp{un}_{kt}"]
                            if kt == 0 and un >= 1:
                                aw.append(f"evac{un-1}")
                            tr.step("tensor", partial(em_av, kt=kt), waits=aw,
                                    tag=f"av{un}_{kt}")

                        for kt in range(16):
                            w = []
                            if un == 0 and kt == 0:
                                w += ["qhalf", "kvlast"]
                            if un == 1 and kt == 0:
                                w.append("qlast")
                            if kt >= 3:
                                w.append(f"exp{un}_{kt-3}")
                            elif un >= 1:
                                w.append(f"exp{un-1}_{13 + kt}")
                            tr.step("tensor", partial(em_sc, kt=kt), waits=w,
                                    tag=f"sc{un}_{kt}")
                            if kt >= 2:
                                emit_av(kt - 2)
                            ew = [f"sc{un}_{kt}"]
                            if kt >= 2:
                                ew.append(f"av{un}_{kt-2}")
                            elif un >= 1:
                                ew.append(f"av{un-1}_{14 + kt}")
                            tr.step("scalar", partial(em_exp, kt=kt), waits=ew,
                                    tag=f"exp{un}_{kt}")
                        emit_av(14)
                        emit_av(15)
                        tr.step("vector", lambda v, un=un: v.tensor_copy(
                            avsb[:, un % 2, :], ps_av[0:65, 0:1024]),
                            waits=[f"av{un}_15"], tag=f"evac{un}")
                        for hx, hl in ((0, hlA), (1, hlB)):
                            cA = 512 * hx
                            if hx == 0:
                                rw_ = [f"evac{un}", "rrz"]
                                if un >= 1:
                                    rw_.append(f"rba{un-1}_1")
                            else:
                                rw_ = [f"rba{un}_0"]
                            tr.step("vector", lambda v, cA=cA, un=un: v.reciprocal(
                                rr[64:65, :], avsb[64:65, un % 2, cA:cA + 512]),
                                waits=rw_, tag=f"rec{un}_{hx}")
                            tr.step("gpsimd", lambda g: g.dma_start(
                                out=d_rrow, in_=rr[64:65, :]),
                                waits=[f"rec{un}_{hx}"]
                                + ([f"rbb{un}_0"] if hx else
                                   ([f"rbb{un-1}_1"] if un else [])),
                                tag=f"rba{un}_{hx}", sem="dEP0")
                            bw = [f"rba{un}_{hx}"]
                            if hx == 1:
                                bw.append(f"ep{un}_0" if par == 0 else f"avt{un}_0")
                            elif un >= 1:
                                bw.append(f"ep{un-1}_1" if par == 0 or (un - 1) % 4 < 2
                                          else f"avt{un-1}_1")
                            tr.step("gpsimd", lambda g: g.dma_start(
                                out=rbsb[:, :], in_=d_rrow.broadcast_to([64, 512])),
                                waits=bw, tag=f"rbb{un}_{hx}", sem="dEP0")
                            if par == 0:
                                tr.step("vector", lambda v, hl=hl, mb=mb, cA=cA, un=un:
                                        v.tensor_mul(
                                            attns[0:64, hl, mb:mb + 512],
                                            avsb[0:64, un % 2, cA:cA + 512], rbsb[:, :]),
                                        waits=[f"rbb{un}_{hx}"], tag=f"ep{un}_{hx}")
                            else:
                                aw_ = [f"rbb{un}_{hx}"]
                                if hx == 1:
                                    aw_.append(f"ep{un}_0")
                                tr.step("vector", lambda v, cA=cA, un=un: v.tensor_mul(
                                    avtmp[:, :], avsb[0:64, un % 2, cA:cA + 512], rbsb[:, :]),
                                    waits=aw_, tag=f"avt{un}_{hx}")
                                tr.step("gpsimd", lambda g, hl=hl, mb=mb: g.dma_start(
                                    out=attns[64:128, hl, mb:mb + 512], in_=avtmp[:, :]),
                                    waits=[f"avt{un}_{hx}"], tag=f"ep{un}_{hx}", sem="dEP1")
                with nc.Block("p4") as bk:
                    tr.emit(bk, sems)
                carry = {k: tr.events[k] for k in ["consts", "faccspill", "bglast"]}
                carry["attdone"] = tr.events["ep31_1"]
                carry["wodone"] = tr.events["wo3"]
                cnts = dict(tr.counts)

        with ExitStack() as s5:
            r64 = [s5.enter_context(nc.sbuf_tensor(f"r64_{i}", [128, 64], F32)) for i in range(6)]
            rw = s5.enter_context(nc.sbuf_tensor("rw", [128, 64, 8], F32))
            rw2 = s5.enter_context(nc.sbuf_tensor("rw2", [128, 64, 8], F32))
            sbc = s5.enter_context(nc.sbuf_tensor("sbc", [128, 8], F32))
            s8 = s5.enter_context(nc.sbuf_tensor("s8", [2, 8], F32))
            ea = s5.enter_context(nc.sbuf_tensor("ea", [64, 128], F32))
            eb = s5.enter_context(nc.sbuf_tensor("eb", [64, 128], F32))
            ind = s5.enter_context(nc.sbuf_tensor("ind", [64, 128], F32))
            sca = s5.enter_context(nc.sbuf_tensor("sca", [64, 8, 128], F32))
            scb = s5.enter_context(nc.sbuf_tensor("scb", [64, 8, 128], F32))
            zer = s5.enter_context(nc.sbuf_tensor("zer", [64, 128], F32))
            t01 = s5.enter_context(nc.sbuf_tensor("t01", [64, 16], F32))
            tsb = s5.enter_context(nc.sbuf_tensor("tsb", [128, 8], F32))
            offs = s5.enter_context(nc.sbuf_tensor("offs", [128, 8], F32))
            off1 = s5.enter_context(nc.sbuf_tensor("off1", [64, 8], F32))
            ppx = [s5.enter_context(nc.sbuf_tensor(f"ppx{i}", [64, 128], F32)) for i in range(4)]
            own8 = s5.enter_context(nc.sbuf_tensor("own8", [8, 4, 128], F32))
            uo = s5.enter_context(nc.sbuf_tensor("uo", [8, 8, 128], F32))
            po = s5.enter_context(nc.sbuf_tensor("po", [8, 8, 128], F32))
            utmp8 = s5.enter_context(nc.sbuf_tensor("utmp8", [64, 8, 128], F32))
            o8 = [s5.enter_context(nc.sbuf_tensor(f"o8_{i}", [8, 128], F32)) for i in range(5)]
            ps_a = s5.enter_context(nc.psum_tensor("ps_a", [128, 1024], F32))
            ps_b = s5.enter_context(nc.psum_tensor("ps_b", [128, 1024], F32))
            ps_c = s5.enter_context(nc.psum_tensor("ps_c", [128, 1024], F32))

            tr = Tracer(cnts, carry)
            tr.pad_vector = True

            def drain(t):
                return t.matmul(ps_b[0:1, 512:1024], onesbf[:, :], wos[:, 0, 0:512],
                                start=True, stop=True, skip_group_check=True)

            def em_srow(t):
                for dk in range(DK):
                    t.matmul(ps_a[0:1, 0:10], onesf[:, :], rt1t[:, dk, :],
                             start=(dk == 0), stop=(dk == DK - 1),
                             skip_group_check=True)
                return drain(t)
            tr.step("tensor", em_srow, waits=["consts", "attdone"], tag="srow")

            def em_s8(v):
                v.memset(A(s8), 0.0)
                return v.tensor_copy(s8[0:1, :], ps_a[0:1, 0:8])
            tr.step("vector", em_s8, waits=["srow"], tag="s8ev")
            tr.step("tensor", lambda t: t.matmul(ps_b[0:128, 0:8], onesc[:, :], s8[:, :],
                                                 start=True, stop=True, skip_group_check=True),
                    waits=["s8ev"], tag="sbcmm")
            tr.step("vector", lambda v: v.tensor_copy(sbc[:, :], ps_b[0:128, 0:8]),
                    waits=["sbcmm"], tag="sbcev")
            tr.step("vector", lambda v: v.tensor_scalar_mul(r64[0][:, :], rtraw[:, :, 8], 1.0 / H),
                    waits=["bglast"], tag="mtok")

            def em_u(v):
                for e in range(8):
                    v.tensor_scalar(r64[1][:, :], r64[0][:, :], sbc[:, e:e + 1], None, ALU.mult)
                    v.tensor_sub(rw[:, :, e], rtraw[:, :, e], r64[1][:, :])
                return None
            tr.step("vector", em_u, waits=["mtok", "sbcev"], tag="u_done")

            def em_top(v):
                v.tensor_max(r64[0][:, :], rw[:, :, 0], rw[:, :, 1])
                for e in range(2, 8):
                    v.tensor_max(r64[0][:, :], r64[0][:, :], rw[:, :, e])
                for e in range(8):
                    v.tensor_tensor(r64[1][:, :], rw[:, :, e], r64[0][:, :], ALU.is_equal)
                    v.tensor_scalar_mul(r64[2][:, :], r64[1][:, :], float(8 - e))
                    if e == 0:
                        v.tensor_copy(r64[3][:, :], r64[2][:, :])
                    else:
                        v.tensor_max(r64[3][:, :], r64[3][:, :], r64[2][:, :])
                v.tensor_scalar(r64[4][:, :], r64[3][:, :], -1.0, 8.0, ALU.mult, ALU.add)
                for e in range(8):
                    v.tensor_scalar(r64[1][:, :], r64[4][:, :], float(e), None, ALU.is_equal)
                    v.scalar_tensor_tensor(rw2[:, :, e], r64[1][:, :], -1e30, rw[:, :, e],
                                           ALU.mult, ALU.add)
                v.tensor_max(r64[0][:, :], rw2[:, :, 0], rw2[:, :, 1])
                for e in range(2, 8):
                    v.tensor_max(r64[0][:, :], r64[0][:, :], rw2[:, :, e])
                for e in range(8):
                    v.tensor_tensor(r64[1][:, :], rw2[:, :, e], r64[0][:, :], ALU.is_equal)
                    v.tensor_scalar_mul(r64[2][:, :], r64[1][:, :], float(8 - e))
                    if e == 0:
                        v.tensor_copy(r64[5][:, :], r64[2][:, :])
                    else:
                        v.tensor_max(r64[5][:, :], r64[5][:, :], r64[2][:, :])
                return v.tensor_scalar(r64[5][:, :], r64[5][:, :], -1.0, 8.0, ALU.mult, ALU.add)
            tr.step("vector", em_top, waits=["u_done"], tag="top_done")

            def em_etr(t):
                t.transpose(ps_a[0:64, 0:128], r64[4][:, :], identt[:, :])
                t.transpose(ps_a[0:64, 512:640], r64[5][:, :], identt[:, :])
                return drain(t)
            tr.step("tensor", em_etr, waits=["top_done"], tag="etr")

            def em_eev(v):
                v.tensor_copy(ea[:, :], ps_a[0:64, 0:128])
                v.tensor_copy(eb[:, :], ps_a[0:64, 512:640])
                return v.memset(zer[:, :], 0.0)
            tr.step("vector", em_eev, waits=["etr"], tag="eev")

            def em_scan(v):
                for e in range(8):
                    v.tensor_scalar(ind[:, :], ea[:, :], float(e), None, ALU.is_equal)
                    v.tensor_tensor_scan(sca[:, e, :], ind[:, :], zer[:, :], 0.0,
                                         ALU.add, ALU.add)
                    v.tensor_scalar(ind[:, :], eb[:, :], float(e), None, ALU.is_equal)
                    v.tensor_tensor_scan(scb[:, e, :], ind[:, :], zer[:, :], 0.0,
                                         ALU.add, ALU.add)
                for e in range(8):
                    v.tensor_copy(t01[:, e:e + 1], sca[:, e, 127:128])
                    v.tensor_copy(t01[:, 8 + e:8 + e + 1], scb[:, e, 127:128])
                return None
            tr.step("vector", em_scan, waits=["eev"], tag="scans")

            def em_tmm(t):
                t.matmul(ps_b[0:128, 8:16], idlow[:, :], t01[:, 0:8], start=True, stop=False,
                         skip_group_check=True)
                t.matmul(ps_b[0:128, 8:16], sh64[:, :], t01[:, 8:16], start=False,
                         stop=True, skip_group_check=True)
                return drain(t)
            tr.step("tensor", em_tmm, waits=["scans"], tag="tmm")
            tr.step("vector", lambda v: v.tensor_copy(tsb[:, :], ps_b[0:128, 8:16]),
                    waits=["tmm"], tag="tev")
            def em_offmm(t):
                t.matmul(ps_b[0:128, 16:24], lstr[:, :], tsb[:, :],
                         start=True, stop=True, skip_group_check=True)
                return drain(t)
            tr.step("tensor", em_offmm, waits=["tev"], tag="offmm")
            tr.step("vector", lambda v: v.tensor_copy(offs[:, :], ps_b[0:128, 16:24]),
                    waits=["offmm"], tag="offev")
            def em_off1mm(t):
                t.matmul(ps_b[0:64, 24:32], shdn[:, :], offs[:, :],
                         start=True, stop=True, skip_group_check=True)
                return drain(t)
            tr.step("tensor", em_off1mm, waits=["offev"], tag="off1mm")
            tr.step("vector", lambda v: v.tensor_copy(off1[:, :], ps_b[0:64, 24:32]),
                    waits=["off1mm"], tag="off1ev")

            def em_pp(v):
                for e in range(8):
                    v.tensor_scalar(ind[:, :], ea[:, :], float(e), None, ALU.is_equal)
                    v.tensor_scalar(ppx[2][:, :], sca[:, e, :], offs[0:64, e:e + 1], None, ALU.add)
                    v.tensor_mul(ppx[3][:, :], ppx[2][:, :], ind[:, :])
                    if e == 0:
                        v.tensor_copy(ppx[0][:, :], ppx[3][:, :])
                    else:
                        v.tensor_add(ppx[0][:, :], ppx[0][:, :], ppx[3][:, :])
                for e in range(8):
                    v.tensor_scalar(ind[:, :], eb[:, :], float(e), None, ALU.is_equal)
                    v.tensor_scalar(ppx[2][:, :], scb[:, e, :], off1[:, e:e + 1], None, ALU.add)
                    v.tensor_mul(ppx[3][:, :], ppx[2][:, :], ind[:, :])
                    if e == 0:
                        v.tensor_copy(ppx[1][:, :], ppx[3][:, :])
                    else:
                        v.tensor_add(ppx[1][:, :], ppx[1][:, :], ppx[3][:, :])
                v.tensor_scalar(ppx[2][:, :], ppx[0][:, :], float(CAP), None, ALU.is_le)
                v.tensor_scalar(ppx[3][:, :], ppx[1][:, :], float(CAP), None, ALU.is_le)
                return None
            tr.step("vector", em_pp, waits=["off1ev"], tag="pp_done")

            def em_ownx(t):
                t.matmul(ps_a[0:8, 256:384], selc8[:, :], ppx[2][:, :], start=True, stop=True,
                         skip_group_check=True)
                t.matmul(ps_a[0:8, 640:768], selc8[:, :], ppx[3][:, :], start=True, stop=True,
                         skip_group_check=True)
                t.matmul(ps_c[0:8, 512:640], selc8[:, :], ea[:, :], start=True, stop=True,
                         skip_group_check=True)
                t.transpose(ps_b[0:8, 128:256], rstdtok[:, :], identt[:, :])
                return drain(t)
            tr.step("tensor", em_ownx, waits=["pp_done"], tag="ownx")

            def em_ownev(v):
                v.tensor_copy(own8[:, 0, :], ps_a[0:8, 256:384])
                v.tensor_copy(own8[:, 1, :], ps_a[0:8, 640:768])
                v.tensor_copy(own8[:, 2, :], ps_c[0:8, 512:640])
                return v.tensor_copy(own8[:, 3, :], ps_b[0:8, 128:256])
            tr.step("vector", em_ownev, waits=["ownx"], tag="ownev")
            def em_utr(t):
                for e in range(8):
                    t.transpose(ps_c[0:64, 128 * e:128 * e + 128], rw[:, :, e], identt[:, :])
                return drain(t)
            tr.step("tensor", em_utr, waits=["ownev"], tag="utr_all")
            tr.step("vector", lambda v: v.tensor_copy(
                A(utmp8).rearrange("p a c -> p (a c)"), ps_c[0:64, 0:1024]),
                waits=["utr_all"], tag="utev_all")

            def em_uxm(t):
                for e in range(8):
                    t.matmul(ps_a[0:8, 128 * e:128 * e + 128], selc8[:, :], utmp8[:, e, :],
                             start=(e % 4 == 0), stop=True, skip_group_check=True)
                return drain(t)
            tr.step("tensor", em_uxm, waits=["utev_all"], tag="uxm_all")
            tr.step("vector", lambda v: v.tensor_copy(
                A(uo).rearrange("p a c -> p (a c)"), ps_a[0:8, 0:1024]),
                waits=["uxm_all"], tag="uoev7")

            def em_pmath(v):
                for e in range(8):
                    v.tensor_mul(po[:, e, :], uo[:, e, :], own8[:, 3, :])
                return None
            tr.step("vector", em_pmath, waits=["uoev7", "ownev"], tag="l_done")
            tr.step("scalar", lambda sc: sc.activation(A(uo).rearrange("p a b -> p (a b)"),
                                                       A(po).rearrange("p a b -> p (a b)"),
                                                       ACT.Exp, bias=0.0, scale=1.0),
                    waits=["l_done"], tag="pexp")

            def em_pvals(v):
                v.tensor_add(o8[0][:, :], uo[:, 0, :], uo[:, 1, :])
                for e in range(2, 8):
                    v.tensor_add(o8[0][:, :], o8[0][:, :], uo[:, e, :])
                v.reciprocal(o8[1][:, :], o8[0][:, :])
                v.tensor_max(o8[2][:, :], uo[:, 0, :], uo[:, 1, :])
                for e in range(2, 8):
                    v.tensor_max(o8[2][:, :], o8[2][:, :], uo[:, e, :])
                v.tensor_mul(o8[2][:, :], o8[2][:, :], o8[1][:, :])
                for e in range(8):
                    v.tensor_scalar(o8[3][:, :], own8[:, 2, :], float(e), None, ALU.is_equal)
                    v.tensor_scalar(o8[3][:, :], o8[3][:, :], -1.0, 1.0, ALU.mult, ALU.add)
                    v.tensor_mul(po[:, e, :], uo[:, e, :], o8[3][:, :])
                v.tensor_max(o8[3][:, :], po[:, 0, :], po[:, 1, :])
                for e in range(2, 8):
                    v.tensor_max(o8[3][:, :], o8[3][:, :], po[:, e, :])
                v.tensor_mul(o8[3][:, :], o8[3][:, :], o8[1][:, :])
                v.tensor_mul(o8[4][:, :], own8[:, 0, :], o8[2][:, :])
                v.tensor_mul(o8[0][:, :], own8[:, 1, :], o8[3][:, :])
                return v.tensor_add(o8[4][:, :], o8[4][:, :], o8[0][:, :])
            tr.step("vector", em_pvals, waits=["pexp"], tag="pvals")
            def em_gtr(t):
                t.transpose(ps_c[0:128, 256:264], o8[4][:, :], identt[0:8, 0:8])
                return drain(t)
            tr.step("tensor", em_gtr, waits=["pvals"], tag="g_tr")
            tr.step("vector", lambda v: v.tensor_copy(gtok[:, :], ps_c[0:128, 256:264]),
                    waits=["g_tr"], tag="g_ev")
            with nc.Block("p5a") as bk:
                tr.emit(bk, sems)
            carry = {k: tr.events[k] for k in ["consts", "faccspill", "attdone", "wodone", "g_ev"]}
            cnts = dict(tr.counts)

        with ExitStack() as s6:
            acc = s6.enter_context(nc.sbuf_tensor("acc", [128, 8, H], F32))
            cmb = s6.enter_context(nc.sbuf_tensor("cmb", [128, 512], F32))
            xst = s6.enter_context(nc.sbuf_tensor("xst", [128, 1, H], F32))
            fst = s6.enter_context(nc.sbuf_tensor("fst", [128, 1, H], F32))
            pw = [s6.enter_context(nc.psum_tensor(f"pw{i}", [128, 1024], F32)) for i in range(4)]
            pwv = [pw[i // 2][:, 512 * (i % 2):512 * (i % 2) + 512] for i in range(4)]
            pwv2 = [pw[2 + i // 2][:, 512 * (i % 2):512 * (i % 2) + 512] for i in range(4)]
            tr = Tracer(cnts, carry)
            for tc in range(8):
                tr.step("sync", lambda sy, tc=tc: sy.dma_start(
                    out=xst[:, 0, :], in_=d_xacc[:, tc, :]),
                    waits=[f"cmb{tc-1}_3"] if tc >= 1 else ["attdone"],
                    tag=f"xld{tc}", sem="dXLD0")
                tr.step("gpsimd", lambda g, tc=tc: g.dma_start(
                    out=fst[:, 0, :], in_=d_facc[tc, :, :]),
                    waits=[f"cmb{tc-1}_3"] if tc >= 1 else ["faccspill", "attdone"],
                    tag=f"fld{tc}", sem="dFLD0")

                def em_wo(t, tc=tc):
                    ps = pwv if tc % 2 == 0 else pwv2
                    slw = A(attns)[:, 2 * tc:2 * tc + 2, :].rearrange(
                        "p h (s a) -> p h s a", a=16)
                    for ac in range(16):
                        lhs = slw[:, :, :, ac:ac + 1]
                        for hq in range(4):
                            last = t.matmul(ps[hq], lhs,
                                            wos[:, ac, 512 * hq:512 * hq + 512],
                                            start=(ac == 0), stop=(ac == 15),
                                            skip_group_check=True)
                    return last
                if tc == 0:
                    ww = ["attdone", "wodone", "g_ev"]
                elif tc == 1:
                    ww = ["g_ev"]
                else:
                    ww = [f"cmb{tc-2}_3"]
                tr.step("tensor", em_wo, waits=ww, tag=f"wo_mm{tc}")

                for hq in range(4):
                    def em_cmb(v, tc=tc, hq=hq):
                        ps = (pwv if tc % 2 == 0 else pwv2)[hq]
                        asl = acc[:, tc, 512 * hq:512 * hq + 512]
                        hsl = slice(512 * hq, 512 * hq + 512)
                        v.scalar_tensor_tensor(cmb[:, :], b2_bc[:, hsl],
                                               gtok[:, tc:tc + 1], ps, ALU.mult, ALU.add)
                        v.scalar_tensor_tensor(asl, fst[:, 0, hsl], gtok[:, tc:tc + 1],
                                               cmb[:, :], ALU.mult, ALU.add)
                        v.tensor_add(cmb[:, :], xst[:, 0, hsl], bo_bc[:, hsl])
                        return v.tensor_add(asl, asl, cmb[:, :])
                    tr.step("vector", em_cmb,
                            waits=[f"wo_mm{tc}", f"xld{tc}", f"fld{tc}", "g_ev"],
                            tag=f"cmb{tc}_{hq}")
            tr.step("sync", lambda sy: sy.dma_start(out=d_out[:, :, :], in_=A(acc)),
                    waits=["cmb7_3"], tag="stored", sem="dOUT")
            tr.step("gpsimd", lambda g: None, waits=["stored"], tag="final", sem="dOUT")
            with nc.Block("p5b") as bk:
                tr.emit(bk, sems)
        eAW.close()

    return nc


_PROG = None


def _get_prog():
    global _PROG
    if _PROG is None:
        _PROG = build_program()
    return _PROG


def kernel(L0_attn_X, L0_attn_Wq, L0_attn_bq, L0_attn_Wk, L0_attn_bk, L0_attn_Wv,
           L0_attn_bv, L0_attn_Wo, L0_attn_bo, L0_ffn_router_w, L0_ffn_w1, L0_ffn_b1,
           L0_ffn_w2, L0_ffn_b2):
    X = np.ascontiguousarray(np.asarray(L0_attn_X, dtype=np.float32)).reshape(NTOK, H)
    Wq = np.asarray(L0_attn_Wq, np.float32)
    Wk = np.asarray(L0_attn_Wk, np.float32)
    Wv = np.asarray(L0_attn_Wv, np.float32)
    Wo = np.asarray(L0_attn_Wo, np.float32)
    bq = np.asarray(L0_attn_bq, np.float32)
    bk = np.asarray(L0_attn_bk, np.float32)
    bv = np.asarray(L0_attn_bv, np.float32)
    bo = np.asarray(L0_attn_bo, np.float32)
    rt = np.asarray(L0_ffn_router_w, np.float32)
    w1 = np.asarray(L0_ffn_w1, np.float32)
    b1 = np.asarray(L0_ffn_b1, np.float32)
    w2 = np.asarray(L0_ffn_w2, np.float32)
    b2 = np.asarray(L0_ffn_b2, np.float32)

    XT = np.ascontiguousarray(X.T)
    xstats = np.ascontiguousarray(XT.reshape(DK, 128, NQ, QTR).transpose(2, 1, 0, 3))
    wo_h = np.ascontiguousarray(Wo.T.reshape(16, 128, H).transpose(1, 0, 2))
    w1_h = np.ascontiguousarray(
        w1.T.reshape(DK, 128, NFG, FG, 128).transpose(2, 1, 3, 0, 4))
    w2_h = np.ascontiguousarray(w2.T.reshape(NFG, FG, 128, H).transpose(0, 2, 1, 3))
    rt1 = np.zeros((H, 10), np.float32)
    rt1[:, 0:8] = rt.T
    rt1[:, 8] = 1.0
    rt1_h = np.ascontiguousarray(rt1.reshape(DK, 128, 10).transpose(1, 0, 2))
    b1_t = np.ascontiguousarray(b1.reshape(64, 128).T)
    ident = np.eye(128, dtype=np.float32)
    lstrict = np.ascontiguousarray(np.tril(np.ones((128, 128), np.float32), -1).T)
    idlow = np.zeros((64, 128), np.float32)
    idlow[np.arange(64), np.arange(64)] = 1.0
    shift64 = np.zeros((64, 128), np.float32)
    shift64[np.arange(64), np.arange(64) + 64] = 1.0
    shiftdn = np.zeros((128, 64), np.float32)
    shiftdn[np.arange(64) + 64, np.arange(64)] = 1.0
    onesd = np.zeros((2, 128), np.float32)
    onesd[1, :] = 1.0
    sel64 = np.zeros((65, 128), np.float32)
    sel64[64, :] = 1.0
    onesc = np.zeros((2, 128), np.float32)
    onesc[0, :] = 1.0
    onesf = np.ones((128, 1), np.float32)
    onesbf = np.ones((128, 1), BF)

    nc = _get_prog()
    in_maps = []
    for c in range(8):
        b = c // 2
        cp = c % 2
        seq = XT[:, S * b:S * (b + 1)]
        xown = np.ascontiguousarray(seq.reshape(DK, 128, 4, QTR).transpose(2, 1, 0, 3))
        ownT = seq[:, OWN * cp:OWN * (cp + 1)]
        xow = np.ascontiguousarray(ownT.reshape(DK, 128, 2, QTR).transpose(2, 1, 0, 3))
        xacc = np.ascontiguousarray(
            X[OWN * c:OWN * (c + 1)].reshape(8, 128, H).transpose(1, 0, 2))
        qcols = np.empty(1024, np.int64)
        i = 0
        for ch in range(8):
            g_, ckv = ch // 2, ch % 2
            for kvpar in range(2):
                kvg = 4 * cp + 2 * ckv + kvpar
                for d in range(64):
                    qcols[i] = kvg * 256 + g_ * 64 + d
                    i += 1
        WqTs = np.ascontiguousarray(Wq.T[:, qcols])
        wq_h = np.ascontiguousarray(
            WqTs.reshape(DK, 128, 8, 128).transpose(1, 2, 0, 3))
        bq_t = np.ascontiguousarray(bq[qcols].reshape(8, 128).T)
        kvrows = np.arange(4 * cp * 64, (4 * cp + 4) * 64)
        WkTs = np.ascontiguousarray(Wk[kvrows, :].T)
        wk_h = np.ascontiguousarray(WkTs.reshape(DK, 128, 256).transpose(1, 0, 2))
        WvTs = np.ascontiguousarray(Wv[kvrows, :].T)
        wv_h = np.ascontiguousarray(WvTs.reshape(DK, 128, 256).transpose(1, 0, 2))
        bk_t = np.ascontiguousarray(bk[kvrows].reshape(2, 128).T)
        brow = np.zeros((3, H), np.float32)
        brow[0] = bo
        brow[1] = b2
        brow[2, 0:256] = bv[kvrows]
        selc8 = np.zeros((64, 8), np.float32)
        for j in range(8):
            selc8[8 * c + j, j] = 1.0
        in_maps.append({
            "xownseq": xown, "xowhalf": xow, "xstats": xstats, "xacc": xacc,
            "wq": wq_h, "wk": wk_h, "wv": wv_h, "wo": wo_h,
            "w1": w1_h, "w2": w2_h, "rt1": rt1_h,
            "bqt": bq_t, "bkt": bk_t, "b1t": b1_t, "brow": brow,
            "identc": ident, "lstrictc": lstrict, "idlowc": idlow,
            "shift64c": shift64, "shiftdnc": shiftdn, "onesdc": onesd,
            "sel64c": sel64, "onescc": onesc, "onesfc": onesf,
            "onesbfc": onesbf, "selc8c": selc8,
        })
    res = run_bass_kernel_spmd(nc, in_maps, core_ids=list(range(8)))
    parts = []
    for c in range(8):
        o = res.results[c]["outp"]
        parts.append(np.ascontiguousarray(np.asarray(o).transpose(1, 0, 2)).reshape(OWN, H))
    return np.concatenate(parts, axis=0).reshape(Bb, S, H)


# version 15 (final)
# speedup vs baseline: 1.1708x; 1.0276x over previous
import numpy as np
import ml_dtypes
from contextlib import ExitStack
from functools import partial

import concourse.bass as bass
import concourse.mybir as mybir
from concourse.bass_utils import run_bass_kernel_spmd

F32 = mybir.dt.float32
BF16 = mybir.dt.bfloat16
ALU = mybir.AluOpType
ACT = mybir.ActivationFunctionType
AX = mybir.AxisListType
BF = ml_dtypes.bfloat16

H = 2048
S = 2048
Bb = 4
NTOK = Bb * S
OWN = 1024
CAP = 512
FF = 8192
DK = 16
QTR = 512
NQ = NTOK // QTR
FG = 8
NFG = FF // 128 // FG
NU = 2 * NFG


def A(th):
    return th[(slice(None),) * len(th.shape)]


class VPad:

    def __init__(self, v, n=4):
        self._v = v
        self._n = n

    def __getattr__(self, name):
        f = getattr(self._v, name)
        if name in ("nop", "wait_ge") or not callable(f):
            return f

        def wrapped(*a, **k):
            r = f(*a, **k)
            for _ in range(self._n):
                self._v.nop()
            return r
        return wrapped


class Tracer:
    ENGINES = ("sync", "gpsimd", "tensor", "scalar", "vector")

    def __init__(self, counts=None, events=None):
        self.steps = []
        self.counts = dict(counts) if counts else {}
        self.events = dict(events) if events else {}

    def step(self, engine, emit_fn, waits=(), tag=None, sem=None):
        if sem is None:
            assert engine not in ("sync", "gpsimd"), "DMA steps need an explicit stream sem"
            sem = engine
        inc = 16 if engine in ("sync", "gpsimd") else 1
        for w in waits:
            assert w in self.events, f"wait on unknown/future event {w!r}"
        self.counts[sem] = self.counts.get(sem, 0) + inc
        if tag is not None:
            assert tag not in self.events, f"dup tag {tag}"
            self.events[tag] = (sem, self.counts[sem])
        self.steps.append((engine, tuple(waits), emit_fn, inc, sem))

    def emit(self, bk, sems):
        per = {e: [] for e in self.ENGINES}
        for st in self.steps:
            per[st[0]].append(st)

        def mk(engine):
            def _(eng):
                if engine == "vector" and getattr(self, "pad_vector", False):
                    eng = VPad(eng)
                for _e, waits, emit_fn, inc, sem in per[engine]:
                    done = set()
                    for w in waits:
                        ws, wv = self.events[w]
                        if ws == engine or (ws, wv) in done:
                            continue
                        done.add((ws, wv))
                        eng.wait_ge(sems[ws], wv)
                    last = emit_fn(eng)
                    if last is None:
                        last = eng.nop()
                    last.then_inc(sems[sem], inc)
            return _

        for e in self.ENGINES:
            if per[e]:
                getattr(bk, e)(mk(e))


def build_program():
    nc = bass.Bass("TRN2", target_bir_lowering=False, detect_race_conditions=False)

    def din(name, shape, dt=F32):
        return nc.dram_tensor(name, shape, dt, kind="ExternalInput").ap()

    d_xown = din("xownseq", [4, 128, DK, QTR])
    d_xow = din("xowhalf", [2, 128, DK, QTR])
    d_xstats = din("xstats", [NQ, 128, DK, QTR])
    d_xacc = din("xacc", [128, 8, H])
    d_wq = din("wq", [128, 8, DK, 128])
    d_wk = din("wk", [128, DK, 256])
    d_wv = din("wv", [128, DK, 256])
    d_wo = din("wo", [128, 16, H])
    d_w1 = din("w1", [NFG, 128, FG, DK, 128])
    d_w2 = din("w2", [NFG, 128, FG, H])
    d_rt1 = din("rt1", [128, DK, 10])
    d_bq = din("bqt", [128, 8])
    d_bk = din("bkt", [128, 2])
    d_b1 = din("b1t", [128, 64])
    d_brow = din("brow", [3, H])
    d_ident = din("identc", [128, 128])
    d_lstrict = din("lstrictc", [128, 128])
    d_idlow = din("idlowc", [64, 128])
    d_sh64 = din("shift64c", [64, 128])
    d_shdn = din("shiftdnc", [128, 64])
    d_onesd = din("onesdc", [2, 128])
    d_sel64 = din("sel64c", [65, 128])
    d_onesc = din("onescc", [2, 128])
    d_onesf = din("onesfc", [128, 1])
    d_onesbf = din("onesbfc", [128, 1], BF16)
    d_selc8 = din("selc8c", [64, 8])
    d_out = nc.dram_tensor("outp", [128, 8, H], F32, kind="ExternalOutput").ap()
    d_xnsp = nc.dram_tensor("xnsp", [DK, 128, S], BF16, kind="Internal", addr_space="Local").ap()
    d_xnow = nc.dram_tensor("xnow", [DK, 128, OWN], BF16, kind="Internal", addr_space="Local").ap()
    d_facc = nc.dram_tensor("faccsp", [8, 128, H], F32, kind="Internal", addr_space="Local").ap()
    d_rrow = nc.dram_tensor("rrowsp", [1, 512], F32, kind="Internal", addr_space="Local").ap()

    es = ExitStack()
    eAW = ExitStack()
    with es:
        sb = lambda name, shape, dt: es.enter_context(nc.sbuf_tensor(name, shape, dt))
        identt = sb("identt", [128, 128], F32)
        lstr = sb("lstrt", [128, 128], F32)
        idlow = sb("idlowt", [64, 128], F32)
        sh64 = sb("sh64t", [64, 128], F32)
        shdn = sb("shdnt", [128, 64], F32)
        onesd = sb("onesdt", [2, 128], F32)
        sel64 = sb("sel64t", [65, 128], F32)
        onesc = sb("onesct", [2, 128], F32)
        onesf = sb("onesft", [128, 1], F32)
        onesbf = sb("onesbft", [128, 1], BF16)
        selc8 = sb("selc8t", [64, 8], F32)
        rt1t = sb("rt1t", [128, DK, 10], F32)
        bqt = sb("bqtt", [128, 8], F32)
        bkt = sb("bktt", [128, 2], F32)
        b1t = sb("b1tt", [128, 64], F32)
        bo_bc = sb("bo_bc", [128, H], F32)
        b2_bc = sb("b2_bc", [128, H], F32)
        bv_bc = sb("bv_bc", [128, 256], F32)
        rtraw = sb("rtraw", [128, 64, 10], F32)
        rstdtok = sb("rstdtok", [128, 8], F32)
        gtok = sb("gtok", [128, 8], F32)

        SEMNAMES = ["tensor", "scalar", "vector", "dCON", "dXQ0", "dXQ1", "dBGX", "dXLD0",
                    "dXLD1", "dOUT", "dWA", "dW1", "dW2", "dWOS",
                    "dXNF0", "dXNF1", "dXNP0", "dXNP1", "dSPILL", "dSPILL2", "dSPILLS",
                    "dFLD0", "dFLD1", "dEP0", "dEP1"]
        sems = {e: es.enter_context(nc.semaphore("sem_" + e)) for e in SEMNAMES}

        tr = Tracer()
        consts = [
            (identt, d_ident), (lstr, d_lstrict), (idlow, d_idlow),
            (sh64, d_sh64), (shdn, d_shdn), (onesd, d_onesd), (sel64, d_sel64),
            (onesc, d_onesc), (onesf, d_onesf), (onesbf, d_onesbf),
            (selc8, d_selc8), (rt1t, d_rt1), (bqt, d_bq), (bkt, d_bk), (b1t, d_b1),
        ]
        for dst, src in consts:
            tr.step("sync", lambda sy, dst=dst, src=src: sy.dma_start(out=A(dst), in_=src),
                    sem="dCON")
        tr.step("sync", lambda sy: sy.dma_start(out=bo_bc[:, :], in_=d_brow[0:1, :].broadcast_to([128, H])), sem="dCON")
        tr.step("sync", lambda sy: sy.dma_start(out=b2_bc[:, :], in_=d_brow[1:2, :].broadcast_to([128, H])), sem="dCON")
        tr.step("sync", lambda sy: sy.dma_start(out=bv_bc[:, :], in_=d_brow[2:3, 0:256].broadcast_to([128, 256])),
                sem="dCON", tag="consts")
        with nc.Block("p0") as bk:
            tr.emit(bk, sems)
        carry = {"consts": tr.events["consts"]}
        cnts = dict(tr.counts)

        with ExitStack() as s1:
            xq = s1.enter_context(nc.sbuf_tensor("xq", [128, 2, DK, QTR], F32))
            xn = s1.enter_context(nc.sbuf_tensor("xn", [128, DK, S], BF16))
            xnw = s1.enter_context(nc.sbuf_tensor("xnw", [128, DK, OWN], BF16))
            xsqb = s1.enter_context(nc.sbuf_tensor("xsqb", [128, 2, QTR], BF16))
            xqb = s1.enter_context(nc.sbuf_tensor("xqb", [128, 2, QTR], BF16))
            tmpn = s1.enter_context(nc.sbuf_tensor("tmpn", [128, QTR], F32))
            row2 = s1.enter_context(nc.sbuf_tensor("row2", [2, 2, QTR], F32))
            mrow2 = s1.enter_context(nc.sbuf_tensor("mrow2", [2, 2, QTR], F32))
            rva = s1.enter_context(nc.sbuf_tensor("rva", [1, 2, QTR], F32))
            rvb = s1.enter_context(nc.sbuf_tensor("rvb", [1, 2, QTR], F32))
            ps0 = s1.enter_context(nc.psum_tensor("ps0", [128, 2048], F32))
            ps1 = s1.enter_context(nc.psum_tensor("ps1", [128, 512], F32))
            ps2 = s1.enter_context(nc.psum_tensor("ps2", [128, 1024], F32))
            ps3 = s1.enter_context(nc.psum_tensor("ps3", [128, 512], F32))
            R_MSp = [ps0[0:1, 0:512], ps2[0:1, 0:512]]
            R_S2p = [ps0[0:1, 512:1024], ps2[0:1, 512:1024]]
            R_MBCp = [ps0[:, 1024:1536], ps3[:, 0:512]]
            R_RBC = ps0[:, 1536:2048]

            tr = Tracer(cnts, carry)
            def em_r2z(v):
                v.memset(A(row2), 0.0)
                return v.memset(A(mrow2), 0.0)
            tr.step("vector", em_r2z, tag="r2z")
            for q in range(6):
                w = [f"norm_done{q-2}", f"s2mm{q-2}_{DK-1}"] if q >= 2 else []
                src_ap = d_xown[q, :, :, :] if q < 4 else d_xow[q - 4, :, :, :]
                tr.step("sync", lambda sy, src_ap=src_ap, q=q: sy.dma_start(
                    out=xq[:, q % 2, :, :], in_=src_ap),
                    waits=w, tag=f"xq{q}", sem=f"dXQ{q % 2}")

                for dk in range(DK):
                    w = [f"xq{q}"]
                    if dk >= 2:
                        w.append(f"s2mm{q}_{dk-2}")
                    elif q >= 1:
                        w.append(f"s2mm{q-1}_{DK-1}")

                    def em_xsq(sc, q=q, dk=dk):
                        sc.activation(xsqb[:, dk % 2, :], xq[:, q % 2, dk, :], ACT.Square,
                                      bias=0.0, scale=1.0)
                        return sc.activation(xqb[:, dk % 2, :], xq[:, q % 2, dk, :], ACT.Identity,
                                             bias=0.0, scale=1.0)
                    tr.step("scalar", em_xsq, waits=w, tag=f"xsq{q}_{dk}")
                    w2_ = [f"xsq{q}_{dk}"]
                    if dk == 0 and q >= 2:
                        w2_ += [f"ve{q-2}", f"evst{q-2}"]

                    def em_s2(t, q=q, dk=dk):
                        t.matmul(R_S2p[q % 2], onesbf[:, :], xsqb[:, dk % 2, :],
                                 start=(dk == 0), stop=(dk == DK - 1),
                                 skip_group_check=True)
                        return t.matmul(R_MSp[q % 2], onesbf[:, :], xqb[:, dk % 2, :],
                                        start=(dk == 0), stop=(dk == DK - 1),
                                        skip_group_check=True)
                    tr.step("tensor", em_s2, waits=w2_, tag=f"s2mm{q}_{dk}")
                tr.step("vector", lambda v, q=q: v.tensor_scalar_mul(
                    mrow2[0:1, q % 2, :], R_MSp[q % 2], 1.0 / H),
                        waits=[f"s2mm{q}_{DK-1}", "consts", "r2z"]
                        + ([f"mbc{q-2}"] if q >= 2 else []),
                        tag=f"evst{q}")
                tr.step("tensor", lambda t, q=q: t.matmul(
                    R_MBCp[q % 2], onesc[:, :], mrow2[:, q % 2, :],
                    start=True, stop=True, skip_group_check=True),
                        waits=[f"evst{q}"] + ([f"norm_done{q-2}"] if q >= 2 else []),
                        tag=f"mbc{q}")

                def em_ve(v, q=q):
                    p = q % 2
                    v.tensor_copy(rvb[0:1, p, :], R_MBCp[p][0:1, :])
                    v.tensor_mul(rva[0:1, p, :], rvb[0:1, p, :], rvb[0:1, p, :])
                    v.scalar_tensor_tensor(rvb[0:1, p, :], R_S2p[p], 1.0 / H,
                                           rva[0:1, p, :], ALU.mult, ALU.subtract)
                    return v.tensor_scalar_add(rva[0:1, p, :], rvb[0:1, p, :], 1e-5)
                tr.step("vector", em_ve, waits=[f"mbc{q}", f"s2mm{q}_{DK-1}"], tag=f"ve{q}")
                tr.step("scalar", lambda sc, q=q: sc.activation(
                    rvb[0:1, q % 2, :], rva[0:1, q % 2, :], ACT.Sqrt, bias=0.0, scale=1.0),
                        waits=[f"ve{q}"], tag=f"sq{q}")

                def em_rrow(v, q=q):
                    p = q % 2
                    v.reciprocal(row2[0:1, p, :], rvb[0:1, p, :])
                    v.tensor_mul(rvb[0:1, p, :], row2[0:1, p, :], row2[0:1, p, :])
                    v.tensor_mul(rvb[0:1, p, :], rvb[0:1, p, :], rva[0:1, p, :])
                    v.tensor_scalar(rva[0:1, p, :], rvb[0:1, p, :], -0.5, 1.5,
                                    ALU.mult, ALU.add)
                    return v.tensor_mul(row2[0:1, p, :], row2[0:1, p, :], rva[0:1, p, :])
                tr.step("vector", em_rrow, waits=[f"sq{q}", "r2z"], tag=f"rrow{q}")
                tr.step("tensor", lambda t, q=q: t.matmul(
                    R_RBC, onesc[:, :], row2[:, q % 2, :],
                    start=True, stop=True, skip_group_check=True),
                        waits=[f"rrow{q}"] + ([f"norm_done{q-1}"] if q else []),
                        tag=f"rbc{q}")
                if q >= 4:
                    j = q - 4

                    def em_rtr(t, q=q):
                        for i in range(4):
                            t.transpose(ps1[:, i:i + 1],
                                        row2[0:1, q % 2, 128 * i:128 * i + 128],
                                        identt[0:1, 0:1])
                        return t.matmul(ps0[0:1, 0:512], onesbf[:, :], xsqb[:, 0, :],
                                        start=True, stop=True, skip_group_check=True)
                    tr.step("tensor", em_rtr,
                            waits=[f"rrow{q}"] + ([f"rtrev{j-1}"] if j else []), tag=f"rtr{j}")
                    tr.step("vector", lambda v, j=j: v.tensor_copy(rstdtok[:, 4 * j:4 * j + 4],
                                                                   ps1[:, 0:4]),
                            waits=[f"rtr{j}"], tag=f"rtrev{j}")
                for dk in range(DK):
                    def em_norm(v, q=q, dk=dk):
                        v.tensor_sub(tmpn[:, :], xq[:, q % 2, dk, :], R_MBCp[q % 2])
                        if q < 4:
                            return v.tensor_mul(xn[:, dk, QTR * q:QTR * q + QTR],
                                                tmpn[:, :], R_RBC)
                        return v.tensor_mul(xnw[:, dk, QTR * (q - 4):QTR * (q - 4) + QTR],
                                            tmpn[:, :], R_RBC)
                    tr.step("vector", em_norm, waits=[f"rbc{q}"], tag=f"norm{q}_{dk}")
                tr.step("vector", lambda v: None, tag=f"norm_done{q}")
            tr.step("sync", lambda sy: sy.dma_start(out=d_xnsp.rearrange("k p n -> p k n"),
                                                    in_=A(xn)),
                    waits=["norm_done3"], tag="xnspill", sem="dSPILLS")
            tr.step("gpsimd", lambda g: g.dma_start(
                out=d_xnow[:, :, 0:QTR].rearrange("k p n -> p k n"),
                in_=A(xnw)[:, :, 0:QTR]),
                    waits=["norm_done4"], tag="xnowspill0", sem="dSPILL2")
            tr.step("gpsimd", lambda g: g.dma_start(
                out=d_xnow[:, :, QTR:OWN].rearrange("k p n -> p k n"),
                in_=A(xnw)[:, :, QTR:OWN]),
                    waits=["norm_done5"], tag="xnowspill", sem="dSPILL2")
            with nc.Block("p1") as bk:
                tr.emit(bk, sems)
            carry = {"consts": tr.events["consts"], "xnspill": tr.events["xnspill"],
                     "xnowspill0": tr.events["xnowspill0"],
                     "xnowspill": tr.events["xnowspill"]}
            cnts = dict(tr.counts)

        with ExitStack() as s2:
            facc = s2.enter_context(nc.sbuf_tensor("facc", [128, 8, H], F32))
            w1b = s2.enter_context(nc.sbuf_tensor("w1b", [128, FG, DK, 128], BF16))
            w2b = s2.enter_context(nc.sbuf_tensor("w2b", [128, FG, H], BF16))
            mid = s2.enter_context(nc.sbuf_tensor("mid", [128, FG, 512], BF16))
            xnf = s2.enter_context(nc.sbuf_tensor("xnf", [128, 8, 512], BF16))
            xgs = s2.enter_context(nc.sbuf_tensor("xgs", [128, DK, QTR], F32))
            pp = [s2.enter_context(nc.psum_tensor(f"pp{i}", [128, 1024], F32)) for i in range(4)]
            ph1 = [pp[i // 2][:, 512 * (i % 2):512 * (i % 2) + 512] for i in range(8)]

            tr = Tracer(cnts, carry)
            tr.step("gpsimd", lambda g: g.dma_start(out=A(w1b), in_=d_w1[0]), tag="w1_0", sem="dW1")
            tr.step("gpsimd", lambda g: g.dma_start(out=A(w2b), in_=d_w2[0]), tag="w2_0", sem="dW2")

            def bg_quarter(q, cover):
                def em_bgst(t, q=q):
                    for dk in range(DK):
                        for tc4 in range(4):
                            last = t.matmul(
                                pp[0][0:128, 10 * tc4:10 * tc4 + 10],
                                xgs[:, dk, 128 * tc4:128 * tc4 + 128], rt1t[:, dk, :],
                                start=(dk == 0 and tc4 == 0), stop=(dk == DK - 1),
                                skip_group_check=True)
                    return last
                tr.step("tensor", em_bgst, waits=[f"bgx{q}", "consts"] + cover, tag=f"bgst{q}")

                def em_bgrs(v, q=q):
                    return v.tensor_copy(
                        A(rtraw)[:, 4 * q:4 * q + 4, :],
                        pp[0][0:128, 0:40].rearrange("p (a c) -> p a c", c=10))
                tr.step("vector", em_bgrs, waits=[f"bgst{q}"], tag=f"bgrs{q}")

            for u in range(NU):
                fg, hf = u // 2, u % 2
                if u >= 1 and (u - 1) < NQ:
                    bg_quarter(u - 1, [f"gel{u-1}_{FG-1}"])
                if u < NQ:
                    tr.step("sync", lambda sy, u=u: sy.dma_start(out=A(xgs), in_=d_xstats[u]),
                            waits=[f"bgst{u-1}"] if u >= 1 else [], tag=f"bgx{u}", sem="dBGX")

                def xnf_step(gdk):
                    w = ["xnowspill" if hf else "xnowspill0"]
                    if gdk >= 2:
                        w.append(f"h1g{u}_{4*(gdk-2)+3}")
                    elif u >= 1:
                        w.append(f"h1g{u-1}_{4*(gdk+2)+3}")
                    tr.step("gpsimd", lambda g, gdk=gdk, hf=hf: g.dma_start(
                        out=xnf[:, 4 * (gdk % 2):4 * (gdk % 2) + 4, :],
                        in_=d_xnow[4 * gdk:4 * gdk + 4, :, 512 * hf:512 * hf + 512]
                        .rearrange("k p n -> p k n")),
                        waits=w, tag=f"xnf{u}_{gdk}", sem=f"dXNF{gdk % 2}")
                xnf_step(0)
                xnf_step(1)
                for dk in range(DK):
                    gdk = dk // 4

                    def em_h1(t, dk=dk, gdk=gdk):
                        sl = xnf[:, 4 * (gdk % 2) + dk % 4, :]
                        for f in range(FG):
                            last = t.matmul(ph1[f], w1b[:, f, dk, :], sl,
                                            start=(dk == 0), stop=(dk == DK - 1),
                                            skip_group_check=True)
                        return last
                    w = [f"xnf{u}_{gdk}", f"w1_{fg}"]
                    if dk == 0:
                        if u >= 1:
                            w += [f"gel{u-1}_{FG-1}", f"h2e{u-1}_6", f"h2e{u-1}_7"]
                        q = u - 1
                        if 0 <= q < NQ:
                            w.append(f"bgrs{q}")
                    tr.step("tensor", em_h1, waits=w, tag=f"h1g{u}_{dk}")
                    if dk % 4 == 3 and gdk + 2 < 4:
                        xnf_step(gdk + 2)
                for f in range(FG):
                    w = [f"h1g{u}_{DK-1}"]
                    if u >= 1 and f == 0:
                        w.append(f"h2m{u-1}_7")
                    tr.step("scalar", lambda sc, fg=fg, f=f: sc.activation(
                        mid[:, f, :], ph1[f], ACT.Gelu,
                        bias=b1t[:, fg * FG + f:fg * FG + f + 1], scale=1.0),
                        waits=w, tag=f"gel{u}_{f}")
                if hf == 1 and fg + 1 < NFG:
                    tr.step("gpsimd", lambda g, fg=fg: g.dma_start(out=A(w1b), in_=d_w1[fg + 1]),
                            waits=[f"h1g{u}_{DK-1}"], tag=f"w1_{fg+1}", sem="dW1")
                for i in range(8):
                    tc, hh = i // 2, i % 2
                    gtc = 4 * hf + tc

                    def em_h2(t, i=i, tc=tc, hh=hh):
                        for f in range(FG):
                            for c2 in range(2):
                                last = t.matmul(
                                    pp[2 + i % 2][:, 512 * c2:512 * c2 + 512],
                                    mid[:, f, 128 * tc:128 * tc + 128],
                                    w2b[:, f, 1024 * hh + 512 * c2:1024 * hh + 512 * c2 + 512],
                                    start=(f == 0), stop=(f == FG - 1), skip_group_check=True)
                        return last
                    w = [f"gel{u}_{FG-1}", "w2_0" if fg == 0 else f"w2_{fg}"]
                    if i >= 2:
                        w.append(f"h2e{u}_{i-2}")
                    elif u >= 1:
                        w.append(f"h2e{u-1}_{6+i}")
                    tr.step("tensor", em_h2, waits=w, tag=f"h2m{u}_{i}")
                    if u <= 1:
                        def em_h2e(v, gtc=gtc, hh=hh, i=i):
                            return v.tensor_copy(facc[:, gtc, 1024 * hh:1024 * hh + 1024],
                                                 pp[2 + i % 2][:, :])
                    else:
                        def em_h2e(v, gtc=gtc, hh=hh, i=i):
                            sl = facc[:, gtc, 1024 * hh:1024 * hh + 1024]
                            return v.tensor_add(sl, sl, pp[2 + i % 2][:, :])
                    tr.step("vector", em_h2e, waits=[f"h2m{u}_{i}"], tag=f"h2e{u}_{i}")
                if hf == 1 and fg + 1 < NFG:
                    tr.step("gpsimd", lambda g, fg=fg: g.dma_start(out=A(w2b), in_=d_w2[fg + 1]),
                            waits=[f"h2m{u}_7"], tag=f"w2_{fg+1}", sem="dW2")
            bg_quarter(NQ - 1, [f"gel{NU-1}_{FG-1}"])
            tr.step("gpsimd", lambda g: g.dma_start(out=d_facc.rearrange("t p h -> p t h"),
                                                    in_=A(facc)),
                    waits=[f"h2e{NU-1}_7"], tag="faccspill", sem="dSPILL")
            with nc.Block("p2") as bk:
                tr.emit(bk, sems)
            carry = {k: tr.events[k] for k in ["consts", "xnspill", "faccspill"]}
            carry["bglast"] = tr.events[f"bgrs{NQ-1}"]
            cnts = dict(tr.counts)

        with ExitStack() as eqkv:
            qslab = eqkv.enter_context(nc.sbuf_tensor("qslab", [128, 8, S], BF16))
            kslab = eqkv.enter_context(nc.sbuf_tensor("kslab", [128, 2, S], BF16))
            vslab = eqkv.enter_context(nc.sbuf_tensor("vslab", [128, 16, 260], BF16))

            with ExitStack() as s3:
                wks = s3.enter_context(nc.sbuf_tensor("wks", [128, DK, 256], BF16))
                wvs = s3.enter_context(nc.sbuf_tensor("wvs", [128, DK, 256], BF16))
                wqb = s3.enter_context(nc.sbuf_tensor("wqb", [128, 8, DK, 128], BF16))
                xnp = s3.enter_context(nc.sbuf_tensor("xnp", [128, 8, 1024], BF16))
                pp = [s3.enter_context(nc.psum_tensor(f"pq{i}", [128, 1024], F32)) for i in range(4)]
                pv8 = [pp[2 + tc // 4][:, 256 * (tc % 4):256 * (tc % 4) + 256] for tc in range(8)]
                pq8 = [pp[i // 2][:, 512 * (i % 2):512 * (i % 2) + 512] for i in range(8)]

                tr = Tracer(cnts, carry)
                tr.step("gpsimd", lambda g: g.dma_start(out=A(wks), in_=d_wk), sem="dWA")
                tr.step("gpsimd", lambda g: g.dma_start(out=A(wvs), in_=d_wv), sem="dWA")
                tr.step("gpsimd", lambda g: g.dma_start(out=A(wqb), in_=d_wq), sem="dWA",
                        tag="wABurst")
                tr.step("vector", lambda v: v.memset(
                    A(vslab).rearrange("p k (g c) -> p k g c", c=65)[:, :, :, 64:65], 1.0),
                    tag="vones")

                for hf in range(2):
                    def kx_step(gdk, hf=hf):
                        w = ["xnspill"]
                        if gdk >= 2:
                            w.append(f"kv{hf}_{4*(gdk-2)+3}")
                        elif hf == 1:
                            w.append(f"kv0_{4*(gdk+2)+3}")
                        tr.step("sync", lambda sy, gdk=gdk, hf=hf: sy.dma_start(
                            out=xnp[:, 4 * (gdk % 2):4 * (gdk % 2) + 4, :],
                            in_=d_xnsp[4 * gdk:4 * gdk + 4, :, 1024 * hf:1024 * hf + 1024]
                            .rearrange("k p n -> p k n")),
                            waits=w, tag=f"kx{hf}_{gdk}", sem=f"dXNP{gdk % 2}")
                    kx_step(0)
                    kx_step(1)
                    for dk in range(DK):
                        gdk = dk // 4

                        def em_kv(t, hf=hf, dk=dk, gdk=gdk):
                            sl = xnp[:, 4 * (gdk % 2) + dk % 4, :]
                            for oc in range(2):
                                for c2 in range(2):
                                    t.matmul(pp[oc][:, 512 * c2:512 * c2 + 512],
                                             wks[:, dk, 128 * oc:128 * oc + 128],
                                             sl[:, 512 * c2:512 * c2 + 512],
                                             start=(dk == 0), stop=(dk == DK - 1),
                                             skip_group_check=True)
                            for tc in range(8):
                                last = t.matmul(pv8[tc], sl[:, 128 * tc:128 * tc + 128],
                                                wvs[:, dk, :],
                                                start=(dk == 0 and tc % 2 == 0),
                                                stop=(dk == DK - 1),
                                                skip_group_check=True)
                            return last
                        kw = [f"kx{hf}_{gdk}", "wABurst"]
                        if dk == 0:
                            if hf == 0:
                                kw.append("bglast")
                            else:
                                kw += [f"kev0_{oc}" for oc in range(2)]
                                kw += [f"vev0_{tc}" for tc in range(8)]
                        tr.step("tensor", em_kv, waits=kw, tag=f"kv{hf}_{dk}")
                        if dk % 4 == 3 and gdk + 2 < 4:
                            kx_step(gdk + 2)
                    for oc in range(2):
                        tr.step("vector", lambda v, hf=hf, oc=oc: v.tensor_scalar(
                            kslab[:, oc, OWN * hf:OWN * hf + OWN], pp[oc][:, :],
                            bkt[:, oc:oc + 1], None, ALU.add),
                            waits=[f"kv{hf}_{DK-1}"], tag=f"kev{hf}_{oc}")
                    for tc in range(8):
                        def em_vev(v, hf=hf, tc=tc):
                            kt = 8 * hf + tc
                            return v.tensor_tensor(
                                A(vslab)[:, kt:kt + 1, :].rearrange(
                                    "p o (g c) -> p (o g) c", c=65)[:, :, 0:64],
                                pv8[tc].rearrange("p (g c) -> p g c", c=64),
                                bv_bc[:, :].rearrange("p (g c) -> p g c", c=64),
                                ALU.add)
                        tr.step("vector", em_vev, waits=[f"kv{hf}_{DK-1}", "vones"],
                                tag=f"vev{hf}_{tc}")

                for tq in range(4):
                    def qx_step(gdk, tq=tq):
                        w = []
                        if gdk >= 2:
                            w.append(f"qprj{tq}_{4*(gdk-2)+3}")
                        elif tq == 0:
                            w.append(f"kv1_{4*(gdk+2)+3}")
                        else:
                            w.append(f"qprj{tq-1}_{4*(gdk+2)+3}")
                        tr.step("sync", lambda sy, gdk=gdk, tq=tq: sy.dma_start(
                            out=xnp[:, 4 * (gdk % 2):4 * (gdk % 2) + 4, 0:512],
                            in_=d_xnsp[4 * gdk:4 * gdk + 4, :, 512 * tq:512 * tq + 512]
                            .rearrange("k p n -> p k n")),
                            waits=w, tag=f"qx{tq}_{gdk}", sem=f"dXNP{gdk % 2}")
                    qx_step(0)
                    qx_step(1)
                    for dk in range(DK):
                        gdk = dk // 4

                        def em_q(t, dk=dk, gdk=gdk):
                            sl = xnp[:, 4 * (gdk % 2) + dk % 4, 0:512]
                            for j in range(8):
                                last = t.matmul(pq8[j], wqb[:, j, dk, :], sl,
                                                start=(dk == 0), stop=(dk == DK - 1),
                                                skip_group_check=True)
                            return last
                        qw = [f"qx{tq}_{gdk}", "wABurst"]
                        if dk == 0:
                            if tq == 0:
                                qw += [f"kev1_{oc}" for oc in range(2)]
                                qw += [f"vev1_{tc}" for tc in range(8)]
                            else:
                                qw += [f"qe{tq-1}_{j}" for j in range(8)]
                        tr.step("tensor", em_q, waits=qw, tag=f"qprj{tq}_{dk}")
                        if dk % 4 == 3 and gdk + 2 < 4:
                            qx_step(gdk + 2)
                    for j in range(8):
                        def em_qe(v, j=j, tq=tq):
                            v.tensor_scalar(
                                qslab[:, j, 256 * tq:256 * tq + 256],
                                pq8[j].rearrange("p (a two) -> p a two", two=2)[:, :, 0],
                                bqt[:, j:j + 1], None, ALU.add)
                            return v.tensor_scalar(
                                qslab[:, j, 1024 + 256 * tq:1024 + 256 * tq + 256],
                                pq8[j].rearrange("p (a two) -> p a two", two=2)[:, :, 1],
                                bqt[:, j:j + 1], None, ALU.add)
                        tr.step("vector", em_qe, waits=[f"qprj{tq}_{DK-1}"], tag=f"qe{tq}_{j}")
                with nc.Block("p3") as bk:
                    tr.emit(bk, sems)
                carry = {k: tr.events[k] for k in ["consts", "faccspill", "bglast"]}
                carry["qlast"] = tr.events["qe3_7"]
                carry["qhalf"] = tr.events["qe1_7"]
                carry["kvlast"] = tr.events["vev1_7"]
                cnts = dict(tr.counts)

            attns = eAW.enter_context(nc.sbuf_tensor("attns", [128, 16, OWN], BF16, side="right"))
            wos = eAW.enter_context(nc.sbuf_tensor("wos", [128, 16, H], BF16, side="right"))
            with ExitStack() as s4:
                probs = s4.enter_context(nc.sbuf_tensor("probs", [128, 2, 1024], BF16))
                avsb = s4.enter_context(nc.sbuf_tensor("avsb", [65, 2, 1024], F32))
                avtmp = s4.enter_context(nc.sbuf_tensor("avtmp", [64, 512], F32))
                rr = s4.enter_context(nc.sbuf_tensor("rr", [65, 512], F32))
                rbsb = s4.enter_context(nc.sbuf_tensor("rbsb", [64, 512], F32))
                ps_s = [s4.enter_context(nc.psum_tensor(f"ps_s{i}", [128, 1024], F32))
                        for i in range(3)]
                ps_av = s4.enter_context(nc.psum_tensor("ps_av", [128, 1024], F32))

                tr = Tracer(cnts, carry)
                for i in range(4):
                    tr.step("gpsimd", lambda g, i=i: g.dma_start(
                        out=wos[:, 4 * i:4 * i + 4, :], in_=d_wo[:, 4 * i:4 * i + 4, :]),
                        tag=f"wo{i}", sem="dWOS")
                tr.step("vector", lambda v: v.memset(rr[0:64, :], 0.0), tag="rrz")
                vv = A(vslab).rearrange("p k (g c) -> p k g c", c=65)
                for pr in range(8):
                    g_, ckv = pr // 2, pr % 2
                    hlA = 8 * ckv + g_
                    hlB = 8 * ckv + 4 + g_
                    for qb in range(4):
                        par = qb // 2
                        mb = 512 * (qb % 2)
                        un = pr * 4 + qb

                        def em_sc(t, ckv=ckv, pr=pr, qb=qb, kt=0):
                            psc = ps_s[kt % 3]
                            t.matmul(psc[:, 0:512],
                                     kslab[0:64, ckv, 128 * kt:128 * kt + 128],
                                     qslab[0:64, pr, 512 * qb:512 * qb + 512],
                                     start=True, stop=True,
                                     tile_position=(0, 0), skip_group_check=True)
                            return t.matmul(psc[:, 512:1024],
                                            kslab[64:128, ckv, 128 * kt:128 * kt + 128],
                                            qslab[64:128, pr, 512 * qb:512 * qb + 512],
                                            start=True, stop=True,
                                            tile_position=(64, 0),
                                            skip_group_check=True)

                        def em_exp(sc, kt=0):
                            return sc.activation(probs[:, kt % 2, :], ps_s[kt % 3][:, :],
                                                 ACT.Exp, bias=0.0, scale=0.125)

                        def em_av(t, ckv=ckv, kt=0):
                            t.matmul(ps_av[0:65, 0:512], vv[:, kt, 2 * ckv, :],
                                     probs[:, kt % 2, 0:512],
                                     start=(kt == 0), stop=(kt == 15), skip_group_check=True)
                            return t.matmul(ps_av[0:65, 512:1024], vv[:, kt, 2 * ckv + 1, :],
                                            probs[:, kt % 2, 512:1024],
                                            start=(kt == 0), stop=(kt == 15),
                                            skip_group_check=True)

                        def emit_av(kt):
                            aw = [f"exp{un}_{kt}"]
                            if kt == 0 and un >= 1:
                                aw.append(f"evac{un-1}")
                            tr.step("tensor", partial(em_av, kt=kt), waits=aw,
                                    tag=f"av{un}_{kt}")

                        for kt in range(16):
                            w = []
                            if un == 0 and kt == 0:
                                w += ["qhalf", "kvlast"]
                            if un == 1 and kt == 0:
                                w.append("qlast")
                            if kt >= 3:
                                w.append(f"exp{un}_{kt-3}")
                            elif un >= 1:
                                w.append(f"exp{un-1}_{13 + kt}")
                            tr.step("tensor", partial(em_sc, kt=kt), waits=w,
                                    tag=f"sc{un}_{kt}")
                            if kt >= 2:
                                emit_av(kt - 2)
                            ew = [f"sc{un}_{kt}"]
                            if kt >= 2:
                                ew.append(f"av{un}_{kt-2}")
                            elif un >= 1:
                                ew.append(f"av{un-1}_{14 + kt}")
                            tr.step("scalar", partial(em_exp, kt=kt), waits=ew,
                                    tag=f"exp{un}_{kt}")
                        emit_av(14)
                        emit_av(15)
                        tr.step("vector", lambda v, un=un: v.tensor_copy(
                            avsb[:, un % 2, :], ps_av[0:65, 0:1024]),
                            waits=[f"av{un}_15"], tag=f"evac{un}")
                        for hx, hl in ((0, hlA), (1, hlB)):
                            cA = 512 * hx
                            if hx == 0:
                                rw_ = [f"evac{un}", "rrz"]
                                if un >= 1:
                                    rw_.append(f"rba{un-1}_1")
                            else:
                                rw_ = [f"rba{un}_0"]
                            tr.step("vector", lambda v, cA=cA, un=un: v.reciprocal(
                                rr[64:65, :], avsb[64:65, un % 2, cA:cA + 512]),
                                waits=rw_, tag=f"rec{un}_{hx}")
                            tr.step("gpsimd", lambda g: g.dma_start(
                                out=d_rrow, in_=rr[64:65, :]),
                                waits=[f"rec{un}_{hx}"]
                                + ([f"rbb{un}_0"] if hx else
                                   ([f"rbb{un-1}_1"] if un else [])),
                                tag=f"rba{un}_{hx}", sem="dEP0")
                            bw = [f"rba{un}_{hx}"]
                            if hx == 1:
                                bw.append(f"ep{un}_0" if par == 0 else f"avt{un}_0")
                            elif un >= 1:
                                bw.append(f"ep{un-1}_1" if par == 0 or (un - 1) % 4 < 2
                                          else f"avt{un-1}_1")
                            tr.step("gpsimd", lambda g: g.dma_start(
                                out=rbsb[:, :], in_=d_rrow.broadcast_to([64, 512])),
                                waits=bw, tag=f"rbb{un}_{hx}", sem="dEP0")
                            if par == 0:
                                tr.step("vector", lambda v, hl=hl, mb=mb, cA=cA, un=un:
                                        v.tensor_mul(
                                            attns[0:64, hl, mb:mb + 512],
                                            avsb[0:64, un % 2, cA:cA + 512], rbsb[:, :]),
                                        waits=[f"rbb{un}_{hx}"], tag=f"ep{un}_{hx}")
                            else:
                                aw_ = [f"rbb{un}_{hx}"]
                                if hx == 1:
                                    aw_.append(f"ep{un}_0")
                                tr.step("vector", lambda v, cA=cA, un=un: v.tensor_mul(
                                    avtmp[:, :], avsb[0:64, un % 2, cA:cA + 512], rbsb[:, :]),
                                    waits=aw_, tag=f"avt{un}_{hx}")
                                tr.step("gpsimd", lambda g, hl=hl, mb=mb: g.dma_start(
                                    out=attns[64:128, hl, mb:mb + 512], in_=avtmp[:, :]),
                                    waits=[f"avt{un}_{hx}"], tag=f"ep{un}_{hx}", sem="dEP1")
                with nc.Block("p4") as bk:
                    tr.emit(bk, sems)
                carry = {k: tr.events[k] for k in ["consts", "faccspill", "bglast"]}
                carry["attdone"] = tr.events["ep31_1"]
                carry["wodone"] = tr.events["wo3"]
                cnts = dict(tr.counts)

        with ExitStack() as s5:
            r64 = [s5.enter_context(nc.sbuf_tensor(f"r64_{i}", [128, 64], F32)) for i in range(6)]
            rw = s5.enter_context(nc.sbuf_tensor("rw", [128, 64, 8], F32))
            rw2 = s5.enter_context(nc.sbuf_tensor("rw2", [128, 64, 8], F32))
            sbc = s5.enter_context(nc.sbuf_tensor("sbc", [128, 8], F32))
            s8 = s5.enter_context(nc.sbuf_tensor("s8", [2, 8], F32))
            ea = s5.enter_context(nc.sbuf_tensor("ea", [64, 128], F32))
            eb = s5.enter_context(nc.sbuf_tensor("eb", [64, 128], F32))
            ind = s5.enter_context(nc.sbuf_tensor("ind", [64, 128], F32))
            sca = s5.enter_context(nc.sbuf_tensor("sca", [64, 8, 128], F32))
            scb = s5.enter_context(nc.sbuf_tensor("scb", [64, 8, 128], F32))
            zer = s5.enter_context(nc.sbuf_tensor("zer", [64, 128], F32))
            t01 = s5.enter_context(nc.sbuf_tensor("t01", [64, 16], F32))
            tsb = s5.enter_context(nc.sbuf_tensor("tsb", [128, 8], F32))
            offs = s5.enter_context(nc.sbuf_tensor("offs", [128, 8], F32))
            off1 = s5.enter_context(nc.sbuf_tensor("off1", [64, 8], F32))
            ppx = [s5.enter_context(nc.sbuf_tensor(f"ppx{i}", [64, 128], F32)) for i in range(4)]
            own8 = s5.enter_context(nc.sbuf_tensor("own8", [8, 4, 128], F32))
            uo = s5.enter_context(nc.sbuf_tensor("uo", [8, 8, 128], F32))
            po = s5.enter_context(nc.sbuf_tensor("po", [8, 8, 128], F32))
            utmp8 = s5.enter_context(nc.sbuf_tensor("utmp8", [64, 8, 128], F32))
            o8 = [s5.enter_context(nc.sbuf_tensor(f"o8_{i}", [8, 128], F32)) for i in range(5)]
            ps_a = s5.enter_context(nc.psum_tensor("ps_a", [128, 1024], F32))
            ps_b = s5.enter_context(nc.psum_tensor("ps_b", [128, 1024], F32))
            ps_c = s5.enter_context(nc.psum_tensor("ps_c", [128, 1024], F32))

            tr = Tracer(cnts, carry)
            tr.pad_vector = True

            def drain(t):
                return t.matmul(ps_b[0:1, 512:1024], onesbf[:, :], wos[:, 0, 0:512],
                                start=True, stop=True, skip_group_check=True)

            def em_srow(t):
                for dk in range(DK):
                    t.matmul(ps_a[0:1, 0:10], onesf[:, :], rt1t[:, dk, :],
                             start=(dk == 0), stop=(dk == DK - 1),
                             skip_group_check=True)
                return drain(t)
            tr.step("tensor", em_srow, waits=["consts", "attdone"], tag="srow")

            def em_s8(v):
                v.memset(A(s8), 0.0)
                return v.tensor_copy(s8[0:1, :], ps_a[0:1, 0:8])
            tr.step("vector", em_s8, waits=["srow"], tag="s8ev")
            tr.step("tensor", lambda t: t.matmul(ps_b[0:128, 0:8], onesc[:, :], s8[:, :],
                                                 start=True, stop=True, skip_group_check=True),
                    waits=["s8ev"], tag="sbcmm")
            tr.step("vector", lambda v: v.tensor_copy(sbc[:, :], ps_b[0:128, 0:8]),
                    waits=["sbcmm"], tag="sbcev")
            tr.step("vector", lambda v: v.tensor_scalar_mul(r64[0][:, :], rtraw[:, :, 8], 1.0 / H),
                    waits=["bglast"], tag="mtok")

            def em_u(v):
                for e in range(8):
                    v.tensor_scalar(r64[1][:, :], r64[0][:, :], sbc[:, e:e + 1], None, ALU.mult)
                    v.tensor_sub(rw[:, :, e], rtraw[:, :, e], r64[1][:, :])
                return None
            tr.step("vector", em_u, waits=["mtok", "sbcev"], tag="u_done")

            def em_top(v):
                v.tensor_max(r64[0][:, :], rw[:, :, 0], rw[:, :, 1])
                for e in range(2, 8):
                    v.tensor_max(r64[0][:, :], r64[0][:, :], rw[:, :, e])
                for e in range(8):
                    v.tensor_tensor(r64[1][:, :], rw[:, :, e], r64[0][:, :], ALU.is_equal)
                    v.tensor_scalar_mul(r64[2][:, :], r64[1][:, :], float(8 - e))
                    if e == 0:
                        v.tensor_copy(r64[3][:, :], r64[2][:, :])
                    else:
                        v.tensor_max(r64[3][:, :], r64[3][:, :], r64[2][:, :])
                v.tensor_scalar(r64[4][:, :], r64[3][:, :], -1.0, 8.0, ALU.mult, ALU.add)
                for e in range(8):
                    v.tensor_scalar(r64[1][:, :], r64[4][:, :], float(e), None, ALU.is_equal)
                    v.scalar_tensor_tensor(rw2[:, :, e], r64[1][:, :], -1e30, rw[:, :, e],
                                           ALU.mult, ALU.add)
                v.tensor_max(r64[0][:, :], rw2[:, :, 0], rw2[:, :, 1])
                for e in range(2, 8):
                    v.tensor_max(r64[0][:, :], r64[0][:, :], rw2[:, :, e])
                for e in range(8):
                    v.tensor_tensor(r64[1][:, :], rw2[:, :, e], r64[0][:, :], ALU.is_equal)
                    v.tensor_scalar_mul(r64[2][:, :], r64[1][:, :], float(8 - e))
                    if e == 0:
                        v.tensor_copy(r64[5][:, :], r64[2][:, :])
                    else:
                        v.tensor_max(r64[5][:, :], r64[5][:, :], r64[2][:, :])
                return v.tensor_scalar(r64[5][:, :], r64[5][:, :], -1.0, 8.0, ALU.mult, ALU.add)
            tr.step("vector", em_top, waits=["u_done"], tag="top_done")

            def em_etr(t):
                t.transpose(ps_a[0:64, 0:128], r64[4][:, :], identt[:, :])
                t.transpose(ps_a[0:64, 512:640], r64[5][:, :], identt[:, :])
                return drain(t)
            tr.step("tensor", em_etr, waits=["top_done"], tag="etr")

            def em_eev(v):
                v.tensor_copy(ea[:, :], ps_a[0:64, 0:128])
                v.tensor_copy(eb[:, :], ps_a[0:64, 512:640])
                return v.memset(zer[:, :], 0.0)
            tr.step("vector", em_eev, waits=["etr"], tag="eev")

            def em_scan(v):
                for e in range(8):
                    v.tensor_scalar(ind[:, :], ea[:, :], float(e), None, ALU.is_equal)
                    v.tensor_tensor_scan(sca[:, e, :], ind[:, :], zer[:, :], 0.0,
                                         ALU.add, ALU.add)
                    v.tensor_scalar(ind[:, :], eb[:, :], float(e), None, ALU.is_equal)
                    v.tensor_tensor_scan(scb[:, e, :], ind[:, :], zer[:, :], 0.0,
                                         ALU.add, ALU.add)
                for e in range(8):
                    v.tensor_copy(t01[:, e:e + 1], sca[:, e, 127:128])
                    v.tensor_copy(t01[:, 8 + e:8 + e + 1], scb[:, e, 127:128])
                return None
            tr.step("vector", em_scan, waits=["eev"], tag="scans")

            def em_tmm(t):
                t.matmul(ps_b[0:128, 8:16], idlow[:, :], t01[:, 0:8], start=True, stop=False,
                         skip_group_check=True)
                t.matmul(ps_b[0:128, 8:16], sh64[:, :], t01[:, 8:16], start=False,
                         stop=True, skip_group_check=True)
                return drain(t)
            tr.step("tensor", em_tmm, waits=["scans"], tag="tmm")
            tr.step("vector", lambda v: v.tensor_copy(tsb[:, :], ps_b[0:128, 8:16]),
                    waits=["tmm"], tag="tev")
            def em_offmm(t):
                t.matmul(ps_b[0:128, 16:24], lstr[:, :], tsb[:, :],
                         start=True, stop=True, skip_group_check=True)
                return drain(t)
            tr.step("tensor", em_offmm, waits=["tev"], tag="offmm")
            tr.step("vector", lambda v: v.tensor_copy(offs[:, :], ps_b[0:128, 16:24]),
                    waits=["offmm"], tag="offev")
            def em_off1mm(t):
                t.matmul(ps_b[0:64, 24:32], shdn[:, :], offs[:, :],
                         start=True, stop=True, skip_group_check=True)
                return drain(t)
            tr.step("tensor", em_off1mm, waits=["offev"], tag="off1mm")
            tr.step("vector", lambda v: v.tensor_copy(off1[:, :], ps_b[0:64, 24:32]),
                    waits=["off1mm"], tag="off1ev")

            def em_pp(v):
                for e in range(8):
                    v.tensor_scalar(ind[:, :], ea[:, :], float(e), None, ALU.is_equal)
                    v.tensor_scalar(ppx[2][:, :], sca[:, e, :], offs[0:64, e:e + 1], None, ALU.add)
                    v.tensor_mul(ppx[3][:, :], ppx[2][:, :], ind[:, :])
                    if e == 0:
                        v.tensor_copy(ppx[0][:, :], ppx[3][:, :])
                    else:
                        v.tensor_add(ppx[0][:, :], ppx[0][:, :], ppx[3][:, :])
                for e in range(8):
                    v.tensor_scalar(ind[:, :], eb[:, :], float(e), None, ALU.is_equal)
                    v.tensor_scalar(ppx[2][:, :], scb[:, e, :], off1[:, e:e + 1], None, ALU.add)
                    v.tensor_mul(ppx[3][:, :], ppx[2][:, :], ind[:, :])
                    if e == 0:
                        v.tensor_copy(ppx[1][:, :], ppx[3][:, :])
                    else:
                        v.tensor_add(ppx[1][:, :], ppx[1][:, :], ppx[3][:, :])
                v.tensor_scalar(ppx[2][:, :], ppx[0][:, :], float(CAP), None, ALU.is_le)
                v.tensor_scalar(ppx[3][:, :], ppx[1][:, :], float(CAP), None, ALU.is_le)
                return None
            tr.step("vector", em_pp, waits=["off1ev"], tag="pp_done")

            def em_ownx(t):
                t.matmul(ps_a[0:8, 256:384], selc8[:, :], ppx[2][:, :], start=True, stop=True,
                         skip_group_check=True)
                t.matmul(ps_a[0:8, 640:768], selc8[:, :], ppx[3][:, :], start=True, stop=True,
                         skip_group_check=True)
                t.matmul(ps_c[0:8, 512:640], selc8[:, :], ea[:, :], start=True, stop=True,
                         skip_group_check=True)
                t.transpose(ps_b[0:8, 128:256], rstdtok[:, :], identt[:, :])
                return drain(t)
            tr.step("tensor", em_ownx, waits=["pp_done"], tag="ownx")

            def em_ownev(v):
                v.tensor_copy(own8[:, 0, :], ps_a[0:8, 256:384])
                v.tensor_copy(own8[:, 1, :], ps_a[0:8, 640:768])
                v.tensor_copy(own8[:, 2, :], ps_c[0:8, 512:640])
                return v.tensor_copy(own8[:, 3, :], ps_b[0:8, 128:256])
            tr.step("vector", em_ownev, waits=["ownx"], tag="ownev")
            def em_utr(t):
                for e in range(8):
                    t.transpose(ps_c[0:64, 128 * e:128 * e + 128], rw[:, :, e], identt[:, :])
                return drain(t)
            tr.step("tensor", em_utr, waits=["ownev"], tag="utr_all")
            tr.step("vector", lambda v: v.tensor_copy(
                A(utmp8).rearrange("p a c -> p (a c)"), ps_c[0:64, 0:1024]),
                waits=["utr_all"], tag="utev_all")

            def em_uxm(t):
                for e in range(8):
                    t.matmul(ps_a[0:8, 128 * e:128 * e + 128], selc8[:, :], utmp8[:, e, :],
                             start=(e % 4 == 0), stop=True, skip_group_check=True)
                return drain(t)
            tr.step("tensor", em_uxm, waits=["utev_all"], tag="uxm_all")
            tr.step("vector", lambda v: v.tensor_copy(
                A(uo).rearrange("p a c -> p (a c)"), ps_a[0:8, 0:1024]),
                waits=["uxm_all"], tag="uoev7")

            def em_pmath(v):
                for e in range(8):
                    v.tensor_mul(po[:, e, :], uo[:, e, :], own8[:, 3, :])
                return None
            tr.step("vector", em_pmath, waits=["uoev7", "ownev"], tag="l_done")
            tr.step("scalar", lambda sc: sc.activation(A(uo).rearrange("p a b -> p (a b)"),
                                                       A(po).rearrange("p a b -> p (a b)"),
                                                       ACT.Exp, bias=0.0, scale=1.0),
                    waits=["l_done"], tag="pexp")

            def em_pvals(v):
                v.tensor_add(o8[0][:, :], uo[:, 0, :], uo[:, 1, :])
                for e in range(2, 8):
                    v.tensor_add(o8[0][:, :], o8[0][:, :], uo[:, e, :])
                v.reciprocal(o8[1][:, :], o8[0][:, :])
                v.tensor_max(o8[2][:, :], uo[:, 0, :], uo[:, 1, :])
                for e in range(2, 8):
                    v.tensor_max(o8[2][:, :], o8[2][:, :], uo[:, e, :])
                v.tensor_mul(o8[2][:, :], o8[2][:, :], o8[1][:, :])
                for e in range(8):
                    v.tensor_scalar(o8[3][:, :], own8[:, 2, :], float(e), None, ALU.is_equal)
                    v.tensor_scalar(o8[3][:, :], o8[3][:, :], -1.0, 1.0, ALU.mult, ALU.add)
                    v.tensor_mul(po[:, e, :], uo[:, e, :], o8[3][:, :])
                v.tensor_max(o8[3][:, :], po[:, 0, :], po[:, 1, :])
                for e in range(2, 8):
                    v.tensor_max(o8[3][:, :], o8[3][:, :], po[:, e, :])
                v.tensor_mul(o8[3][:, :], o8[3][:, :], o8[1][:, :])
                v.tensor_mul(o8[4][:, :], own8[:, 0, :], o8[2][:, :])
                v.tensor_mul(o8[0][:, :], own8[:, 1, :], o8[3][:, :])
                return v.tensor_add(o8[4][:, :], o8[4][:, :], o8[0][:, :])
            tr.step("vector", em_pvals, waits=["pexp"], tag="pvals")
            def em_gtr(t):
                t.transpose(ps_c[0:128, 256:264], o8[4][:, :], identt[0:8, 0:8])
                return drain(t)
            tr.step("tensor", em_gtr, waits=["pvals"], tag="g_tr")
            tr.step("vector", lambda v: v.tensor_copy(gtok[:, :], ps_c[0:128, 256:264]),
                    waits=["g_tr"], tag="g_ev")
            with nc.Block("p5a") as bk:
                tr.emit(bk, sems)
            carry = {k: tr.events[k] for k in ["consts", "faccspill", "attdone", "wodone", "g_ev"]}
            cnts = dict(tr.counts)

        with ExitStack() as s6:
            acc = s6.enter_context(nc.sbuf_tensor("acc", [128, 8, H], F32))
            cmb = s6.enter_context(nc.sbuf_tensor("cmb", [128, 512], F32))
            xst = s6.enter_context(nc.sbuf_tensor("xst", [128, 1, H], F32))
            fst = s6.enter_context(nc.sbuf_tensor("fst", [128, 1, H], F32))
            pw = [s6.enter_context(nc.psum_tensor(f"pw{i}", [128, 1024], F32)) for i in range(4)]
            pwv = [pw[i // 2][:, 512 * (i % 2):512 * (i % 2) + 512] for i in range(4)]
            pwv2 = [pw[2 + i // 2][:, 512 * (i % 2):512 * (i % 2) + 512] for i in range(4)]
            tr = Tracer(cnts, carry)
            for tc in range(8):
                tr.step("sync", lambda sy, tc=tc: sy.dma_start(
                    out=xst[:, 0, :], in_=d_xacc[:, tc, :]),
                    waits=[f"cmb{tc-1}_3"] if tc >= 1 else ["attdone"],
                    tag=f"xld{tc}", sem="dXLD0")
                tr.step("gpsimd", lambda g, tc=tc: g.dma_start(
                    out=fst[:, 0, :], in_=d_facc[tc, :, :]),
                    waits=[f"cmb{tc-1}_3"] if tc >= 1 else ["faccspill", "attdone"],
                    tag=f"fld{tc}", sem="dFLD0")

                def em_wo(t, tc=tc):
                    ps = pwv if tc % 2 == 0 else pwv2
                    slw = A(attns)[:, 2 * tc:2 * tc + 2, :].rearrange(
                        "p h (s a) -> p h s a", a=16)
                    for ac in range(16):
                        lhs = slw[:, :, :, ac:ac + 1]
                        for hq in range(4):
                            last = t.matmul(ps[hq], lhs,
                                            wos[:, ac, 512 * hq:512 * hq + 512],
                                            start=(ac == 0), stop=(ac == 15),
                                            skip_group_check=True)
                    return last
                if tc == 0:
                    ww = ["attdone", "wodone", "g_ev"]
                elif tc == 1:
                    ww = ["g_ev"]
                else:
                    ww = [f"cmb{tc-2}_3"]
                tr.step("tensor", em_wo, waits=ww, tag=f"wo_mm{tc}")

                for hq in range(4):
                    def em_cmb(v, tc=tc, hq=hq):
                        ps = (pwv if tc % 2 == 0 else pwv2)[hq]
                        asl = acc[:, tc, 512 * hq:512 * hq + 512]
                        hsl = slice(512 * hq, 512 * hq + 512)
                        v.scalar_tensor_tensor(cmb[:, :], b2_bc[:, hsl],
                                               gtok[:, tc:tc + 1], ps, ALU.mult, ALU.add)
                        v.scalar_tensor_tensor(asl, fst[:, 0, hsl], gtok[:, tc:tc + 1],
                                               cmb[:, :], ALU.mult, ALU.add)
                        v.tensor_add(cmb[:, :], xst[:, 0, hsl], bo_bc[:, hsl])
                        return v.tensor_add(asl, asl, cmb[:, :])
                    tr.step("vector", em_cmb,
                            waits=[f"wo_mm{tc}", f"xld{tc}", f"fld{tc}", "g_ev"],
                            tag=f"cmb{tc}_{hq}")
            tr.step("sync", lambda sy: sy.dma_start(out=d_out[:, :, :], in_=A(acc)),
                    waits=["cmb7_3"], tag="stored", sem="dOUT")
            tr.step("gpsimd", lambda g: None, waits=["stored"], tag="final", sem="dOUT")
            with nc.Block("p5b") as bk:
                tr.emit(bk, sems)
        eAW.close()

    return nc


_PROG = None


def _get_prog():
    global _PROG
    if _PROG is None:
        _PROG = build_program()
    return _PROG


def kernel(L0_attn_X, L0_attn_Wq, L0_attn_bq, L0_attn_Wk, L0_attn_bk, L0_attn_Wv,
           L0_attn_bv, L0_attn_Wo, L0_attn_bo, L0_ffn_router_w, L0_ffn_w1, L0_ffn_b1,
           L0_ffn_w2, L0_ffn_b2):
    X = np.ascontiguousarray(np.asarray(L0_attn_X, dtype=np.float32)).reshape(NTOK, H)
    Wq = np.asarray(L0_attn_Wq, np.float32)
    Wk = np.asarray(L0_attn_Wk, np.float32)
    Wv = np.asarray(L0_attn_Wv, np.float32)
    Wo = np.asarray(L0_attn_Wo, np.float32)
    bq = np.asarray(L0_attn_bq, np.float32)
    bk = np.asarray(L0_attn_bk, np.float32)
    bv = np.asarray(L0_attn_bv, np.float32)
    bo = np.asarray(L0_attn_bo, np.float32)
    rt = np.asarray(L0_ffn_router_w, np.float32)
    w1 = np.asarray(L0_ffn_w1, np.float32)
    b1 = np.asarray(L0_ffn_b1, np.float32)
    w2 = np.asarray(L0_ffn_w2, np.float32)
    b2 = np.asarray(L0_ffn_b2, np.float32)

    XT = np.ascontiguousarray(X.T)
    xstats = np.ascontiguousarray(XT.reshape(DK, 128, NQ, QTR).transpose(2, 1, 0, 3))
    wo_h = np.ascontiguousarray(Wo.T.reshape(16, 128, H).transpose(1, 0, 2))
    w1_h = np.ascontiguousarray(
        w1.T.reshape(DK, 128, NFG, FG, 128).transpose(2, 1, 3, 0, 4))
    w2_h = np.ascontiguousarray(w2.T.reshape(NFG, FG, 128, H).transpose(0, 2, 1, 3))
    rt1 = np.zeros((H, 10), np.float32)
    rt1[:, 0:8] = rt.T
    rt1[:, 8] = 1.0
    rt1_h = np.ascontiguousarray(rt1.reshape(DK, 128, 10).transpose(1, 0, 2))
    b1_t = np.ascontiguousarray(b1.reshape(64, 128).T)
    ident = np.eye(128, dtype=np.float32)
    lstrict = np.ascontiguousarray(np.tril(np.ones((128, 128), np.float32), -1).T)
    idlow = np.zeros((64, 128), np.float32)
    idlow[np.arange(64), np.arange(64)] = 1.0
    shift64 = np.zeros((64, 128), np.float32)
    shift64[np.arange(64), np.arange(64) + 64] = 1.0
    shiftdn = np.zeros((128, 64), np.float32)
    shiftdn[np.arange(64) + 64, np.arange(64)] = 1.0
    onesd = np.zeros((2, 128), np.float32)
    onesd[1, :] = 1.0
    sel64 = np.zeros((65, 128), np.float32)
    sel64[64, :] = 1.0
    onesc = np.zeros((2, 128), np.float32)
    onesc[0, :] = 1.0
    onesf = np.ones((128, 1), np.float32)
    onesbf = np.ones((128, 1), BF)

    nc = _get_prog()
    in_maps = []
    for c in range(8):
        b = c // 2
        cp = c % 2
        seq = XT[:, S * b:S * (b + 1)]
        xown = np.ascontiguousarray(seq.reshape(DK, 128, 4, QTR).transpose(2, 1, 0, 3))
        ownT = seq[:, OWN * cp:OWN * (cp + 1)]
        xow = np.ascontiguousarray(ownT.reshape(DK, 128, 2, QTR).transpose(2, 1, 0, 3))
        xacc = np.ascontiguousarray(
            X[OWN * c:OWN * (c + 1)].reshape(8, 128, H).transpose(1, 0, 2))
        qcols = np.empty(1024, np.int64)
        i = 0
        for ch in range(8):
            g_, ckv = ch // 2, ch % 2
            for kvpar in range(2):
                kvg = 4 * cp + 2 * ckv + kvpar
                for d in range(64):
                    qcols[i] = kvg * 256 + g_ * 64 + d
                    i += 1
        WqTs = np.ascontiguousarray(Wq.T[:, qcols])
        wq_h = np.ascontiguousarray(
            WqTs.reshape(DK, 128, 8, 128).transpose(1, 2, 0, 3))
        bq_t = np.ascontiguousarray(bq[qcols].reshape(8, 128).T)
        kvrows = np.arange(4 * cp * 64, (4 * cp + 4) * 64)
        WkTs = np.ascontiguousarray(Wk[kvrows, :].T)
        wk_h = np.ascontiguousarray(WkTs.reshape(DK, 128, 256).transpose(1, 0, 2))
        WvTs = np.ascontiguousarray(Wv[kvrows, :].T)
        wv_h = np.ascontiguousarray(WvTs.reshape(DK, 128, 256).transpose(1, 0, 2))
        bk_t = np.ascontiguousarray(bk[kvrows].reshape(2, 128).T)
        brow = np.zeros((3, H), np.float32)
        brow[0] = bo
        brow[1] = b2
        brow[2, 0:256] = bv[kvrows]
        selc8 = np.zeros((64, 8), np.float32)
        for j in range(8):
            selc8[8 * c + j, j] = 1.0
        in_maps.append({
            "xownseq": xown, "xowhalf": xow, "xstats": xstats, "xacc": xacc,
            "wq": wq_h, "wk": wk_h, "wv": wv_h, "wo": wo_h,
            "w1": w1_h, "w2": w2_h, "rt1": rt1_h,
            "bqt": bq_t, "bkt": bk_t, "b1t": b1_t, "brow": brow,
            "identc": ident, "lstrictc": lstrict, "idlowc": idlow,
            "shift64c": shift64, "shiftdnc": shiftdn, "onesdc": onesd,
            "sel64c": sel64, "onescc": onesc, "onesfc": onesf,
            "onesbfc": onesbf, "selc8c": selc8,
        })
    res = run_bass_kernel_spmd(nc, in_maps, core_ids=list(range(8)))
    parts = []
    for c in range(8):
        o = res.results[c]["outp"]
        parts.append(np.ascontiguousarray(np.asarray(o).transpose(1, 0, 2)).reshape(OWN, H))
    return np.concatenate(parts, axis=0).reshape(Bb, S, H)
